# version 15
# baseline (speedup 1.0000x reference)
import numpy as np
import ml_dtypes
import concourse.bass as bass
import concourse.mybir as mybir
from concourse.bass_utils import run_bass_kernel_spmd
from contextlib import ExitStack

F32 = mybir.dt.float32
BF16 = mybir.dt.bfloat16
I32 = mybir.dt.int32
AF = mybir.ActivationFunctionType
OP = mybir.AluOpType
AX = mybir.AxisListType

MAGIC = 12582912.0
T_TOK = 4096
D = 1024
INNER = 4096
NT = T_TOK // 128
GRP = 256
NG = T_TOK // GRP
SUP = 256
NSUP = T_TOK // SUP


class SemC:
    def __init__(self, name):
        self.name = name
        self.h = None
        self.n = 0


def build_program():
    nc = bass.Bass("TRN2", target_bir_lowering=False)

    d_x = nc.dram_tensor("x", [T_TOK, D], F32, kind="ExternalInput").ap()
    d_Wg = nc.dram_tensor("Wg", [D, D], F32, kind="ExternalInput").ap()
    d_Wv = nc.dram_tensor("Wv", [D, D], F32, kind="ExternalInput").ap()
    d_Wd = nc.dram_tensor("Wd", [D, D], F32, kind="ExternalInput").ap()
    d_bias = nc.dram_tensor("biasT", [128, 24], F32, kind="ExternalInput").ap()
    d_wgate = nc.dram_tensor("wgate", [INNER, D], F32, kind="ExternalInput").ap()
    d_wup = nc.dram_tensor("wup", [INNER, D], F32, kind="ExternalInput").ap()
    d_wdown = nc.dram_tensor("wdown", [D, INNER], F32, kind="ExternalInput").ap()
    d_consts = nc.dram_tensor("consts", [128, 16], F32, kind="ExternalInput").ap()
    d_idf = nc.dram_tensor("identf", [128, 128], F32, kind="ExternalInput").ap()
    d_idb = nc.dram_tensor("identb", [128, 128], BF16, kind="ExternalInput").ap()
    d_ones = nc.dram_tensor("ones128", [128, 1], F32, kind="ExternalInput").ap()
    d_onesT = nc.dram_tensor("onesT", [1, 128], F32, kind="ExternalInput").ap()
    d_out = nc.dram_tensor("out", [T_TOK, D], F32, kind="ExternalOutput").ap()

    scr_acc = nc.dram_tensor("scr_acc", [T_TOK, D], F32, kind="Internal").ap()
    scr_xqT = nc.dram_tensor("scr_xqT", [NT, 128, D], BF16, kind="Internal").ap()
    scr_xn = nc.dram_tensor("scr_xn", [T_TOK, D], BF16, kind="Internal").ap()
    scr_pqF = nc.dram_tensor("scr_pqF", [NT, 128, INNER], BF16, kind="Internal").ap()
    scr_wgT = nc.dram_tensor("scr_wgT", [D, INNER], BF16, kind="Internal").ap()
    scr_wuT = nc.dram_tensor("scr_wuT", [D, INNER], BF16, kind="Internal").ap()
    scr_wdT = nc.dram_tensor("scr_wdT", [INNER, D], BF16, kind="Internal").ap()

    es = ExitStack()

    def sb(name, shape, dt):
        return es.enter_context(nc.sbuf_tensor(name, shape, dt))

    def ps(name, shape, dt):
        return es.enter_context(nc.psum_tensor(name, shape, dt))

    with es:
        wqg = sb("wqg", [128, 8 * INNER], BF16)
        wqu = sb("wqu", [128, 8 * INNER], BF16)
        x_grp = [sb("x_grp0", [128, 2 * D], F32), sb("x_grp1", [128, 2 * D], F32)]
        epi = sb("epi", [128, 6 * GRP], F32)
        sgb = epi[:, 0 * GRP:1 * GRP]
        thb = epi[:, 1 * GRP:2 * GRP]
        sdb = epi[:, 2 * GRP:3 * GRP]
        xsb = epi[:, 3 * GRP:4 * GRP]
        ab = epi[:, 4 * GRP:5 * GRP]
        hb = epi[:, 5 * GRP:6 * GRP]
        carry = sb("carry", [128, 8], F32)
        qt = sb("qt", [128, 512], F32)
        qcm = sb("qcm", [128, 512], F32)
        qnd = sb("qnd", [128, 512], F32)
        qg1 = sb("qg1", [128, 512], F32)
        nhalf = sb("nhalf", [128, 512], F32)
        phalf = sb("phalf", [128, 512], F32)
        qci = sb("qci", [128, 512], I32)
        qtmp = sb("qtmp", [128, D], BF16)
        meanall = sb("meanall", [128, NT], F32)
        rstdall = sb("rstdall", [128, NT], F32)
        dxall = sb("dxall", [128, NT], F32)
        varall = dxall
        dpwall = sb("dpwall", [128, NT], F32)
        wsc = sb("wsc", [128, 8], F32)
        consts_sb = sb("consts_sb", [128, 16], F32)
        bias_sb = sb("bias_sb", [128, 24], F32)
        idf = sb("idf_sb", [128, 128], F32)
        ones128 = sb("ones128_sb", [128, 1], F32)
        onesT = sb("onesT_sb", [1, 128], F32)
        scal = sb("scal", [128, 8], F32)
        tin1 = sb("tin1", [1, 8], F32)
        pbuf = sb("pbuf", [128, INNER], F32)
        sg_slab = sb("sg_slab", [128, 512], F32)
        sg2 = [epi[:, 512:1024], epi[:, 0:512]]
        wdr = sb("wdr", [128, 3 * 1024], BF16)
        pqs = sb("pqs", [128, 4 * 256], BF16)
        pqroll = sb("pqroll", [128, 1024], BF16)
        pqstage = sb("pqstage", [128, INNER], BF16)
        xqF_t = [sb("xqF_t0", [128, 8 * 128], BF16), sb("xqF_t1", [128, 8 * 128], BF16)]
        xbig = wqu[:, 8192:16384]
        dvecs = sb("dvecs", [128, 4], F32)
        wchunk = [pbuf[:, j * D:(j + 1) * D] for j in range(4)]

        def accsup(c0, w):
            return pbuf[:, c0:c0 + w]

        gnd2 = x_grp[0][:, 0:512]
        gb1 = x_grp[0][:, 512:1024]
        gb2 = x_grp[0][:, 1024:1536]
        gc1 = x_grp[0][:, 1536:2048]
        gcm = [x_grp[1][:, 0:512], x_grp[1][:, 512:1024]]

        ATTN_BASE = {"g": 0, "v": 8192, "d": 16384}

        def attn_lhs(mat, k, m):
            b = ATTN_BASE[mat] + k * D + m * 128
            return wqg[:, b:b + 128]

        def xnF(c0, w):
            return wqu[:, c0:c0 + w]

        def xqF(c0, w):
            return wqu[:, 2048 + c0:2048 + c0 + w]

        def xnbf(c0, w):
            return wqu[:, 4096 + c0:4096 + c0 + w]

        def wdT_build(c0, w):
            return wqu[:, c0:c0 + w]

        def pqF_sup(c0, w):
            return wqg[:, c0:c0 + w]

        def wdT_str(j):
            return wqg[:, 8192 + j * D:8192 + (j + 1) * D]

        pb = [ps(f"pb{i}", [128, 512], F32) for i in range(8)]

        names = ["G", "T", "V", "S", "LA0", "LX0", "LX1", "LP0", "LP1",
                 "LQ0", "LQ1", "LB0", "LB1", "LW0", "LW1", "LW2", "LW3",
                 "LP2", "LP3", "LW4", "LW5", "LW6", "LW7",
                 "TD0", "TD1", "TD2", "TD3", "TD4", "TD5", "TD6", "TD7", "SN", "WT",
                 "SW1", "SW2", "SW3", "GC",
                 "SA", "SX", "SW", "SP", "SP2", "SO0", "SO1", "SXT0", "SXT1"]
        sems = {n: SemC(n) for n in names}
        for smc in sems.values():
            smc.h = es.enter_context(nc.semaphore(smc.name))
        G, T, V, S = sems["G"], sems["T"], sems["V"], sems["S"]
        LA0 = sems["LA0"]
        LX = [sems["LX0"], sems["LX1"]]
        LP = [sems["LP0"], sems["LP1"], sems["LP2"], sems["LP3"]]
        LQ = [sems["LQ0"], sems["LQ1"]]
        LB = [sems["LB0"], sems["LB1"]]
        LW = [sems[f"LW{j}"] for j in range(8)]
        SA, SX, SW, SP = sems["SA"], sems["SX"], sems["SW"], sems["SP"]
        SP2 = sems["SP2"]
        SXT = [sems["SXT0"], sems["SXT1"]]
        SO = [sems["SO0"], sems["SO1"]]
        TD = [sems[f"TD{j}"] for j in range(8)]
        SN, WT = sems["SN"], sems["WT"]
        SWs = [sems["SW"], sems["SW1"], sems["SW2"], sems["SW3"]]
        GC = sems["GC"]

        block = es.enter_context(nc.Block())
        plan = {"t": [], "v": [], "s": [], "g": [], "y": []}
        ENG_SEM = {"v": V, "s": S, "gc": GC}

        def op(eng, build, sem=None, inc=0, wait=()):
            waits = list(wait)
            if eng == "gc":
                eng = "g"
                esem = ENG_SEM["gc"]
                if esem.n > 0:
                    waits.append((esem, esem.n))
                sem, inc = esem, 1
            elif eng in ("v", "s"):
                esem = ENG_SEM[eng]
                if esem.n > 0:
                    waits.append((esem, esem.n))
                sem, inc = esem, 1
            for (smc, val) in waits:
                if val > 0:
                    plan[eng].append(lambda e, _s=smc, _v=val: e.wait_ge(_s.h, _v))
            if sem is not None and inc:
                sem.n += inc

            def f(e, _b=build, _s=sem, _i=inc):
                ins = _b(e)
                if _s is not None and _i:
                    ins.then_inc(_s.h, _i)
            plan[eng].append(f)

        def wv(smc):
            return (smc, smc.n)

        C_NH = consts_sb[:, 0:1]
        C_PH = consts_sb[:, 1:2]
        C_MG = consts_sb[:, 2:3]
        C_NMG = consts_sb[:, 3:4]
        C_EPS = consts_sb[:, 4:5]

        def emit_round_neg(src_ap, s_vec_ap, dst_ap, clip1=False, pre_wait=()):
            fd = src_ap.shape[-1]
            tq, ci, cm, nd, b1 = (qt[:, 0:fd], qci[:, 0:fd], qcm[:, 0:fd],
                                  qnd[:, 0:fd], qg1[:, 0:fd])
            if s_vec_ap is None:
                op("v", lambda e, a=src_ap, o=tq: e.tensor_copy(o, a), wait=pre_wait)
            else:
                op("v", lambda e, a=src_ap, sv=s_vec_ap, o=tq:
                   e.tensor_scalar(o, a, sv, None, OP.mult), wait=pre_wait)
            if clip1:
                op("v", lambda e, a=nhalf[:, 0:fd], b=tq, o=b1:
                   e.tensor_tensor(o, a, b, OP.is_gt))
                op("v", lambda e, a=tq, b=phalf[:, 0:fd], o=cm:
                   e.tensor_tensor(o, a, b, OP.is_gt))
                op("v", lambda e, a=b1, b=cm, o=dst_ap: e.tensor_tensor(o, a, b, OP.subtract))
                return
            op("v", lambda e, a=tq, o=ci: e.tensor_scalar(o, a, MAGIC, None, OP.add))
            op("v", lambda e, a=ci, o=cm: e.tensor_scalar(o, a, -MAGIC, None, OP.add))
            op("v", lambda e, a=cm, b=tq, o=nd: e.tensor_tensor(o, a, b, OP.subtract))
            op("v", lambda e, a=nhalf[:, 0:fd], b=nd, o=b1:
               e.tensor_tensor(o, a, b, OP.is_gt))
            op("v", lambda e, a=nd, b=phalf[:, 0:fd], o=tq:
               e.tensor_tensor(o, a, b, OP.is_gt))
            op("v", lambda e, a=tq, b=b1, o=nd: e.tensor_tensor(o, a, b, OP.subtract))
            op("v", lambda e, a=nd, b=cm, o=dst_ap: e.tensor_tensor(o, a, b, OP.subtract))

        qtP = [qt[:, 0:512], x_grp[1][:, 1024:1536]]

        def gq_front(src_ap, s_vec_ap, par, qt_guard=(), pre_wait=()):
            cmP = gcm[par]
            tP = qtP[par]
            ndX = qnd[:, 0:512] if par == 0 else gnd2
            op("v", lambda e, a=src_ap, sv=s_vec_ap, o=tP:
               e.tensor_scalar(o, a, sv, None, OP.mult),
               wait=tuple(pre_wait) + tuple(qt_guard))
            op("v", lambda e, a=tP, o=qci[:, 0:512]:
               e.tensor_scalar(o, a, MAGIC, None, OP.add))
            op("v", lambda e, a=qci[:, 0:512], o=cmP:
               e.tensor_scalar(o, a, -MAGIC, None, OP.add))
            vf = wv(V)
            op("gc", lambda e, a=cmP, b=tP, o=ndX:
               e.tensor_tensor(o, a, b, OP.subtract), wait=(vf,))
            return (ndX, cmP, wv(GC))

        def gq_tail(state, dst_ap, b_guard=()):
            ndX, cmP, gmark = state
            op("v", lambda e, a=nhalf[:, 0:512], b=ndX, o=gb1:
               e.tensor_tensor(o, a, b, OP.is_gt), wait=(gmark,) + tuple(b_guard))
            op("v", lambda e, a=ndX, b=phalf[:, 0:512], o=gb2:
               e.tensor_tensor(o, a, b, OP.is_gt))
            op("v", lambda e, a=gb2, b=gb1, o=gc1:
               e.tensor_tensor(o, a, b, OP.subtract))
            op("v", lambda e, a=gc1, b=cmP, o=dst_ap:
               e.tensor_tensor(o, a, b, OP.subtract))
            return wv(V)

        def emit_round_neg_gw(src_ap, s_vec_ap, dst_ap, pre_wait=(), b_guard=()):
            op("v", lambda e, a=src_ap, sv=s_vec_ap, o=qt[:, 0:512]:
               e.tensor_scalar(o, a, sv, None, OP.mult), wait=pre_wait)
            op("v", lambda e, a=nhalf[:, 0:512], b=qt[:, 0:512], o=gb1:
               e.tensor_tensor(o, a, b, OP.is_gt), wait=tuple(b_guard))
            op("v", lambda e, a=qt[:, 0:512], b=phalf[:, 0:512], o=gb2:
               e.tensor_tensor(o, a, b, OP.is_gt))
            vb = wv(V)
            op("gc", lambda e, a=gb1, b=gb2, o=dst_ap:
               e.tensor_tensor(o, a, b, OP.subtract), wait=(vb,))
            return wv(GC)

        tp_state = {"n": 0, "marks": []}

        def emit_xpose4(srcs, dst_ap, src_wait=(), extra_wait=()):
            n = tp_state["n"]
            tp_state["n"] += 1
            sem = TD[n % 8]
            twait = list(src_wait) + list(extra_wait)
            if n - 8 >= 0:
                twait.append(tp_state["marks"][n - 8])
            for i, s4 in enumerate(srcs):
                op("y", lambda e, i_=s4, o=dst_ap[:, i * 128:(i + 1) * 128]:
                   e.dma_start_transpose(o, i_),
                   sem=sem, inc=16, wait=tuple(twait) if i == 0 else ())
            tp_state["marks"].append(wv(sem))

        def last_xpose_marks(k=8):
            return tuple(tp_state["marks"][-k:]) if tp_state["marks"] else ()

        for dst, src in ((consts_sb, d_consts), (idf, d_idf),
                         (ones128, d_ones), (onesT, d_onesT), (bias_sb, d_bias)):
            op("g", lambda e, o=dst, i=src: e.dma_start(out=o[:], in_=i), sem=G, inc=16)
        for nm, src in (("g", d_Wg), ("v", d_Wv), ("d", d_Wd)):
            for k in range(8):
                b = ATTN_BASE[nm] + k * D
                op("g", lambda e, o=wqg[:, b:b + D], i=src[k * 128:(k + 1) * 128, :]:
                   e.dma_start(out=o, in_=i), sem=G, inc=16)
        g_after_consts = G.n

        op("v", lambda e: e.memset(carry[:], 0.0))
        op("v", lambda e: e.memset(nhalf[:], -0.5))
        op("v", lambda e: e.memset(phalf[:], 0.5))

        for i in range(NT):
            buf, sq = wchunk[0], wchunk[1]
            op("y", lambda e, o=buf, i_=d_x[i * 128:(i + 1) * 128, :]:
               e.dma_start(out=o[:], in_=i_), sem=LA0, inc=16, wait=(wv(V), wv(S)))
            ldw = wv(LA0)
            op("v", lambda e, i_=buf, o=qt[:, i:i + 1]:
               e.tensor_reduce(o, i_[:], AX.X, OP.add), wait=(ldw,))
            op("s", lambda e, i_=buf, o=sq: e.activation(o[:], i_[:], AF.Square, bias=0.0, scale=1.0),
               wait=(ldw, wv(V)))
            op("v", lambda e, i_=sq, o=qcm[:, i:i + 1]:
               e.tensor_reduce(o, i_[:], AX.X, OP.add), wait=(wv(S),))
        op("v", lambda e: e.tensor_scalar(meanall[:], qt[:, 0:NT], 1.0 / D, None, OP.mult))
        op("v", lambda e: e.tensor_scalar(qnd[:, 0:NT], qcm[:, 0:NT], 1.0 / D, None, OP.mult))
        op("v", lambda e: e.tensor_tensor(qg1[:, 0:NT], meanall[:], meanall[:], OP.mult))
        op("v", lambda e: e.tensor_tensor(varall[:], qnd[:, 0:NT], qg1[:, 0:NT], OP.subtract))
        op("s", lambda e: e.activation(qt[:, 32:32 + NT], varall[:], AF.Sqrt, bias=C_EPS, scale=1.0),
           wait=(wv(V),))
        op("v", lambda e: e.reciprocal(qcm[:, 32:32 + NT], qt[:, 32:32 + NT]), wait=(wv(S),))
        op("v", lambda e: e.tensor_tensor(qnd[:, 32:32 + NT], qcm[:, 32:32 + NT],
                                          qcm[:, 32:32 + NT], OP.mult))
        op("v", lambda e: e.tensor_scalar(qg1[:, 32:32 + NT], varall[:], 1e-5, None, OP.add))
        op("v", lambda e: e.tensor_tensor(qt[:, 64:64 + NT], qnd[:, 32:32 + NT],
                                          qg1[:, 32:32 + NT], OP.mult))
        op("v", lambda e: e.tensor_scalar(qcm[:, 64:64 + NT], qt[:, 64:64 + NT],
                                          -0.5, 1.5, OP.mult, OP.add))
        op("v", lambda e: e.tensor_tensor(rstdall[:], qcm[:, 32:32 + NT],
                                          qcm[:, 64:64 + NT], OP.mult))

        sa_group = {}
        xqT_store = {}
        xq_xp_marks = {}
        hbank_mark = [None, None]
        act_marks = [None, None]
        sn_group = {}
        mm_done_mark = None
        xg_read_done = {}
        for g in range(NG):
            xg = x_grp[g % 2]
            lx = LX[g % 2]
            xl_wait = (xg_read_done[g - 2],) if g >= 2 else (wv(V),)
            for tt in range(2):
                op("y", lambda e, o=xg[:, tt * D:(tt + 1) * D],
                   i_=d_x[g * GRP + tt * 128:g * GRP + (tt + 1) * 128, :]:
                   e.dma_start(out=o, in_=i_),
                   sem=lx, inc=16, wait=xl_wait if tt == 0 else ())
            ld_x = wv(lx)
            if g > 0:
                op("v", lambda e: e.nop(), wait=((SA, sa_group[g - 1]),))
                op("s", lambda e: e.nop(), wait=((SA, sa_group[g - 1]),))

            for tt in range(2):
                idx = 2 * g + tt
                xtile = xg[:, tt * D:(tt + 1) * D]
                xnb = xnbf(tt * D, D)
                ln_wait = [ld_x]
                if g > 0 and tt == 0:
                    ln_wait.append((SN, sn_group[g - 1]))
                op("v", lambda e, i_=xtile, o=xnb, m=meanall[:, idx:idx + 1],
                   r=rstdall[:, idx:idx + 1]:
                   e.tensor_scalar(o, i_, m, r, OP.subtract, OP.mult),
                   wait=tuple(ln_wait))
                mx, mc, rec, sv = scal[:, 0:1], scal[:, 1:2], scal[:, 2:3], scal[:, 3:4]
                op("v", lambda e, i_=xnb, o=mx:
                   e.tensor_reduce(o, i_, AX.X, OP.max, apply_absolute_value=True))
                op("v", lambda e, i_=mx, o=mc: e.tensor_scalar(o, i_, 1e-5, None, OP.max))
                op("v", lambda e, i_=mc, o=rec: e.reciprocal(o, i_))
                op("v", lambda e, i_=rec, o=sv: e.tensor_scalar(o, i_, 127.0, None, OP.mult))
                op("v", lambda e, i_=mc, o=dxall[:, idx:idx + 1]:
                   e.tensor_scalar(o, i_, 1.0 / 127.0, None, OP.mult))
                for pc in range(2):
                    qw = tuple(xq_xp_marks[2 * (g - 1) + tt]) if (g > 0 and pc == 0) else ()
                    emit_round_neg(xnb[:, pc * 512:(pc + 1) * 512], sv,
                                   pqstage[:, tt * D + pc * 512:tt * D + (pc + 1) * 512],
                                   pre_wait=qw)
            vq_done = wv(V)
            for tt in range(2):
                rows = g * GRP + tt * 128
                op("y", lambda e, i_=xnbf(tt * D, D), o=scr_xn[rows:rows + 128, :]:
                   e.dma_start(out=o, in_=i_), sem=SN, inc=16, wait=(vq_done,))
            sn_done = wv(SN)
            for tt in range(2):
                ti2 = 2 * g + tt
                xstg = qtmp if ti2 % 2 == 0 else pqroll
                xw = [vq_done]
                if ti2 - 2 in xqT_store:
                    xw.append(xqT_store[ti2 - 2])
                for half in range(2):
                    emit_xpose4(
                        [pqstage[:, tt * D + half * 512 + b4 * 128:
                                 tt * D + half * 512 + (b4 + 1) * 128]
                         for b4 in range(4)],
                        xstg[:, half * 512:(half + 1) * 512],
                        src_wait=tuple(xw))
                xq_xp_marks[ti2] = tuple(last_xpose_marks(2))
                op("y", lambda e, i_=xstg, o=scr_xqT[ti2]:
                   e.dma_start(out=o, in_=i_[:]), sem=SXT[ti2 % 2], inc=16,
                   wait=xq_xp_marks[ti2])
                xqT_store[ti2] = wv(SXT[ti2 % 2])
            xn_rd_wait = [sn_done]
            if g > 0:
                xn_rd_wait.append(mm_done_mark)
            for k in range(8):
                op("y", lambda e, o=xnF(k * GRP, GRP),
                   i_=scr_xn[g * GRP:(g + 1) * GRP, k * 128:(k + 1) * 128]:
                   e.dma_start_transpose(o, i_), sem=TD[k], inc=16,
                   wait=tuple(xn_rd_wait) if k == 0 else ())
            xpose_marks_g = last_xpose_marks(8)

            xn_evac_marks = tuple((TD[k], TD[k].n) for k in range(8))
            sd_mark = None
            eps_mark = None
            scan_marks = {}

            def emit_htrans_stage(mh):
                for tt in range(2):
                    pdst2 = pb[4 + tt][:, 0:128]
                    tw = [scan_marks[mh]]
                    if hbank_mark[tt] is not None:
                        tw.append(hbank_mark[tt])
                    op("t", lambda e, i_=epi[:, 5 * GRP + tt * 128:5 * GRP + (tt + 1) * 128], o=pdst2:
                       e.transpose(o, i_, idf[:]), sem=T, inc=1, wait=tuple(tw))
                    hmark = wv(T)
                    stage = (qt, qcm, qnd, qg1)[tt * 2 + (mh // 4)]
                    scol = (mh % 4) * 128
                    op("v", lambda e, i_=pdst2, o=stage[:, scol:scol + 128],
                       x_=xg[:, tt * D + mh * 128: tt * D + mh * 128 + 128]:
                       e.tensor_tensor(o, i_, x_, OP.add), wait=(hmark,))
                    hbank_mark[tt] = wv(V)

            def emit_acc_dma(h2):
                vd = wv(V)
                for tt in range(2):
                    stage = (qt, qcm, qnd, qg1)[tt * 2 + h2]
                    rows = g * GRP + tt * 128
                    op("y", lambda e, i_=stage,
                       o=scr_acc[rows:rows + 128, h2 * 512:(h2 + 1) * 512]:
                       e.dma_start(out=o, in_=i_[:]), sem=SA, inc=16, wait=(vd,))

            for m in range(8):
                par = m % 2
                pg = pb[2 * par][:, 0:GRP]
                pv = pb[2 * par][:, GRP:2 * GRP]
                pd = pb[2 * par + 1][:, 0:GRP]
                mm_wait = list(xn_evac_marks) if m < 2 else []
                if act_marks[par] is not None:
                    mm_wait.append(act_marks[par])
                mm_wait = tuple(mm_wait)
                first = True
                for mat, pdst in (("g", pg), ("v", pv), ("d", pd)):
                    for k in range(8):
                        op("t", lambda e, l=attn_lhs(mat, k, m),
                           r=xnF(k * GRP, GRP), o=pdst, st=(k == 0), sp=(k == 7):
                           e.matmul(o, l, r, start=st, stop=sp),
                           sem=T, inc=(1 if k == 7 else 0),
                           wait=mm_wait if first else ())
                        first = False
                t_gvd = wv(T)
                if m >= 1:
                    emit_htrans_stage(m - 1)
                    if m == 4:
                        emit_acc_dma(0)
                op("s", lambda e, i_=pg, o=sgb, b=bias_sb[:, m:m + 1]:
                   e.activation(o, i_, AF.Sigmoid, bias=b, scale=1.0),
                   wait=(t_gvd,) if eps_mark is None else (t_gvd, eps_mark))
                op("s", lambda e, i_=pv, o=thb, b=bias_sb[:, 8 + m:9 + m]:
                   e.activation(o, i_, AF.Tanh, bias=b, scale=1.0))
                op("s", lambda e, i_=pd, o=sdb, b=bias_sb[:, 16 + m:17 + m]:
                   e.activation(o, i_, AF.Sigmoid, bias=b, scale=1.0))
                sd_mark = wv(S)
                act_marks[par] = sd_mark
                op("v", lambda e: e.tensor_tensor(xsb, sgb,
                                                  thb, OP.mult), wait=(sd_mark,))
                op("v", lambda e: e.tensor_scalar(ab, sdb,
                                                  0.998, 0.001, OP.mult, OP.add))
                eps_mark = wv(V)
                op("v", lambda e, c=carry[:, m:m + 1]: e.tensor_tensor_scan(
                    hb, ab, xsb, c, OP.mult, OP.add))
                op("v", lambda e, c=carry[:, m:m + 1]: e.tensor_copy(c, epi[:, 6 * GRP - 1:6 * GRP]))
                scan_marks[m] = wv(V)
            emit_htrans_stage(7)
            xg_read_done[g] = wv(V)
            emit_acc_dma(1)

            sa_group[g] = SA.n
            sn_group[g] = sn_done[1]
            mm_done_mark = wv(T)

        op("v", lambda e: e.nop(), wait=((SA, sa_group[NG - 1]),))
        op("s", lambda e: e.nop(), wait=((SA, sa_group[NG - 1]),))

        def prep_scalars(sw_col, dw_col, inv_n):
            op("t", lambda e: e.matmul(pb[5][0:1, 0:1], ones128[:], scal[:, 4:5],
                                       start=True, stop=True), sem=T, inc=1, wait=(wv(V),))
            op("v", lambda e: e.tensor_copy(tin1[0:1, 0:1], pb[5][0:1, 0:1]), wait=(wv(T),))
            op("v", lambda e: e.tensor_scalar(tin1[0:1, 1:2], tin1[0:1, 0:1], inv_n, None, OP.mult))
            op("v", lambda e: e.tensor_scalar(tin1[0:1, 2:3], tin1[0:1, 1:2], 1e-5, None, OP.max))
            op("v", lambda e: e.reciprocal(tin1[0:1, 3:4], tin1[0:1, 2:3]))
            op("v", lambda e: e.reciprocal(tin1[0:1, 4:5], tin1[0:1, 3:4]))
            for srcc, col in ((3, sw_col), (4, dw_col)):
                op("t", lambda e, i_=tin1[0:1, srcc:srcc + 1]:
                   e.matmul(pb[5][:, 1:2], onesT[:], i_, start=True, stop=True),
                   sem=T, inc=1, wait=(wv(V),))
                op("v", lambda e, o=wsc[:, col:col + 1]: e.tensor_copy(o, pb[5][:, 1:2]),
                   wait=(wv(T),))

        def prep_mean(load_list):
            npart = len(load_list) * 4
            assert npart <= 512
            vmarks = {}
            for li, src_slice in enumerate(load_list):
                buf = wchunk[li % 4]
                lp = LP[li % 4]
                lw_ = (vmarks[li - 4],) if li >= 4 else (wv(V),)
                op("y", lambda e, o=buf, i_=src_slice: e.dma_start(out=o, in_=i_),
                   sem=lp, inc=16, wait=lw_)
                ldw = wv(lp)
                for pc in range(4):
                    col = li * 4 + pc
                    op("v", lambda e, i_=buf[:, pc * 256:(pc + 1) * 256], o=qg1[:, col:col + 1]:
                       e.tensor_reduce(o, i_, AX.X, OP.add, apply_absolute_value=True),
                       wait=(ldw,) if pc == 0 else ())
                vmarks[li] = wv(V)
            op("v", lambda e, w=npart: e.tensor_reduce(scal[:, 4:5], qg1[:, 0:w], AX.X, OP.add))

        def qtmp4(cc, c0, w):
            return pqstage[:, cc * D + c0:cc * D + c0 + w]

        pq_state = {"slot_xp": {}, "xbig_st": [None], "prev_g": [None]}

        def prep_quant(load_list, sw_col, group_dst, first_guard=()):
            vmarks = {}
            prev_g = pq_state["prev_g"]
            slot_xp = pq_state["slot_xp"]
            for li, src_slice in enumerate(load_list):
                buf = wchunk[li % 4]
                lp = LP[li % 4]
                lw_ = (vmarks[li - 4],) if li >= 4 else (wv(V),)
                op("y", lambda e, o=buf, i_=src_slice: e.dma_start(out=o, in_=i_),
                   sem=lp, inc=16, wait=lw_)
                ldw = wv(lp)
                gmark = None
                for pc in range(2):
                    pw = [ldw] if pc == 0 else []
                    if pc == 0 and (li % 4) in slot_xp:
                        pw.extend(slot_xp[li % 4])
                    if pc == 0 and (li % 4) not in slot_xp:
                        pw.extend(first_guard)
                    bg = (prev_g[0],) if prev_g[0] is not None else ()
                    gmark = emit_round_neg_gw(buf[:, pc * 512:(pc + 1) * 512],
                                              wsc[:, sw_col:sw_col + 1],
                                              qtmp4(li % 4, pc * 512, 512),
                                              pre_wait=tuple(pw), b_guard=bg)
                    prev_g[0] = gmark
                vmarks[li] = wv(V)
                xw = [gmark]
                if li % 8 == 0 and pq_state["xbig_st"][0] is not None:
                    xw.extend(pq_state["xbig_st"][0])
                n = tp_state["n"]
                tp_state["n"] += 1
                sem = TD[n % 8]
                if n - 8 >= 0:
                    xw.append(tp_state["marks"][n - 8])
                cb = (li % 8) * 128
                for j in range(8):
                    op("y", lambda e, i_=qtmp4(li % 4, j * 128, 128),
                       o=xbig[:, j * 1024 + cb:j * 1024 + cb + 128]:
                       e.dma_start_transpose(o, i_), sem=sem, inc=16,
                       wait=tuple(xw) if j == 0 else ())
                tp_state["marks"].append(wv(sem))
                slot_xp[li % 4] = (tp_state["marks"][-1],)
                if li % 8 == 7:
                    gmarks = tuple(tp_state["marks"][-8:])
                    g8 = li // 8
                    for j in range(8):
                        op("y", lambda e, i_=xbig[:, j * 1024:(j + 1) * 1024],
                           o=group_dst(g8, j):
                           e.dma_start(out=o, in_=i_), sem=SWs[j % 4], inc=16,
                           wait=gmarks if j == 0 else ())
                    pq_state["xbig_st"][0] = tuple(wv(s4) for s4 in SWs)

        down_l1 = [d_wdown[c * 128:(c + 1) * 128, q * D:(q + 1) * D]
                   for c in range(8) for q in range(4)]
        prep_mean(down_l1)
        prep_scalars(4, 5, 1.0 / (D * INNER))
        down_l2 = [d_wdown[c * 128:(c + 1) * 128, q * D:(q + 1) * D]
                   for q in range(4) for c in range(8)]
        prep_quant(down_l2, 4,
                   lambda g8, j: scr_wdT[g8 * D + j * 128:g8 * D + (j + 1) * 128, :],
                   first_guard=tuple(xq_xp_marks[NT - 2]) + tuple(xq_xp_marks[NT - 1]))
        st_wd = tuple(wv(s4) for s4 in SWs)

        gate_l1 = [d_wgate[c * 128:(c + 1) * 128, :] for c in range(32)]
        prep_mean(gate_l1)
        prep_scalars(0, 1, 1.0 / (INNER * D))
        prep_quant(gate_l1, 0,
                   lambda g8, j: scr_wgT[j * 128:(j + 1) * 128,
                                         g8 * 1024:(g8 + 1) * 1024])
        st_wg = tuple(wv(s4) for s4 in SWs)

        up_l1 = [d_wup[c * 128:(c + 1) * 128, :] for c in range(32)]
        prep_mean(up_l1)
        prep_scalars(2, 3, 1.0 / (INNER * D))
        prep_quant(up_l1, 2,
                   lambda g8, j: scr_wuT[j * 128:(j + 1) * 128,
                                         g8 * 1024:(g8 + 1) * 1024])
        st_wu = tuple(wv(s4) for s4 in SWs)

        last_attn_T = wv(T)
        for k in range(8):
            op("y", lambda e, o=wqg[:, k * INNER:(k + 1) * INNER],
               i_=scr_wgT[k * 128:(k + 1) * 128, :]:
               e.dma_start(out=o, in_=i_), sem=WT, inc=16,
               wait=(st_wg + (last_attn_T,)) if k == 0 else ())
        for k in range(8):
            op("y", lambda e, o=wqu[:, k * INNER:(k + 1) * INNER],
               i_=scr_wuT[k * 128:(k + 1) * 128, :]:
               e.dma_start(out=o, in_=i_), sem=WT, inc=16,
               wait=(st_wu + (last_attn_T, (SN, sn_group[NG - 1]))) if k == 0 else ())
        wt_done = wv(WT)

        slab_ctr = {"n": 0}
        st_pq = {}
        t_after_tile = {}
        bank_marks = [None] * 4
        sg2_marks = [None, None]
        prep_w_marks = (wt_done,)
        roll = [pqroll[:, 0:512], pqroll[:, 512:1024], qtmp[:, 0:512], qtmp[:, 512:1024]]
        b1_piece_marks = {}
        roll_reader = {}
        gqs = {}
        gfin = {}

        def b1_xpose_piece(ti, pc, extra=()):
            slot = roll[pc % 4]
            emit_xpose4([slot[:, b4 * 128:(b4 + 1) * 128] for b4 in range(4)],
                        pqstage[:, pc * 512:(pc + 1) * 512],
                        src_wait=(gfin[(ti, pc)],), extra_wait=extra)
            b1_piece_marks[(ti, pc)] = tp_state["marks"][-1]
            roll_reader[(ti, pc + 4)] = tp_state["marks"][-1]
            if pc + 4 >= 8:
                roll_reader[(ti + 1, pc + 4 - 8)] = tp_state["marks"][-1]

        def b1_front(ti, pc):
            pk = (ti, pc - 2) if (ti, pc - 2) in gqs else (ti - 1, pc + 6)
            qg = (gqs[pk][2],) if pk in gqs else ()
            gqs[(ti, pc)] = gq_front(pbuf[:, pc * 512:(pc + 1) * 512],
                                     scal[:, 1:2], pc % 2, qt_guard=qg)

        def b1_tail(ti, pc):
            bg = []
            if (ti, pc) in roll_reader:
                bg.append(roll_reader[(ti, pc)])
            if ti == 0 and pc < 4:
                bg.append(xqT_store[NT - 1] if pc < 2 else xqT_store[NT - 2])
            gfin[(ti, pc)] = gq_tail(gqs[(ti, pc)], roll[pc % 4],
                                     b_guard=tuple(bg))

        st_pq_lo = {}
        st_pq_hi = {}

        def b1_store_lo(ti):
            marks = tuple(b1_piece_marks[(ti, pc)] for pc in range(4))
            op("y", lambda e, i_=pqstage[:, 0:2048], o=scr_pqF[ti][:, 0:2048]:
               e.dma_start(out=o, in_=i_), sem=SP, inc=16, wait=marks)
            st_pq_lo[ti] = SP.n

        def b1_store_hi(ti):
            marks = tuple(b1_piece_marks[(ti, pc)] for pc in range(4, 8))
            op("y", lambda e, i_=pqstage[:, 2048:4096], o=scr_pqF[ti][:, 2048:4096]:
               e.dma_start(out=o, in_=i_), sem=SP2, inc=16, wait=marks)
            st_pq_hi[ti] = SP2.n

        t_k_bykg = {}
        t_sup_mark = {}
        evac_v_marks = {}
        wd_ld = {}
        pq_ld = {}
        b2_ld_ptr = {}
        acc_stage = [qcm[:, 0:512], sg_slab[:, 0:512]]
        out_stage = [epi[:, 1024:1536], qg1[:, 0:512]]
        acc_rd_mark = [None, None]
        out_rd_mark = [None, None]
        so_done = {0: 0, 1: 0}

        def b2_load_kstep(u, k):
            kg = u * 32 + k
            lw = LW[kg % 8]
            ww = []
            if kg < 6:
                ww.extend(st_wd)
            if kg - 3 >= 0:
                ww.append((T, t_k_bykg[kg - 3]))
            op("y", lambda e, o=wdr[:, (kg % 3) * 1024:(kg % 3 + 1) * 1024],
               i_=scr_wdT[k * 128:(k + 1) * 128, :]:
               e.dma_start(out=o, in_=i_), sem=lw, inc=16, wait=tuple(ww))
            wd_ld[kg] = wv(lw)
            lp = LP[k % 4]
            pw = []
            if k == 0:
                pw.append((SP, st_pq_lo[2 * u + 1]))
            if k == 16:
                pw.append((SP2, st_pq_hi[2 * u + 1]))
            if kg - 4 >= 0:
                pw.append((T, t_k_bykg[kg - 4]))
            for j in range(2):
                op("y", lambda e,
                   o=pqs[:, (k % 4) * 256 + j * 128:(k % 4) * 256 + (j + 1) * 128],
                   i_=scr_pqF[2 * u + j][:, k * 128:(k + 1) * 128]:
                   e.dma_start(out=o, in_=i_), sem=lp, inc=16,
                   wait=tuple(pw) if j == 0 else ())
            pq_ld[kg] = wv(lp)

        def b2_mm_kstep(u, k):
            kg = u * 32 + k
            first = [wd_ld[kg], pq_ld[kg]]
            if k == 0 and (u - 1) in evac_v_marks:
                first.append((V, evac_v_marks[u - 1]))
            for j in range(2):
                for h in range(2):
                    op("t", lambda e, l=pqs[:, (k % 4) * 256 + j * 128:
                                            (k % 4) * 256 + (j + 1) * 128],
                       r=wdr[:, (kg % 3) * 1024 + h * 512:
                             (kg % 3) * 1024 + (h + 1) * 512],
                       o=pb[4 + 2 * j + h][:], st=(k == 0), sp=(k == 31):
                       e.matmul(o, l, r, start=st, stop=sp),
                       sem=T, inc=(1 if (j == 1 and h == 1) else 0),
                       wait=tuple(first) if (j == 0 and h == 0) else ())
            t_k_bykg[kg] = T.n
            if k == 31:
                t_sup_mark[u] = T.n

        def b2_step(u, sig):
            if u not in b2_ld_ptr:
                b2_ld_ptr[u] = 0
            tgt = min(2 * sig + 2, 31)
            while b2_ld_ptr[u] <= tgt:
                b2_load_kstep(u, b2_ld_ptr[u])
                b2_ld_ptr[u] += 1
            b2_mm_kstep(u, 2 * sig)
            b2_mm_kstep(u, 2 * sig + 1)

        def b2_evac(u):
            tmark = (T, t_sup_mark[u])
            for j in range(2):
                for h in range(2):
                    par = (2 * j + h) % 2
                    aw = []
                    if j == 0 and h == 0:
                        aw.append((SA, sa_group[NG - 1] if u == 0 else sa_group[u]))
                    if acc_rd_mark[par] is not None:
                        aw.append(acc_rd_mark[par])
                    op("y", lambda e, o=acc_stage[par],
                       i_=scr_acc[u * SUP + j * 128:u * SUP + (j + 1) * 128,
                                  h * 512:(h + 1) * 512]:
                       e.dma_start(out=o, in_=i_), sem=LB[par], inc=16,
                       wait=tuple(aw))
                    am = wv(LB[par])
                    sw = [tmark, am]
                    if u == 0 and par == 1 and out_rd_mark[1] is None:
                        sw.append((SA, sa_group[NG - 1]))
                    if out_rd_mark[par] is not None:
                        sw.append(out_rd_mark[par])
                    op("v", lambda e, o=out_stage[par], p_=pb[4 + 2 * j + h][:],
                       sc=dpwall[:, 2 * u + j:2 * u + j + 1], a_=acc_stage[par]:
                       e.scalar_tensor_tensor(o, p_, sc, a_, OP.mult, OP.add),
                       wait=tuple(sw))
                    acc_rd_mark[par] = wv(V)
                    rows = u * SUP + j * 128
                    op("y", lambda e, i_=out_stage[par],
                       o=d_out[rows:rows + 128, h * 512:(h + 1) * 512]:
                       e.dma_start(out=o, in_=i_[:]), sem=SO[par], inc=16,
                       wait=(acc_rd_mark[par],))
                    out_rd_mark[par] = wv(SO[par])
            evac_v_marks[u] = V.n

        for i in range(NT):
            xqt = xqF_t[i % 2]
            xq_wait = [xqT_store[i]]
            if i >= 2:
                xq_wait.append((T, t_after_tile[i - 2]))
            op("y", lambda e, o=xqt[:], i_=scr_xqT[i]:
               e.dma_start(out=o, in_=i_), sem=LQ[i % 2], inc=16,
               wait=tuple(xq_wait))
            ld_xq = (wv(LQ[i % 2]),)
            op("v", lambda e, o=dvecs[:, 0:1], a=dxall[:, i:i + 1], b=wsc[:, 1:2]:
               e.tensor_tensor(o, a, b, OP.mult), wait=(wv(S),))
            op("v", lambda e, o=dvecs[:, 1:2], a=dxall[:, i:i + 1], b=wsc[:, 3:4]:
               e.tensor_tensor(o, a, b, OP.mult))
            dv_mark = wv(V)

            for s_ in range(8):
                bkg = (2 * slab_ctr["n"]) % 4
                bku = (2 * slab_ctr["n"] + 1) % 4
                slab_ctr["n"] += 1
                pg_, pu_ = pb[bkg][:], pb[bku][:]
                mmw = list(ld_xq) if s_ == 0 else []
                if i == 0 and s_ == 0:
                    mmw.extend(prep_w_marks)
                for bk in (bkg, bku):
                    if bank_marks[bk] is not None:
                        mmw.append(bank_marks[bk])
                for k in range(8):
                    bg0 = k * INNER + s_ * 512
                    op("t", lambda e, l=xqt[:, k * 128:(k + 1) * 128],
                       r=wqg[:, bg0:bg0 + 512], o=pg_, st=(k == 0), sp=(k == 7):
                       e.matmul(o, l, r, start=st, stop=sp),
                       sem=T, inc=(1 if k == 7 else 0), wait=tuple(mmw) if k == 0 else ())
                    op("t", lambda e, l=xqt[:, k * 128:(k + 1) * 128],
                       r=wqu[:, bg0:bg0 + 512], o=pu_, st=(k == 0), sp=(k == 7):
                       e.matmul(o, l, r, start=st, stop=sp),
                       sem=T, inc=(1 if k == 7 else 0))
                tmm = wv(T)
                spar = slab_ctr["n"] % 2
                sw = [tmm]
                if sg2_marks[spar] is not None:
                    sw.append(sg2_marks[spar])
                if s_ <= 1:
                    sw.append(dv_mark)
                op("s", lambda e, i_=pg_, o=sg2[spar], sc=dvecs[:, 0:1]:
                   e.activation(o, i_, AF.Silu, bias=0.0, scale=sc), wait=tuple(sw))
                silu_mark = wv(S)
                bank_marks[bkg] = silu_mark
                if i > 0:
                    if s_ == 0:
                        b1_front(i - 1, 0)
                    if s_ <= 6:
                        b1_front(i - 1, s_ + 1)
                op("v", lambda e, i_=pu_, o=pbuf[:, s_ * 512:(s_ + 1) * 512], sc=dvecs[:, 1:2],
                   sg=sg2[spar]:
                   e.scalar_tensor_tensor(o, i_, sc, sg, OP.mult, OP.mult),
                   wait=(silu_mark,))
                sg2_marks[spar] = wv(V)
                bank_marks[bku] = wv(V)
                if i > 0:
                    b1_tail(i - 1, s_)
                    if s_ >= 1:
                        if s_ == 1 and i >= 2:
                            ex = ((SP, st_pq_lo[i - 2]),)
                        elif s_ == 1 and i == 1:
                            ex = st_wu
                        elif s_ == 5 and i >= 2:
                            ex = ((SP2, st_pq_hi[i - 2]),)
                        else:
                            ex = ()
                        b1_xpose_piece(i - 1, s_ - 1, extra=ex)
                    if s_ == 4:
                        b1_store_lo(i - 1)
                if i >= 3:
                    b2_step((i - 3) // 2, ((i - 3) % 2) * 8 + s_)
            t_after_tile[i] = T.n
            if i > 0:
                b1_xpose_piece(i - 1, 7)
                b1_store_hi(i - 1)
            mxp, mcp, recp, svp = scal[:, 6:7], scal[:, 7:8], scal[:, 0:1], scal[:, 1:2]
            op("v", lambda e, o=mxp: e.tensor_reduce(o, pbuf[:], AX.X, OP.max,
                                                     apply_absolute_value=True))
            op("v", lambda e, i_=mxp, o=mcp: e.tensor_scalar(o, i_, 1e-5, None, OP.max))
            op("v", lambda e, i_=mcp, o=recp: e.reciprocal(o, i_))
            op("v", lambda e, i_=recp, o=svp: e.tensor_scalar(o, i_, 127.0, None, OP.mult))
            op("v", lambda e, i_=mcp, o=scal[:, 2:3]:
               e.tensor_scalar(o, i_, 1.0 / 127.0, None, OP.mult))
            op("v", lambda e, o=dpwall[:, i:i + 1], a=scal[:, 2:3], b=wsc[:, 5:6]:
               e.tensor_tensor(o, a, b, OP.mult))
            if i >= 3 and (i - 3) % 2 == 1:
                b2_evac((i - 3) // 2)
        i = NT - 1
        for pc in range(8):
            b1_front(i, pc)
            b1_tail(i, pc)
            if pc >= 1:
                if pc == 1:
                    ex = ((SP, st_pq_lo[NT - 2]),)
                elif pc == 5:
                    ex = ((SP2, st_pq_hi[NT - 2]),)
                else:
                    ex = ()
                b1_xpose_piece(i, pc - 1, extra=ex)
            if pc == 4:
                b1_store_lo(i)
        b1_xpose_piece(i, 7)
        b1_store_hi(i)

        for sig in range(8, 16):
            b2_step(14, sig)
        b2_evac(14)
        for sig in range(16):
            b2_step(15, sig)
        b2_evac(15)

        fin0, fin1 = SO[0].n, SO[1].n
        op("y", lambda e, a=fin0, b=fin1: e.wait_ge(SO[0].h, a))
        op("y", lambda e, a=fin0, b=fin1: e.wait_ge(SO[1].h, b))

        @block.sync
        def _(e):
            for f in plan["y"]:
                f(e)

        @block.gpsimd
        def _(e):
            for f in plan["g"]:
                f(e)

        @block.vector
        def _(e):
            e.wait_ge(G.h, g_after_consts)
            for f in plan["v"]:
                f(e)

        @block.scalar
        def _(e):
            e.wait_ge(G.h, g_after_consts)
            for f in plan["s"]:
                f(e)

        @block.tensor
        def _(e):
            e.wait_ge(G.h, g_after_consts)
            for f in plan["t"]:
                f(e)

    return nc


_NC_CACHE = {}


def kernel(L0_attn_x, L0_attn_Wg, L0_attn_bg, L0_attn_Wv, L0_attn_bv,
           L0_attn_Wd, L0_attn_bd, L0_ffn_w_gate, L0_ffn_w_up, L0_ffn_w_down):
    x = np.ascontiguousarray(np.asarray(L0_attn_x, dtype=np.float32))
    B = x.shape[0]
    Wg = np.ascontiguousarray(np.asarray(L0_attn_Wg, np.float32))
    Wv = np.ascontiguousarray(np.asarray(L0_attn_Wv, np.float32))
    Wd = np.ascontiguousarray(np.asarray(L0_attn_Wd, np.float32))
    bg = np.asarray(L0_attn_bg, np.float32).reshape(8, 128).T
    bv = np.asarray(L0_attn_bv, np.float32).reshape(8, 128).T
    bd = np.asarray(L0_attn_bd, np.float32).reshape(8, 128).T
    biasT = np.ascontiguousarray(np.concatenate([bg, bv, bd], axis=1))
    wgate = np.ascontiguousarray(np.asarray(L0_ffn_w_gate, np.float32))
    wup = np.ascontiguousarray(np.asarray(L0_ffn_w_up, np.float32))
    wdown = np.ascontiguousarray(np.asarray(L0_ffn_w_down, np.float32))

    consts = np.zeros((128, 16), np.float32)
    consts[:, 0] = -0.5
    consts[:, 1] = 0.5
    consts[:, 2] = MAGIC
    consts[:, 3] = -MAGIC
    consts[:, 4] = 1e-5
    identf = np.eye(128, dtype=np.float32)
    identb = np.asarray(np.eye(128, dtype=np.float32).astype(ml_dtypes.bfloat16))
    ones128 = np.ones((128, 1), np.float32)
    onesT = np.ones((1, 128), np.float32)

    if "nc" not in _NC_CACHE:
        _NC_CACHE["nc"] = build_program()
    nc = _NC_CACHE["nc"]

    in_maps = []
    for b in range(B):
        in_maps.append(dict(
            x=np.ascontiguousarray(x[b]), Wg=Wg, Wv=Wv, Wd=Wd, biasT=biasT,
            wgate=wgate, wup=wup, wdown=wdown, consts=consts,
            identf=identf, identb=identb, ones128=ones128, onesT=onesT,
        ))
    res = run_bass_kernel_spmd(nc, in_maps, core_ids=list(range(B)))
    out = np.stack([np.asarray(r["out"], np.float32) for r in res.results], axis=0)
    return out


# version 16
# speedup vs baseline: 1.0259x; 1.0259x over previous
import numpy as np
import ml_dtypes
import concourse.bass as bass
import concourse.mybir as mybir
from concourse.bass_utils import run_bass_kernel_spmd
from contextlib import ExitStack

F32 = mybir.dt.float32
BF16 = mybir.dt.bfloat16
I32 = mybir.dt.int32
AF = mybir.ActivationFunctionType
OP = mybir.AluOpType
AX = mybir.AxisListType

MAGIC = 12582912.0
T_TOK = 4096
D = 1024
INNER = 4096
NT = T_TOK // 128
GRP = 256
NG = T_TOK // GRP
SUP = 256
NSUP = T_TOK // SUP


class SemC:
    def __init__(self, name):
        self.name = name
        self.h = None
        self.n = 0


def build_program():
    nc = bass.Bass("TRN2", target_bir_lowering=False)

    d_x = nc.dram_tensor("x", [T_TOK, D], F32, kind="ExternalInput").ap()
    d_Wg = nc.dram_tensor("Wg", [D, D], F32, kind="ExternalInput").ap()
    d_Wv = nc.dram_tensor("Wv", [D, D], F32, kind="ExternalInput").ap()
    d_Wd = nc.dram_tensor("Wd", [D, D], F32, kind="ExternalInput").ap()
    d_bias = nc.dram_tensor("biasT", [128, 24], F32, kind="ExternalInput").ap()
    d_wgate = nc.dram_tensor("wgate", [INNER, D], F32, kind="ExternalInput").ap()
    d_wup = nc.dram_tensor("wup", [INNER, D], F32, kind="ExternalInput").ap()
    d_wdown = nc.dram_tensor("wdown", [D, INNER], F32, kind="ExternalInput").ap()
    d_consts = nc.dram_tensor("consts", [128, 16], F32, kind="ExternalInput").ap()
    d_idf = nc.dram_tensor("identf", [128, 128], F32, kind="ExternalInput").ap()
    d_idb = nc.dram_tensor("identb", [128, 128], BF16, kind="ExternalInput").ap()
    d_ones = nc.dram_tensor("ones128", [128, 1], F32, kind="ExternalInput").ap()
    d_onesT = nc.dram_tensor("onesT", [1, 128], F32, kind="ExternalInput").ap()
    d_out = nc.dram_tensor("out", [T_TOK, D], F32, kind="ExternalOutput").ap()

    scr_acc = nc.dram_tensor("scr_acc", [T_TOK, D], F32, kind="Internal").ap()
    scr_xqT = nc.dram_tensor("scr_xqT", [NT, 128, D], BF16, kind="Internal").ap()
    scr_xn = nc.dram_tensor("scr_xn", [T_TOK, D], BF16, kind="Internal").ap()
    scr_pqF = nc.dram_tensor("scr_pqF", [NT, 128, INNER], BF16, kind="Internal").ap()
    scr_wgT = nc.dram_tensor("scr_wgT", [D, INNER], BF16, kind="Internal").ap()
    scr_wuT = nc.dram_tensor("scr_wuT", [D, INNER], BF16, kind="Internal").ap()
    scr_wdT = nc.dram_tensor("scr_wdT", [INNER, D], BF16, kind="Internal").ap()

    es = ExitStack()

    def sb(name, shape, dt):
        return es.enter_context(nc.sbuf_tensor(name, shape, dt))

    def ps(name, shape, dt):
        return es.enter_context(nc.psum_tensor(name, shape, dt))

    with es:
        wqg = sb("wqg", [128, 8 * INNER], BF16)
        wqu = sb("wqu", [128, 8 * INNER], BF16)
        x_grp = [sb("x_grp0", [128, 2 * D], F32), sb("x_grp1", [128, 2 * D], F32)]
        epi = sb("epi", [128, 6 * GRP], F32)
        sgb = epi[:, 0 * GRP:1 * GRP]
        thb = epi[:, 1 * GRP:2 * GRP]
        sdb = epi[:, 2 * GRP:3 * GRP]
        xsb = epi[:, 3 * GRP:4 * GRP]
        ab = epi[:, 4 * GRP:5 * GRP]
        hb = epi[:, 5 * GRP:6 * GRP]
        carry = sb("carry", [128, 8], F32)
        qt = sb("qt", [128, 512], F32)
        qcm = sb("qcm", [128, 512], F32)
        qnd = sb("qnd", [128, 512], F32)
        qg1 = sb("qg1", [128, 512], F32)
        nhalf = sb("nhalf", [128, 512], F32)
        phalf = sb("phalf", [128, 512], F32)
        qci = sb("qci", [128, 512], I32)
        qtmp = sb("qtmp", [128, D], BF16)
        meanall = sb("meanall", [128, NT], F32)
        rstdall = sb("rstdall", [128, NT], F32)
        dxall = sb("dxall", [128, NT], F32)
        varall = dxall
        dpwall = sb("dpwall", [128, NT], F32)
        wsc = sb("wsc", [128, 8], F32)
        consts_sb = sb("consts_sb", [128, 16], F32)
        bias_sb = sb("bias_sb", [128, 24], F32)
        idf = sb("idf_sb", [128, 128], F32)
        ones128 = sb("ones128_sb", [128, 1], F32)
        onesT = sb("onesT_sb", [1, 128], F32)
        scal = sb("scal", [128, 8], F32)
        tin1 = sb("tin1", [1, 8], F32)
        pbuf = sb("pbuf", [128, INNER], F32)
        sg_slab = sb("sg_slab", [128, 512], F32)
        sg2 = [epi[:, 512:1024], epi[:, 0:512]]
        wdr = sb("wdr", [128, 3 * 1024], BF16)
        pqs = sb("pqs", [128, 4 * 256], BF16)
        pqroll = sb("pqroll", [128, 1024], BF16)
        pqstage = sb("pqstage", [128, INNER], BF16)
        xqF_t = [sb("xqF_t0", [128, 8 * 128], BF16), sb("xqF_t1", [128, 8 * 128], BF16)]
        xbig = wqu[:, 8192:16384]
        dvecs = sb("dvecs", [128, 4], F32)
        wchunk = [pbuf[:, j * D:(j + 1) * D] for j in range(4)]

        def accsup(c0, w):
            return pbuf[:, c0:c0 + w]

        gnd2 = x_grp[0][:, 0:512]
        gb1 = x_grp[0][:, 512:1024]
        gb2 = x_grp[0][:, 1024:1536]
        gc1 = x_grp[0][:, 1536:2048]
        gcm = [x_grp[1][:, 0:512], x_grp[1][:, 512:1024]]

        ATTN_BASE = {"g": 0, "v": 8192, "d": 16384}

        def attn_lhs(mat, k, m):
            b = ATTN_BASE[mat] + k * D + m * 128
            return wqg[:, b:b + 128]

        def xnF(c0, w):
            return wqu[:, c0:c0 + w]

        def xqF(c0, w):
            return wqu[:, 2048 + c0:2048 + c0 + w]

        def xnbf(c0, w):
            return wqu[:, 4096 + c0:4096 + c0 + w]

        def wdT_build(c0, w):
            return wqu[:, c0:c0 + w]

        def pqF_sup(c0, w):
            return wqg[:, c0:c0 + w]

        def wdT_str(j):
            return wqg[:, 8192 + j * D:8192 + (j + 1) * D]

        pb = [ps(f"pb{i}", [128, 512], F32) for i in range(8)]

        names = ["G", "T", "V", "S", "LA0", "LX0", "LX1", "LP0", "LP1",
                 "LQ0", "LQ1", "LB0", "LB1", "LW0", "LW1", "LW2", "LW3",
                 "LP2", "LP3", "LW4", "LW5", "LW6", "LW7",
                 "TD0", "TD1", "TD2", "TD3", "TD4", "TD5", "TD6", "TD7", "SN", "WT",
                 "SW1", "SW2", "SW3", "GC",
                 "SA", "SX", "SW", "SP", "SP2", "SO0", "SO1", "SXT0", "SXT1"]
        sems = {n: SemC(n) for n in names}
        for smc in sems.values():
            smc.h = es.enter_context(nc.semaphore(smc.name))
        G, T, V, S = sems["G"], sems["T"], sems["V"], sems["S"]
        LA0 = sems["LA0"]
        LX = [sems["LX0"], sems["LX1"]]
        LP = [sems["LP0"], sems["LP1"], sems["LP2"], sems["LP3"]]
        LQ = [sems["LQ0"], sems["LQ1"]]
        LB = [sems["LB0"], sems["LB1"]]
        LW = [sems[f"LW{j}"] for j in range(8)]
        SA, SX, SW, SP = sems["SA"], sems["SX"], sems["SW"], sems["SP"]
        SP2 = sems["SP2"]
        SXT = [sems["SXT0"], sems["SXT1"]]
        SO = [sems["SO0"], sems["SO1"]]
        TD = [sems[f"TD{j}"] for j in range(8)]
        SN, WT = sems["SN"], sems["WT"]
        SWs = [sems["SW"], sems["SW1"], sems["SW2"], sems["SW3"]]
        GC = sems["GC"]

        block = es.enter_context(nc.Block())
        plan = {"t": [], "v": [], "s": [], "g": [], "y": []}
        ENG_SEM = {"v": V, "s": S, "gc": GC}

        def op(eng, build, sem=None, inc=0, wait=()):
            waits = list(wait)
            if eng == "gc":
                eng = "g"
                esem = ENG_SEM["gc"]
                if esem.n > 0:
                    waits.append((esem, esem.n))
                sem, inc = esem, 1
            elif eng in ("v", "s"):
                esem = ENG_SEM[eng]
                if esem.n > 0:
                    waits.append((esem, esem.n))
                sem, inc = esem, 1
            for (smc, val) in waits:
                if val > 0:
                    plan[eng].append(lambda e, _s=smc, _v=val: e.wait_ge(_s.h, _v))
            if sem is not None and inc:
                sem.n += inc

            def f(e, _b=build, _s=sem, _i=inc):
                ins = _b(e)
                if _s is not None and _i:
                    ins.then_inc(_s.h, _i)
            plan[eng].append(f)

        def wv(smc):
            return (smc, smc.n)

        C_NH = consts_sb[:, 0:1]
        C_PH = consts_sb[:, 1:2]
        C_MG = consts_sb[:, 2:3]
        C_NMG = consts_sb[:, 3:4]
        C_EPS = consts_sb[:, 4:5]

        def emit_round_neg(src_ap, s_vec_ap, dst_ap, clip1=False, pre_wait=()):
            fd = src_ap.shape[-1]
            tq, ci, cm, nd, b1 = (qt[:, 0:fd], qci[:, 0:fd], qcm[:, 0:fd],
                                  qnd[:, 0:fd], qg1[:, 0:fd])
            if s_vec_ap is None:
                op("v", lambda e, a=src_ap, o=tq: e.tensor_copy(o, a), wait=pre_wait)
            else:
                op("v", lambda e, a=src_ap, sv=s_vec_ap, o=tq:
                   e.tensor_scalar(o, a, sv, None, OP.mult), wait=pre_wait)
            if clip1:
                op("v", lambda e, a=nhalf[:, 0:fd], b=tq, o=b1:
                   e.tensor_tensor(o, a, b, OP.is_gt))
                op("v", lambda e, a=tq, b=phalf[:, 0:fd], o=cm:
                   e.tensor_tensor(o, a, b, OP.is_gt))
                op("v", lambda e, a=b1, b=cm, o=dst_ap: e.tensor_tensor(o, a, b, OP.subtract))
                return
            op("v", lambda e, a=tq, o=ci: e.tensor_scalar(o, a, MAGIC, None, OP.add))
            op("v", lambda e, a=ci, o=cm: e.tensor_scalar(o, a, -MAGIC, None, OP.add))
            op("v", lambda e, a=cm, b=tq, o=nd: e.tensor_tensor(o, a, b, OP.subtract))
            op("v", lambda e, a=nhalf[:, 0:fd], b=nd, o=b1:
               e.tensor_tensor(o, a, b, OP.is_gt))
            op("v", lambda e, a=nd, b=phalf[:, 0:fd], o=tq:
               e.tensor_tensor(o, a, b, OP.is_gt))
            op("v", lambda e, a=tq, b=b1, o=nd: e.tensor_tensor(o, a, b, OP.subtract))
            op("v", lambda e, a=nd, b=cm, o=dst_ap: e.tensor_tensor(o, a, b, OP.subtract))

        qtP = [qt[:, 0:512], x_grp[1][:, 1024:1536]]

        def gq_front(src_ap, s_vec_ap, par, qt_guard=(), pre_wait=()):
            cmP = gcm[par]
            tP = qtP[par]
            ndX = qnd[:, 0:512] if par == 0 else gnd2
            op("v", lambda e, a=src_ap, sv=s_vec_ap, o=tP:
               e.tensor_scalar(o, a, sv, None, OP.mult),
               wait=tuple(pre_wait) + tuple(qt_guard))
            op("v", lambda e, a=tP, o=qci[:, 0:512]:
               e.tensor_scalar(o, a, MAGIC, None, OP.add))
            op("v", lambda e, a=qci[:, 0:512], o=cmP:
               e.tensor_scalar(o, a, -MAGIC, None, OP.add))
            vf = wv(V)
            op("gc", lambda e, a=cmP, b=tP, o=ndX:
               e.tensor_tensor(o, a, b, OP.subtract), wait=(vf,))
            return (ndX, cmP, wv(GC))

        def gq_tail(state, dst_ap, b_guard=()):
            ndX, cmP, gmark = state
            op("v", lambda e, a=nhalf[:, 0:512], b=ndX, o=gb1:
               e.tensor_tensor(o, a, b, OP.is_gt), wait=(gmark,) + tuple(b_guard))
            op("v", lambda e, a=ndX, b=phalf[:, 0:512], o=gb2:
               e.tensor_tensor(o, a, b, OP.is_gt))
            op("v", lambda e, a=gb2, b=gb1, o=gc1:
               e.tensor_tensor(o, a, b, OP.subtract))
            op("v", lambda e, a=gc1, b=cmP, o=dst_ap:
               e.tensor_tensor(o, a, b, OP.subtract))
            return wv(V)

        def emit_round_neg_gw(src_ap, s_vec_ap, dst_ap, pre_wait=(), b_guard=()):
            op("v", lambda e, a=src_ap, sv=s_vec_ap, o=qt[:, 0:512]:
               e.tensor_scalar(o, a, sv, None, OP.mult), wait=pre_wait)
            op("v", lambda e, a=nhalf[:, 0:512], b=qt[:, 0:512], o=gb1:
               e.tensor_tensor(o, a, b, OP.is_gt), wait=tuple(b_guard))
            op("v", lambda e, a=qt[:, 0:512], b=phalf[:, 0:512], o=gb2:
               e.tensor_tensor(o, a, b, OP.is_gt))
            vb = wv(V)
            op("gc", lambda e, a=gb1, b=gb2, o=dst_ap:
               e.tensor_tensor(o, a, b, OP.subtract), wait=(vb,))
            return wv(GC)

        tp_state = {"n": 0, "marks": []}

        def emit_xpose4(srcs, dst_ap, src_wait=(), extra_wait=()):
            n = tp_state["n"]
            tp_state["n"] += 1
            sem = TD[n % 8]
            twait = list(src_wait) + list(extra_wait)
            if n - 8 >= 0:
                twait.append(tp_state["marks"][n - 8])
            for i, s4 in enumerate(srcs):
                op("y", lambda e, i_=s4, o=dst_ap[:, i * 128:(i + 1) * 128]:
                   e.dma_start_transpose(o, i_),
                   sem=sem, inc=16, wait=tuple(twait) if i == 0 else ())
            tp_state["marks"].append(wv(sem))

        def last_xpose_marks(k=8):
            return tuple(tp_state["marks"][-k:]) if tp_state["marks"] else ()

        for dst, src in ((consts_sb, d_consts), (idf, d_idf),
                         (ones128, d_ones), (onesT, d_onesT), (bias_sb, d_bias)):
            op("g", lambda e, o=dst, i=src: e.dma_start(out=o[:], in_=i), sem=G, inc=16)
        for nm, src in (("g", d_Wg), ("v", d_Wv), ("d", d_Wd)):
            for k in range(8):
                b = ATTN_BASE[nm] + k * D
                op("g", lambda e, o=wqg[:, b:b + D], i=src[k * 128:(k + 1) * 128, :]:
                   e.dma_start(out=o, in_=i), sem=G, inc=16)
        g_after_consts = G.n

        op("v", lambda e: e.memset(carry[:], 0.0))
        op("v", lambda e: e.memset(nhalf[:], -0.5))
        op("v", lambda e: e.memset(phalf[:], 0.5))

        for i in range(NT):
            buf, sq = wchunk[0], wchunk[1]
            op("y", lambda e, o=buf, i_=d_x[i * 128:(i + 1) * 128, :]:
               e.dma_start(out=o[:], in_=i_), sem=LA0, inc=16, wait=(wv(V), wv(S)))
            ldw = wv(LA0)
            op("v", lambda e, i_=buf, o=qt[:, i:i + 1]:
               e.tensor_reduce(o, i_[:], AX.X, OP.add), wait=(ldw,))
            op("s", lambda e, i_=buf, o=sq: e.activation(o[:], i_[:], AF.Square, bias=0.0, scale=1.0),
               wait=(ldw, wv(V)))
            op("v", lambda e, i_=sq, o=qcm[:, i:i + 1]:
               e.tensor_reduce(o, i_[:], AX.X, OP.add), wait=(wv(S),))
        op("v", lambda e: e.tensor_scalar(meanall[:], qt[:, 0:NT], 1.0 / D, None, OP.mult))
        op("v", lambda e: e.tensor_scalar(qnd[:, 0:NT], qcm[:, 0:NT], 1.0 / D, None, OP.mult))
        op("v", lambda e: e.tensor_tensor(qg1[:, 0:NT], meanall[:], meanall[:], OP.mult))
        op("v", lambda e: e.tensor_tensor(varall[:], qnd[:, 0:NT], qg1[:, 0:NT], OP.subtract))
        op("s", lambda e: e.activation(qt[:, 32:32 + NT], varall[:], AF.Sqrt, bias=C_EPS, scale=1.0),
           wait=(wv(V),))
        op("v", lambda e: e.reciprocal(qcm[:, 32:32 + NT], qt[:, 32:32 + NT]), wait=(wv(S),))
        op("v", lambda e: e.tensor_tensor(qnd[:, 32:32 + NT], qcm[:, 32:32 + NT],
                                          qcm[:, 32:32 + NT], OP.mult))
        op("v", lambda e: e.tensor_scalar(qg1[:, 32:32 + NT], varall[:], 1e-5, None, OP.add))
        op("v", lambda e: e.tensor_tensor(qt[:, 64:64 + NT], qnd[:, 32:32 + NT],
                                          qg1[:, 32:32 + NT], OP.mult))
        op("v", lambda e: e.tensor_scalar(qcm[:, 64:64 + NT], qt[:, 64:64 + NT],
                                          -0.5, 1.5, OP.mult, OP.add))
        op("v", lambda e: e.tensor_tensor(rstdall[:], qcm[:, 32:32 + NT],
                                          qcm[:, 64:64 + NT], OP.mult))

        sa_group = {}
        xqT_store = {}
        xq_xp_marks = {}
        hbank_mark = [None, None]
        act_marks = [None, None]
        sn_group = {}
        mm_done_mark = None
        xg_read_done = {}
        for g in range(NG):
            xg = x_grp[g % 2]
            lx = LX[g % 2]
            xl_wait = (xg_read_done[g - 2],) if g >= 2 else (wv(V),)
            for tt in range(2):
                op("y", lambda e, o=xg[:, tt * D:(tt + 1) * D],
                   i_=d_x[g * GRP + tt * 128:g * GRP + (tt + 1) * 128, :]:
                   e.dma_start(out=o, in_=i_),
                   sem=lx, inc=16, wait=xl_wait if tt == 0 else ())
            ld_x = wv(lx)
            if g > 0:
                op("v", lambda e: e.nop(), wait=((SA, sa_group[g - 1]),))
                op("s", lambda e: e.nop(), wait=((SA, sa_group[g - 1]),))

            for tt in range(2):
                idx = 2 * g + tt
                xtile = xg[:, tt * D:(tt + 1) * D]
                xnb = xnbf(tt * D, D)
                ln_wait = [ld_x]
                if g > 0 and tt == 0:
                    ln_wait.append((SN, sn_group[g - 1]))
                op("v", lambda e, i_=xtile, o=xnb, m=meanall[:, idx:idx + 1],
                   r=rstdall[:, idx:idx + 1]:
                   e.tensor_scalar(o, i_, m, r, OP.subtract, OP.mult),
                   wait=tuple(ln_wait))
                mx, mc, rec, sv = scal[:, 0:1], scal[:, 1:2], scal[:, 2:3], scal[:, 3:4]
                op("v", lambda e, i_=xnb, o=mx:
                   e.tensor_reduce(o, i_, AX.X, OP.max, apply_absolute_value=True))
                op("v", lambda e, i_=mx, o=mc: e.tensor_scalar(o, i_, 1e-5, None, OP.max))
                op("v", lambda e, i_=mc, o=rec: e.reciprocal(o, i_))
                op("v", lambda e, i_=rec, o=sv: e.tensor_scalar(o, i_, 127.0, None, OP.mult))
                op("v", lambda e, i_=mc, o=dxall[:, idx:idx + 1]:
                   e.tensor_scalar(o, i_, 1.0 / 127.0, None, OP.mult))
                for pc in range(2):
                    qw = tuple(xq_xp_marks[2 * (g - 1) + tt]) if (g > 0 and pc == 0) else ()
                    emit_round_neg(xnb[:, pc * 512:(pc + 1) * 512], sv,
                                   pqstage[:, tt * D + pc * 512:tt * D + (pc + 1) * 512],
                                   pre_wait=qw)
            vq_done = wv(V)
            for tt in range(2):
                rows = g * GRP + tt * 128
                op("y", lambda e, i_=xnbf(tt * D, D), o=scr_xn[rows:rows + 128, :]:
                   e.dma_start(out=o, in_=i_), sem=SN, inc=16, wait=(vq_done,))
            sn_done = wv(SN)
            for tt in range(2):
                ti2 = 2 * g + tt
                xstg = qtmp if ti2 % 2 == 0 else pqroll
                xw = [vq_done]
                if ti2 - 2 in xqT_store:
                    xw.append(xqT_store[ti2 - 2])
                for half in range(2):
                    emit_xpose4(
                        [pqstage[:, tt * D + half * 512 + b4 * 128:
                                 tt * D + half * 512 + (b4 + 1) * 128]
                         for b4 in range(4)],
                        xstg[:, half * 512:(half + 1) * 512],
                        src_wait=tuple(xw))
                xq_xp_marks[ti2] = tuple(last_xpose_marks(2))
                op("y", lambda e, i_=xstg, o=scr_xqT[ti2]:
                   e.dma_start(out=o, in_=i_[:]), sem=SXT[ti2 % 2], inc=16,
                   wait=xq_xp_marks[ti2])
                xqT_store[ti2] = wv(SXT[ti2 % 2])
            xn_rd_wait = [sn_done]
            if g > 0:
                xn_rd_wait.append(mm_done_mark)
            for k in range(8):
                op("y", lambda e, o=xnF(k * GRP, GRP),
                   i_=scr_xn[g * GRP:(g + 1) * GRP, k * 128:(k + 1) * 128]:
                   e.dma_start_transpose(o, i_), sem=TD[k], inc=16,
                   wait=tuple(xn_rd_wait) if k == 0 else ())
            xpose_marks_g = last_xpose_marks(8)

            xn_evac_marks = tuple((TD[k], TD[k].n) for k in range(8))
            sd_mark = None
            eps_mark = None
            scan_marks = {}

            def emit_htrans_stage(mh):
                for tt in range(2):
                    pdst2 = pb[4 + tt][:, 0:128]
                    tw = [scan_marks[mh]]
                    if hbank_mark[tt] is not None:
                        tw.append(hbank_mark[tt])
                    op("t", lambda e, i_=epi[:, 5 * GRP + tt * 128:5 * GRP + (tt + 1) * 128], o=pdst2:
                       e.transpose(o, i_, idf[:]), sem=T, inc=1, wait=tuple(tw))
                    hmark = wv(T)
                    stage = (qt, qcm, qnd, qg1)[tt * 2 + (mh // 4)]
                    scol = (mh % 4) * 128
                    op("v", lambda e, i_=pdst2, o=stage[:, scol:scol + 128],
                       x_=xg[:, tt * D + mh * 128: tt * D + mh * 128 + 128]:
                       e.tensor_tensor(o, i_, x_, OP.add), wait=(hmark,))
                    hbank_mark[tt] = wv(V)

            def emit_acc_dma(h2):
                vd = wv(V)
                for tt in range(2):
                    stage = (qt, qcm, qnd, qg1)[tt * 2 + h2]
                    rows = g * GRP + tt * 128
                    op("y", lambda e, i_=stage,
                       o=scr_acc[rows:rows + 128, h2 * 512:(h2 + 1) * 512]:
                       e.dma_start(out=o, in_=i_[:]), sem=SA, inc=16, wait=(vd,))

            for m in range(8):
                par = m % 2
                pg = pb[2 * par][:, 0:GRP]
                pv = pb[2 * par][:, GRP:2 * GRP]
                pd = pb[2 * par + 1][:, 0:GRP]
                mm_wait = list(xn_evac_marks) if m < 2 else []
                if act_marks[par] is not None:
                    mm_wait.append(act_marks[par])
                mm_wait = tuple(mm_wait)
                first = True
                for mat, pdst in (("g", pg), ("v", pv), ("d", pd)):
                    for k in range(8):
                        op("t", lambda e, l=attn_lhs(mat, k, m),
                           r=xnF(k * GRP, GRP), o=pdst, st=(k == 0), sp=(k == 7):
                           e.matmul(o, l, r, start=st, stop=sp),
                           sem=T, inc=(1 if k == 7 else 0),
                           wait=mm_wait if first else ())
                        first = False
                t_gvd = wv(T)
                if m >= 1:
                    emit_htrans_stage(m - 1)
                    if m == 4:
                        emit_acc_dma(0)
                op("s", lambda e, i_=pg, o=sgb, b=bias_sb[:, m:m + 1]:
                   e.activation(o, i_, AF.Sigmoid, bias=b, scale=1.0),
                   wait=(t_gvd,) if eps_mark is None else (t_gvd, eps_mark))
                op("s", lambda e, i_=pv, o=thb, b=bias_sb[:, 8 + m:9 + m]:
                   e.activation(o, i_, AF.Tanh, bias=b, scale=1.0))
                op("s", lambda e, i_=pd, o=sdb, b=bias_sb[:, 16 + m:17 + m]:
                   e.activation(o, i_, AF.Sigmoid, bias=b, scale=1.0))
                sd_mark = wv(S)
                act_marks[par] = sd_mark
                op("v", lambda e: e.tensor_tensor(xsb, sgb,
                                                  thb, OP.mult), wait=(sd_mark,))
                op("v", lambda e: e.tensor_scalar(ab, sdb,
                                                  0.998, 0.001, OP.mult, OP.add))
                eps_mark = wv(V)
                op("v", lambda e, c=carry[:, m:m + 1]: e.tensor_tensor_scan(
                    hb, ab, xsb, c, OP.mult, OP.add))
                op("v", lambda e, c=carry[:, m:m + 1]: e.tensor_copy(c, epi[:, 6 * GRP - 1:6 * GRP]))
                scan_marks[m] = wv(V)
            emit_htrans_stage(7)
            xg_read_done[g] = wv(V)
            emit_acc_dma(1)

            sa_group[g] = SA.n
            sn_group[g] = sn_done[1]
            mm_done_mark = wv(T)

        op("v", lambda e: e.nop(), wait=((SA, sa_group[NG - 1]),))
        op("s", lambda e: e.nop(), wait=((SA, sa_group[NG - 1]),))

        def prep_scalars(sw_col, dw_col, inv_n):
            op("t", lambda e: e.matmul(pb[5][0:1, 0:1], ones128[:], scal[:, 4:5],
                                       start=True, stop=True), sem=T, inc=1, wait=(wv(V),))
            op("v", lambda e: e.tensor_copy(tin1[0:1, 0:1], pb[5][0:1, 0:1]), wait=(wv(T),))
            op("v", lambda e: e.tensor_scalar(tin1[0:1, 1:2], tin1[0:1, 0:1], inv_n, None, OP.mult))
            op("v", lambda e: e.tensor_scalar(tin1[0:1, 2:3], tin1[0:1, 1:2], 1e-5, None, OP.max))
            op("v", lambda e: e.reciprocal(tin1[0:1, 3:4], tin1[0:1, 2:3]))
            op("v", lambda e: e.reciprocal(tin1[0:1, 4:5], tin1[0:1, 3:4]))
            for srcc, col in ((3, sw_col), (4, dw_col)):
                op("t", lambda e, i_=tin1[0:1, srcc:srcc + 1]:
                   e.matmul(pb[5][:, 1:2], onesT[:], i_, start=True, stop=True),
                   sem=T, inc=1, wait=(wv(V),))
                op("v", lambda e, o=wsc[:, col:col + 1]: e.tensor_copy(o, pb[5][:, 1:2]),
                   wait=(wv(T),))

        def prep_mean(load_list):
            npart = len(load_list) * 4
            assert npart <= 512
            vmarks = {}
            for li, src_slice in enumerate(load_list):
                buf = wchunk[li % 4]
                lp = LP[li % 4]
                lw_ = (vmarks[li - 4],) if li >= 4 else (wv(V),)
                op("y", lambda e, o=buf, i_=src_slice: e.dma_start(out=o, in_=i_),
                   sem=lp, inc=16, wait=lw_)
                ldw = wv(lp)
                for pc in range(4):
                    col = li * 4 + pc
                    op("v", lambda e, i_=buf[:, pc * 256:(pc + 1) * 256], o=qg1[:, col:col + 1]:
                       e.tensor_reduce(o, i_, AX.X, OP.add, apply_absolute_value=True),
                       wait=(ldw,) if pc == 0 else ())
                vmarks[li] = wv(V)
            op("v", lambda e, w=npart: e.tensor_reduce(scal[:, 4:5], qg1[:, 0:w], AX.X, OP.add))

        def qtmp4(cc, c0, w):
            return pqstage[:, cc * D + c0:cc * D + c0 + w]

        pq_state = {"slot_xp": {}, "xbig_st": [None], "prev_g": [None]}

        def prep_quant(load_list, sw_col, group_dst, first_guard=()):
            vmarks = {}
            prev_g = pq_state["prev_g"]
            slot_xp = pq_state["slot_xp"]
            for li, src_slice in enumerate(load_list):
                buf = wchunk[li % 4]
                lp = LP[li % 4]
                lw_ = (vmarks[li - 4],) if li >= 4 else (wv(V),)
                op("y", lambda e, o=buf, i_=src_slice: e.dma_start(out=o, in_=i_),
                   sem=lp, inc=16, wait=lw_)
                ldw = wv(lp)
                gmark = None
                for pc in range(2):
                    pw = [ldw] if pc == 0 else []
                    if pc == 0 and (li % 4) in slot_xp:
                        pw.extend(slot_xp[li % 4])
                    if pc == 0 and (li % 4) not in slot_xp:
                        pw.extend(first_guard)
                    bg = (prev_g[0],) if prev_g[0] is not None else ()
                    gmark = emit_round_neg_gw(buf[:, pc * 512:(pc + 1) * 512],
                                              wsc[:, sw_col:sw_col + 1],
                                              qtmp4(li % 4, pc * 512, 512),
                                              pre_wait=tuple(pw), b_guard=bg)
                    prev_g[0] = gmark
                vmarks[li] = wv(V)
                xw = [gmark]
                if li % 8 == 0 and pq_state["xbig_st"][0] is not None:
                    xw.extend(pq_state["xbig_st"][0])
                n = tp_state["n"]
                tp_state["n"] += 1
                sem = TD[n % 8]
                if n - 8 >= 0:
                    xw.append(tp_state["marks"][n - 8])
                cb = (li % 8) * 128
                for j in range(8):
                    op("y", lambda e, i_=qtmp4(li % 4, j * 128, 128),
                       o=xbig[:, j * 1024 + cb:j * 1024 + cb + 128]:
                       e.dma_start_transpose(o, i_), sem=sem, inc=16,
                       wait=tuple(xw) if j == 0 else ())
                tp_state["marks"].append(wv(sem))
                slot_xp[li % 4] = (tp_state["marks"][-1],)
                if li % 8 == 7:
                    gmarks = tuple(tp_state["marks"][-8:])
                    g8 = li // 8
                    for j in range(8):
                        op("y", lambda e, i_=xbig[:, j * 1024:(j + 1) * 1024],
                           o=group_dst(g8, j):
                           e.dma_start(out=o, in_=i_), sem=SWs[j % 4], inc=16,
                           wait=gmarks if j == 0 else ())
                    pq_state["xbig_st"][0] = tuple(wv(s4) for s4 in SWs)

        down_l1 = [d_wdown[c * 128:(c + 1) * 128, q * D:(q + 1) * D]
                   for c in range(8) for q in range(4)]
        prep_mean(down_l1)
        prep_scalars(4, 5, 1.0 / (D * INNER))
        down_l2 = [d_wdown[c * 128:(c + 1) * 128, q * D:(q + 1) * D]
                   for q in range(4) for c in range(8)]
        prep_quant(down_l2, 4,
                   lambda g8, j: scr_wdT[g8 * D + j * 128:g8 * D + (j + 1) * 128, :],
                   first_guard=tuple(xq_xp_marks[NT - 2]) + tuple(xq_xp_marks[NT - 1]))
        st_wd = tuple(wv(s4) for s4 in SWs)

        gate_l1 = [d_wgate[c * 128:(c + 1) * 128, :] for c in range(32)]
        prep_mean(gate_l1)
        prep_scalars(0, 1, 1.0 / (INNER * D))
        prep_quant(gate_l1, 0,
                   lambda g8, j: scr_wgT[j * 128:(j + 1) * 128,
                                         g8 * 1024:(g8 + 1) * 1024])
        st_wg = tuple(wv(s4) for s4 in SWs)

        up_l1 = [d_wup[c * 128:(c + 1) * 128, :] for c in range(32)]
        prep_mean(up_l1)
        prep_scalars(2, 3, 1.0 / (INNER * D))
        prep_quant(up_l1, 2,
                   lambda g8, j: scr_wuT[j * 128:(j + 1) * 128,
                                         g8 * 1024:(g8 + 1) * 1024])
        st_wu = tuple(wv(s4) for s4 in SWs)

        last_attn_T = wv(T)
        for k in range(8):
            op("y", lambda e, o=wqg[:, k * INNER:(k + 1) * INNER],
               i_=scr_wgT[k * 128:(k + 1) * 128, :]:
               e.dma_start(out=o, in_=i_), sem=WT, inc=16,
               wait=(st_wg + (last_attn_T,)) if k == 0 else ())
        for k in range(8):
            op("y", lambda e, o=wqu[:, k * INNER:(k + 1) * INNER],
               i_=scr_wuT[k * 128:(k + 1) * 128, :]:
               e.dma_start(out=o, in_=i_), sem=WT, inc=16,
               wait=(st_wu + (last_attn_T, (SN, sn_group[NG - 1]))) if k == 0 else ())
        wt_done = wv(WT)

        slab_ctr = {"n": 0}
        st_pq = {}
        t_after_tile = {}
        bank_marks = [None] * 4
        sg2_marks = [None, None]
        prep_w_marks = (wt_done,)
        roll = [pqroll[:, 0:512], pqroll[:, 512:1024], qtmp[:, 0:512], qtmp[:, 512:1024]]
        b1_piece_marks = {}
        roll_reader = {}
        gqs = {}
        gfin = {}

        def b1_xpose_piece(ti, pc, extra=()):
            slot = roll[pc % 4]
            emit_xpose4([slot[:, b4 * 128:(b4 + 1) * 128] for b4 in range(4)],
                        pqstage[:, pc * 512:(pc + 1) * 512],
                        src_wait=(gfin[(ti, pc)],), extra_wait=extra)
            b1_piece_marks[(ti, pc)] = tp_state["marks"][-1]
            roll_reader[(ti, pc + 4)] = tp_state["marks"][-1]
            if pc + 4 >= 8:
                roll_reader[(ti + 1, pc + 4 - 8)] = tp_state["marks"][-1]

        def b1_front(ti, pc):
            pk = (ti, pc - 2) if (ti, pc - 2) in gqs else (ti - 1, pc + 6)
            qg = (gqs[pk][2],) if pk in gqs else ()
            gqs[(ti, pc)] = gq_front(pbuf[:, pc * 512:(pc + 1) * 512],
                                     scal[:, 1:2], pc % 2, qt_guard=qg)

        def b1_tail(ti, pc):
            bg = []
            if (ti, pc) in roll_reader:
                bg.append(roll_reader[(ti, pc)])
            if ti == 0 and pc < 4:
                bg.append(xqT_store[NT - 1] if pc < 2 else xqT_store[NT - 2])
            gfin[(ti, pc)] = gq_tail(gqs[(ti, pc)], roll[pc % 4],
                                     b_guard=tuple(bg))

        st_pq_lo = {}
        st_pq_hi = {}

        def b1_store_lo(ti):
            marks = tuple(b1_piece_marks[(ti, pc)] for pc in range(4))
            op("g", lambda e, i_=pqstage[:, 0:2048], o=scr_pqF[ti][:, 0:2048]:
               e.dma_start(out=o, in_=i_), sem=SP, inc=16, wait=marks)
            st_pq_lo[ti] = SP.n

        def b1_store_hi(ti):
            marks = tuple(b1_piece_marks[(ti, pc)] for pc in range(4, 8))
            op("g", lambda e, i_=pqstage[:, 2048:4096], o=scr_pqF[ti][:, 2048:4096]:
               e.dma_start(out=o, in_=i_), sem=SP2, inc=16, wait=marks)
            st_pq_hi[ti] = SP2.n

        t_k_bykg = {}
        t_sup_mark = {}
        evac_v_marks = {}
        wd_ld = {}
        pq_ld = {}
        b2_ld_ptr = {}
        acc_stage = [qcm[:, 0:512], sg_slab[:, 0:512]]
        out_stage = [epi[:, 1024:1536], qg1[:, 0:512]]
        acc_rd_mark = [None, None]
        out_rd_mark = [None, None]
        so_done = {0: 0, 1: 0}

        def b2_load_kstep(u, k):
            kg = u * 32 + k
            lw = LW[kg % 8]
            ww = []
            if kg < 6:
                ww.extend(st_wd)
            if kg - 3 >= 0:
                ww.append((T, t_k_bykg[kg - 3]))
            op("y", lambda e, o=wdr[:, (kg % 3) * 1024:(kg % 3 + 1) * 1024],
               i_=scr_wdT[k * 128:(k + 1) * 128, :]:
               e.dma_start(out=o, in_=i_), sem=lw, inc=16, wait=tuple(ww))
            wd_ld[kg] = wv(lw)
            lp = LP[k % 4]
            pw = []
            if k == 0:
                pw.append((SP, st_pq_lo[2 * u + 1]))
            if k == 16:
                pw.append((SP2, st_pq_hi[2 * u + 1]))
            if kg - 4 >= 0:
                pw.append((T, t_k_bykg[kg - 4]))
            for j in range(2):
                op("y", lambda e,
                   o=pqs[:, (k % 4) * 256 + j * 128:(k % 4) * 256 + (j + 1) * 128],
                   i_=scr_pqF[2 * u + j][:, k * 128:(k + 1) * 128]:
                   e.dma_start(out=o, in_=i_), sem=lp, inc=16,
                   wait=tuple(pw) if j == 0 else ())
            pq_ld[kg] = wv(lp)

        def b2_mm_kstep(u, k):
            kg = u * 32 + k
            first = [wd_ld[kg], pq_ld[kg]]
            if k == 0 and (u - 1) in evac_v_marks:
                first.append((V, evac_v_marks[u - 1]))
            for j in range(2):
                for h in range(2):
                    op("t", lambda e, l=pqs[:, (k % 4) * 256 + j * 128:
                                            (k % 4) * 256 + (j + 1) * 128],
                       r=wdr[:, (kg % 3) * 1024 + h * 512:
                             (kg % 3) * 1024 + (h + 1) * 512],
                       o=pb[4 + 2 * j + h][:], st=(k == 0), sp=(k == 31):
                       e.matmul(o, l, r, start=st, stop=sp),
                       sem=T, inc=(1 if (j == 1 and h == 1) else 0),
                       wait=tuple(first) if (j == 0 and h == 0) else ())
            t_k_bykg[kg] = T.n
            if k == 31:
                t_sup_mark[u] = T.n

        def b2_step(u, sig):
            if u not in b2_ld_ptr:
                b2_ld_ptr[u] = 0
            tgt = min(2 * sig + 2, 31)
            while b2_ld_ptr[u] <= tgt:
                b2_load_kstep(u, b2_ld_ptr[u])
                b2_ld_ptr[u] += 1
            b2_mm_kstep(u, 2 * sig)
            b2_mm_kstep(u, 2 * sig + 1)

        def b2_evac(u):
            tmark = (T, t_sup_mark[u])
            for j in range(2):
                for h in range(2):
                    par = (2 * j + h) % 2
                    aw = []
                    if j == 0 and h == 0:
                        aw.append((SA, sa_group[NG - 1] if u == 0 else sa_group[u]))
                    if acc_rd_mark[par] is not None:
                        aw.append(acc_rd_mark[par])
                    op("g", lambda e, o=acc_stage[par],
                       i_=scr_acc[u * SUP + j * 128:u * SUP + (j + 1) * 128,
                                  h * 512:(h + 1) * 512]:
                       e.dma_start(out=o, in_=i_), sem=LB[par], inc=16,
                       wait=tuple(aw))
                    am = wv(LB[par])
                    sw = [tmark, am]
                    if u == 0 and par == 1 and out_rd_mark[1] is None:
                        sw.append((SA, sa_group[NG - 1]))
                    if out_rd_mark[par] is not None:
                        sw.append(out_rd_mark[par])
                    op("v", lambda e, o=out_stage[par], p_=pb[4 + 2 * j + h][:],
                       sc=dpwall[:, 2 * u + j:2 * u + j + 1], a_=acc_stage[par]:
                       e.scalar_tensor_tensor(o, p_, sc, a_, OP.mult, OP.add),
                       wait=tuple(sw))
                    acc_rd_mark[par] = wv(V)
                    rows = u * SUP + j * 128
                    op("g", lambda e, i_=out_stage[par],
                       o=d_out[rows:rows + 128, h * 512:(h + 1) * 512]:
                       e.dma_start(out=o, in_=i_[:]), sem=SO[par], inc=16,
                       wait=(acc_rd_mark[par],))
                    out_rd_mark[par] = wv(SO[par])
            evac_v_marks[u] = V.n

        ld_xq_mark = {}

        def emit_xq_load(ti):
            xq_wait = [xqT_store[ti]]
            if ti >= 2:
                xq_wait.append((T, t_after_tile[ti - 2]))
            op("y", lambda e, o=xqF_t[ti % 2][:], i_=scr_xqT[ti]:
               e.dma_start(out=o, in_=i_), sem=LQ[ti % 2], inc=16,
               wait=tuple(xq_wait))
            ld_xq_mark[ti] = wv(LQ[ti % 2])

        for i in range(NT):
            xqt = xqF_t[i % 2]
            if i == 0:
                emit_xq_load(0)
            if i + 1 < NT:
                emit_xq_load(i + 1)
            ld_xq = (ld_xq_mark[i],)
            op("v", lambda e, o=dvecs[:, 0:1], a=dxall[:, i:i + 1], b=wsc[:, 1:2]:
               e.tensor_tensor(o, a, b, OP.mult), wait=(wv(S),))
            op("v", lambda e, o=dvecs[:, 1:2], a=dxall[:, i:i + 1], b=wsc[:, 3:4]:
               e.tensor_tensor(o, a, b, OP.mult))
            dv_mark = wv(V)

            for s_ in range(8):
                bkg = (2 * slab_ctr["n"]) % 4
                bku = (2 * slab_ctr["n"] + 1) % 4
                slab_ctr["n"] += 1
                pg_, pu_ = pb[bkg][:], pb[bku][:]
                mmw = list(ld_xq) if s_ == 0 else []
                if i == 0 and s_ == 0:
                    mmw.extend(prep_w_marks)
                for bk in (bkg, bku):
                    if bank_marks[bk] is not None:
                        mmw.append(bank_marks[bk])
                for k in range(8):
                    bg0 = k * INNER + s_ * 512
                    op("t", lambda e, l=xqt[:, k * 128:(k + 1) * 128],
                       r=wqg[:, bg0:bg0 + 512], o=pg_, st=(k == 0), sp=(k == 7):
                       e.matmul(o, l, r, start=st, stop=sp),
                       sem=T, inc=(1 if k == 7 else 0), wait=tuple(mmw) if k == 0 else ())
                    op("t", lambda e, l=xqt[:, k * 128:(k + 1) * 128],
                       r=wqu[:, bg0:bg0 + 512], o=pu_, st=(k == 0), sp=(k == 7):
                       e.matmul(o, l, r, start=st, stop=sp),
                       sem=T, inc=(1 if k == 7 else 0))
                tmm = wv(T)
                spar = slab_ctr["n"] % 2
                sw = [tmm]
                if sg2_marks[spar] is not None:
                    sw.append(sg2_marks[spar])
                if s_ <= 1:
                    sw.append(dv_mark)
                op("s", lambda e, i_=pg_, o=sg2[spar], sc=dvecs[:, 0:1]:
                   e.activation(o, i_, AF.Silu, bias=0.0, scale=sc), wait=tuple(sw))
                silu_mark = wv(S)
                bank_marks[bkg] = silu_mark
                if i > 0:
                    if s_ == 0:
                        b1_front(i - 1, 0)
                    if s_ <= 6:
                        b1_front(i - 1, s_ + 1)
                op("v", lambda e, i_=pu_, o=pbuf[:, s_ * 512:(s_ + 1) * 512], sc=dvecs[:, 1:2],
                   sg=sg2[spar]:
                   e.scalar_tensor_tensor(o, i_, sc, sg, OP.mult, OP.mult),
                   wait=(silu_mark,))
                sg2_marks[spar] = wv(V)
                bank_marks[bku] = wv(V)
                if i > 0:
                    b1_tail(i - 1, s_)
                    if s_ >= 1:
                        if s_ == 1 and i >= 2:
                            ex = ((SP, st_pq_lo[i - 2]),)
                        elif s_ == 1 and i == 1:
                            ex = st_wu
                        elif s_ == 5 and i >= 2:
                            ex = ((SP2, st_pq_hi[i - 2]),)
                        else:
                            ex = ()
                        b1_xpose_piece(i - 1, s_ - 1, extra=ex)
                    if s_ == 6:
                        b1_store_lo(i - 1)
                if i >= 3:
                    b2_step((i - 3) // 2, ((i - 3) % 2) * 8 + s_)
            t_after_tile[i] = T.n
            if i > 0:
                b1_xpose_piece(i - 1, 7)
                b1_store_hi(i - 1)
            mxp, mcp, recp, svp = scal[:, 6:7], scal[:, 7:8], scal[:, 0:1], scal[:, 1:2]
            op("v", lambda e, o=mxp: e.tensor_reduce(o, pbuf[:], AX.X, OP.max,
                                                     apply_absolute_value=True))
            op("v", lambda e, i_=mxp, o=mcp: e.tensor_scalar(o, i_, 1e-5, None, OP.max))
            op("v", lambda e, i_=mcp, o=recp: e.reciprocal(o, i_))
            op("v", lambda e, i_=recp, o=svp: e.tensor_scalar(o, i_, 127.0, None, OP.mult))
            op("v", lambda e, i_=mcp, o=scal[:, 2:3]:
               e.tensor_scalar(o, i_, 1.0 / 127.0, None, OP.mult))
            op("v", lambda e, o=dpwall[:, i:i + 1], a=scal[:, 2:3], b=wsc[:, 5:6]:
               e.tensor_tensor(o, a, b, OP.mult))
            if i >= 3 and (i - 3) % 2 == 1:
                b2_evac((i - 3) // 2)
        i = NT - 1
        for pc in range(8):
            b1_front(i, pc)
            b1_tail(i, pc)
            if pc >= 1:
                if pc == 1:
                    ex = ((SP, st_pq_lo[NT - 2]),)
                elif pc == 5:
                    ex = ((SP2, st_pq_hi[NT - 2]),)
                else:
                    ex = ()
                b1_xpose_piece(i, pc - 1, extra=ex)
            if pc == 4:
                b1_store_lo(i)
        b1_xpose_piece(i, 7)
        b1_store_hi(i)

        for sig in range(8, 16):
            b2_step(14, sig)
        b2_evac(14)
        for sig in range(16):
            b2_step(15, sig)
        b2_evac(15)

        fin0, fin1 = SO[0].n, SO[1].n
        op("y", lambda e, a=fin0, b=fin1: e.wait_ge(SO[0].h, a))
        op("y", lambda e, a=fin0, b=fin1: e.wait_ge(SO[1].h, b))

        @block.sync
        def _(e):
            for f in plan["y"]:
                f(e)

        @block.gpsimd
        def _(e):
            for f in plan["g"]:
                f(e)

        @block.vector
        def _(e):
            e.wait_ge(G.h, g_after_consts)
            for f in plan["v"]:
                f(e)

        @block.scalar
        def _(e):
            e.wait_ge(G.h, g_after_consts)
            for f in plan["s"]:
                f(e)

        @block.tensor
        def _(e):
            e.wait_ge(G.h, g_after_consts)
            for f in plan["t"]:
                f(e)

    return nc


_NC_CACHE = {}


def kernel(L0_attn_x, L0_attn_Wg, L0_attn_bg, L0_attn_Wv, L0_attn_bv,
           L0_attn_Wd, L0_attn_bd, L0_ffn_w_gate, L0_ffn_w_up, L0_ffn_w_down):
    x = np.ascontiguousarray(np.asarray(L0_attn_x, dtype=np.float32))
    B = x.shape[0]
    Wg = np.ascontiguousarray(np.asarray(L0_attn_Wg, np.float32))
    Wv = np.ascontiguousarray(np.asarray(L0_attn_Wv, np.float32))
    Wd = np.ascontiguousarray(np.asarray(L0_attn_Wd, np.float32))
    bg = np.asarray(L0_attn_bg, np.float32).reshape(8, 128).T
    bv = np.asarray(L0_attn_bv, np.float32).reshape(8, 128).T
    bd = np.asarray(L0_attn_bd, np.float32).reshape(8, 128).T
    biasT = np.ascontiguousarray(np.concatenate([bg, bv, bd], axis=1))
    wgate = np.ascontiguousarray(np.asarray(L0_ffn_w_gate, np.float32))
    wup = np.ascontiguousarray(np.asarray(L0_ffn_w_up, np.float32))
    wdown = np.ascontiguousarray(np.asarray(L0_ffn_w_down, np.float32))

    consts = np.zeros((128, 16), np.float32)
    consts[:, 0] = -0.5
    consts[:, 1] = 0.5
    consts[:, 2] = MAGIC
    consts[:, 3] = -MAGIC
    consts[:, 4] = 1e-5
    identf = np.eye(128, dtype=np.float32)
    identb = np.asarray(np.eye(128, dtype=np.float32).astype(ml_dtypes.bfloat16))
    ones128 = np.ones((128, 1), np.float32)
    onesT = np.ones((1, 128), np.float32)

    if "nc" not in _NC_CACHE:
        _NC_CACHE["nc"] = build_program()
    nc = _NC_CACHE["nc"]

    in_maps = []
    for b in range(B):
        in_maps.append(dict(
            x=np.ascontiguousarray(x[b]), Wg=Wg, Wv=Wv, Wd=Wd, biasT=biasT,
            wgate=wgate, wup=wup, wdown=wdown, consts=consts,
            identf=identf, identb=identb, ones128=ones128, onesT=onesT,
        ))
    res = run_bass_kernel_spmd(nc, in_maps, core_ids=list(range(B)))
    out = np.stack([np.asarray(r["out"], np.float32) for r in res.results], axis=0)
    return out


# version 17
# speedup vs baseline: 1.0800x; 1.0527x over previous
import numpy as np
import ml_dtypes
import concourse.bass as bass
import concourse.mybir as mybir
from concourse.bass_utils import run_bass_kernel_spmd
from contextlib import ExitStack

F32 = mybir.dt.float32
BF16 = mybir.dt.bfloat16
I32 = mybir.dt.int32
AF = mybir.ActivationFunctionType
OP = mybir.AluOpType
AX = mybir.AxisListType

MAGIC = 12582912.0
T_TOK = 4096
D = 1024
INNER = 4096
NT = T_TOK // 128
GRP = 256
NG = T_TOK // GRP
SUP = 256
NSUP = T_TOK // SUP


class SemC:
    def __init__(self, name):
        self.name = name
        self.h = None
        self.n = 0


def build_program():
    nc = bass.Bass("TRN2", target_bir_lowering=False)

    d_x = nc.dram_tensor("x", [T_TOK, D], F32, kind="ExternalInput").ap()
    d_Wg = nc.dram_tensor("Wg", [D, D], F32, kind="ExternalInput").ap()
    d_Wv = nc.dram_tensor("Wv", [D, D], F32, kind="ExternalInput").ap()
    d_Wd = nc.dram_tensor("Wd", [D, D], F32, kind="ExternalInput").ap()
    d_bias = nc.dram_tensor("biasT", [128, 24], F32, kind="ExternalInput").ap()
    d_wgate = nc.dram_tensor("wgate", [INNER, D], F32, kind="ExternalInput").ap()
    d_wup = nc.dram_tensor("wup", [INNER, D], F32, kind="ExternalInput").ap()
    d_wdown = nc.dram_tensor("wdown", [D, INNER], F32, kind="ExternalInput").ap()
    d_consts = nc.dram_tensor("consts", [128, 16], F32, kind="ExternalInput").ap()
    d_idf = nc.dram_tensor("identf", [128, 128], F32, kind="ExternalInput").ap()
    d_idb = nc.dram_tensor("identb", [128, 128], BF16, kind="ExternalInput").ap()
    d_ones = nc.dram_tensor("ones128", [128, 1], F32, kind="ExternalInput").ap()
    d_onesT = nc.dram_tensor("onesT", [1, 128], F32, kind="ExternalInput").ap()
    d_out = nc.dram_tensor("out", [T_TOK, D], F32, kind="ExternalOutput").ap()

    scr_acc = nc.dram_tensor("scr_acc", [T_TOK, D], F32, kind="Internal").ap()
    scr_xqT = nc.dram_tensor("scr_xqT", [NT, 128, D], BF16, kind="Internal").ap()
    scr_xn = nc.dram_tensor("scr_xn", [T_TOK, D], BF16, kind="Internal").ap()
    scr_pqF = nc.dram_tensor("scr_pqF", [NT, 128, INNER], BF16, kind="Internal").ap()
    scr_wgT = nc.dram_tensor("scr_wgT", [D, INNER], BF16, kind="Internal").ap()
    scr_wuT = nc.dram_tensor("scr_wuT", [D, INNER], BF16, kind="Internal").ap()
    scr_wdT = nc.dram_tensor("scr_wdT", [INNER, D], BF16, kind="Internal").ap()

    es = ExitStack()

    def sb(name, shape, dt):
        return es.enter_context(nc.sbuf_tensor(name, shape, dt))

    def ps(name, shape, dt):
        return es.enter_context(nc.psum_tensor(name, shape, dt))

    with es:
        wqg = sb("wqg", [128, 8 * INNER], BF16)
        wqu = sb("wqu", [128, 8 * INNER], BF16)
        x_grp = [sb("x_grp0", [128, 2 * D], F32), sb("x_grp1", [128, 2 * D], F32)]
        epi = sb("epi", [128, 6 * GRP], F32)
        sgb = epi[:, 0 * GRP:1 * GRP]
        thb = epi[:, 1 * GRP:2 * GRP]
        sdb = epi[:, 2 * GRP:3 * GRP]
        xsb = epi[:, 3 * GRP:4 * GRP]
        ab = epi[:, 4 * GRP:5 * GRP]
        hb = epi[:, 5 * GRP:6 * GRP]
        carry = sb("carry", [128, 8], F32)
        qt = sb("qt", [128, 512], F32)
        qcm = sb("qcm", [128, 512], F32)
        qnd = sb("qnd", [128, 512], F32)
        qg1 = sb("qg1", [128, 512], F32)
        nhalf = sb("nhalf", [128, 512], F32)
        phalf = sb("phalf", [128, 512], F32)
        qci = sb("qci", [128, 512], I32)
        qtmp = sb("qtmp", [128, D], BF16)
        meanall = sb("meanall", [128, NT], F32)
        rstdall = sb("rstdall", [128, NT], F32)
        dxall = sb("dxall", [128, NT], F32)
        varall = dxall
        dpwall = sb("dpwall", [128, NT], F32)
        wsc = sb("wsc", [128, 8], F32)
        consts_sb = sb("consts_sb", [128, 16], F32)
        bias_sb = sb("bias_sb", [128, 24], F32)
        idf = sb("idf_sb", [128, 128], F32)
        ones128 = sb("ones128_sb", [128, 1], F32)
        onesT = sb("onesT_sb", [1, 128], F32)
        scal = sb("scal", [128, 8], F32)
        tin1 = sb("tin1", [1, 8], F32)
        pbuf = sb("pbuf", [128, INNER], F32)
        sg_slab = sb("sg_slab", [128, 512], F32)
        sg2 = [epi[:, 512:1024], epi[:, 0:512]]
        wdr = sb("wdr", [128, 3 * 1024], BF16)
        pqs = sb("pqs", [128, 4 * 256], BF16)
        pqroll = sb("pqroll", [128, 1024], BF16)
        pqstage = sb("pqstage", [128, INNER], BF16)
        xqF_t = [sb("xqF_t0", [128, 8 * 128], BF16), sb("xqF_t1", [128, 8 * 128], BF16)]
        xbig = wqu[:, 8192:16384]
        dvecs = sb("dvecs", [128, 4], F32)
        wchunk = [pbuf[:, j * D:(j + 1) * D] for j in range(4)]

        def accsup(c0, w):
            return pbuf[:, c0:c0 + w]

        gnd2 = x_grp[0][:, 0:512]
        gb1 = x_grp[0][:, 512:1024]
        gb2 = x_grp[0][:, 1024:1536]
        gc1 = x_grp[0][:, 1536:2048]
        gcm = [x_grp[1][:, 0:512], x_grp[1][:, 512:1024]]

        ATTN_BASE = {"g": 0, "v": 8192, "d": 16384}

        def attn_lhs(mat, k, m):
            b = ATTN_BASE[mat] + k * D + m * 128
            return wqg[:, b:b + 128]

        def xnF(c0, w):
            return wqu[:, c0:c0 + w]

        def xqF(c0, w):
            return wqu[:, 2048 + c0:2048 + c0 + w]

        def xnbf(c0, w):
            return wqu[:, 4096 + c0:4096 + c0 + w]

        def wdT_build(c0, w):
            return wqu[:, c0:c0 + w]

        def pqF_sup(c0, w):
            return wqg[:, c0:c0 + w]

        def wdT_str(j):
            return wqg[:, 8192 + j * D:8192 + (j + 1) * D]

        pb = [ps(f"pb{i}", [128, 512], F32) for i in range(8)]

        names = ["G", "T", "V", "S", "LA0", "LX0", "LX1", "LP0", "LP1",
                 "LQ0", "LQ1", "LB0", "LB1", "LW0", "LW1", "LW2", "LW3",
                 "LP2", "LP3", "LW4", "LW5", "LW6", "LW7",
                 "TD0", "TD1", "TD2", "TD3", "TD4", "TD5", "TD6", "TD7", "SN", "WT",
                 "SW1", "SW2", "SW3", "GC",
                 "SA", "SX", "SW", "SP", "SP2", "SO0", "SO1", "SXT0", "SXT1",
                 "PD0", "PD1", "PD2", "PD3"]
        sems = {n: SemC(n) for n in names}
        for smc in sems.values():
            smc.h = es.enter_context(nc.semaphore(smc.name))
        G, T, V, S = sems["G"], sems["T"], sems["V"], sems["S"]
        LA0 = sems["LA0"]
        LX = [sems["LX0"], sems["LX1"]]
        LP = [sems["LP0"], sems["LP1"], sems["LP2"], sems["LP3"]]
        LQ = [sems["LQ0"], sems["LQ1"]]
        LB = [sems["LB0"], sems["LB1"]]
        LW = [sems[f"LW{j}"] for j in range(8)]
        SA, SX, SW, SP = sems["SA"], sems["SX"], sems["SW"], sems["SP"]
        SP2 = sems["SP2"]
        SXT = [sems["SXT0"], sems["SXT1"]]
        PD = [sems[f"PD{j}"] for j in range(4)]
        SO = [sems["SO0"], sems["SO1"]]
        TD = [sems[f"TD{j}"] for j in range(8)]
        SN, WT = sems["SN"], sems["WT"]
        SWs = [sems["SW"], sems["SW1"], sems["SW2"], sems["SW3"]]
        GC = sems["GC"]

        block = es.enter_context(nc.Block())
        plan = {"t": [], "v": [], "s": [], "g": [], "y": []}
        ENG_SEM = {"v": V, "s": S, "gc": GC}

        def op(eng, build, sem=None, inc=0, wait=()):
            waits = list(wait)
            if eng == "gc":
                eng = "g"
                esem = ENG_SEM["gc"]
                if esem.n > 0:
                    waits.append((esem, esem.n))
                sem, inc = esem, 1
            elif eng in ("v", "s"):
                esem = ENG_SEM[eng]
                if esem.n > 0:
                    waits.append((esem, esem.n))
                sem, inc = esem, 1
            for (smc, val) in waits:
                if val > 0:
                    plan[eng].append(lambda e, _s=smc, _v=val: e.wait_ge(_s.h, _v))
            if sem is not None and inc:
                sem.n += inc

            def f(e, _b=build, _s=sem, _i=inc):
                ins = _b(e)
                if _s is not None and _i:
                    ins.then_inc(_s.h, _i)
            plan[eng].append(f)

        def wv(smc):
            return (smc, smc.n)

        C_NH = consts_sb[:, 0:1]
        C_PH = consts_sb[:, 1:2]
        C_MG = consts_sb[:, 2:3]
        C_NMG = consts_sb[:, 3:4]
        C_EPS = consts_sb[:, 4:5]

        def emit_round_neg(src_ap, s_vec_ap, dst_ap, clip1=False, pre_wait=()):
            fd = src_ap.shape[-1]
            tq, ci, cm, nd, b1 = (qt[:, 0:fd], qci[:, 0:fd], qcm[:, 0:fd],
                                  qnd[:, 0:fd], qg1[:, 0:fd])
            if s_vec_ap is None:
                op("v", lambda e, a=src_ap, o=tq: e.tensor_copy(o, a), wait=pre_wait)
            else:
                op("v", lambda e, a=src_ap, sv=s_vec_ap, o=tq:
                   e.tensor_scalar(o, a, sv, None, OP.mult), wait=pre_wait)
            if clip1:
                op("v", lambda e, a=nhalf[:, 0:fd], b=tq, o=b1:
                   e.tensor_tensor(o, a, b, OP.is_gt))
                op("v", lambda e, a=tq, b=phalf[:, 0:fd], o=cm:
                   e.tensor_tensor(o, a, b, OP.is_gt))
                op("v", lambda e, a=b1, b=cm, o=dst_ap: e.tensor_tensor(o, a, b, OP.subtract))
                return
            op("v", lambda e, a=tq, o=ci: e.tensor_scalar(o, a, MAGIC, None, OP.add))
            op("v", lambda e, a=ci, o=cm: e.tensor_scalar(o, a, -MAGIC, None, OP.add))
            op("v", lambda e, a=cm, b=tq, o=nd: e.tensor_tensor(o, a, b, OP.subtract))
            op("v", lambda e, a=nhalf[:, 0:fd], b=nd, o=b1:
               e.tensor_tensor(o, a, b, OP.is_gt))
            op("v", lambda e, a=nd, b=phalf[:, 0:fd], o=tq:
               e.tensor_tensor(o, a, b, OP.is_gt))
            op("v", lambda e, a=tq, b=b1, o=nd: e.tensor_tensor(o, a, b, OP.subtract))
            op("v", lambda e, a=nd, b=cm, o=dst_ap: e.tensor_tensor(o, a, b, OP.subtract))

        qtP = [qt[:, 0:512], x_grp[1][:, 1024:1536]]

        def gq_front(src_ap, s_vec_ap, par, qt_guard=(), pre_wait=()):
            cmP = gcm[par]
            tP = qtP[par]
            ndX = qnd[:, 0:512] if par == 0 else gnd2
            op("v", lambda e, a=src_ap, sv=s_vec_ap, o=tP:
               e.tensor_scalar(o, a, sv, None, OP.mult),
               wait=tuple(pre_wait) + tuple(qt_guard))
            op("v", lambda e, a=tP, o=qci[:, 0:512]:
               e.tensor_scalar(o, a, MAGIC, None, OP.add))
            op("v", lambda e, a=qci[:, 0:512], o=cmP:
               e.tensor_scalar(o, a, -MAGIC, None, OP.add))
            vf = wv(V)
            op("gc", lambda e, a=cmP, b=tP, o=ndX:
               e.tensor_tensor(o, a, b, OP.subtract), wait=(vf,))
            return (ndX, cmP, wv(GC))

        def gq_tail(state, dst_ap, b_guard=()):
            ndX, cmP, gmark = state
            op("v", lambda e, a=nhalf[:, 0:512], b=ndX, o=gb1:
               e.tensor_tensor(o, a, b, OP.is_gt), wait=(gmark,) + tuple(b_guard))
            op("v", lambda e, a=ndX, b=phalf[:, 0:512], o=gb2:
               e.tensor_tensor(o, a, b, OP.is_gt))
            op("v", lambda e, a=gb2, b=gb1, o=gc1:
               e.tensor_tensor(o, a, b, OP.subtract))
            op("v", lambda e, a=gc1, b=cmP, o=dst_ap:
               e.tensor_tensor(o, a, b, OP.subtract))
            return wv(V)

        def emit_round_neg_gw(src_ap, s_vec_ap, dst_ap, pre_wait=(), b_guard=()):
            op("v", lambda e, a=src_ap, sv=s_vec_ap, o=qt[:, 0:512]:
               e.tensor_scalar(o, a, sv, None, OP.mult), wait=pre_wait)
            op("v", lambda e, a=nhalf[:, 0:512], b=qt[:, 0:512], o=gb1:
               e.tensor_tensor(o, a, b, OP.is_gt), wait=tuple(b_guard))
            op("v", lambda e, a=qt[:, 0:512], b=phalf[:, 0:512], o=gb2:
               e.tensor_tensor(o, a, b, OP.is_gt))
            vb = wv(V)
            op("gc", lambda e, a=gb1, b=gb2, o=dst_ap:
               e.tensor_tensor(o, a, b, OP.subtract), wait=(vb,))
            return wv(GC)

        tp_state = {"n": 0, "marks": []}

        def emit_xpose4(srcs, dst_ap, src_wait=(), extra_wait=()):
            n = tp_state["n"]
            tp_state["n"] += 1
            sem = TD[n % 8]
            twait = list(src_wait) + list(extra_wait)
            if n - 8 >= 0:
                twait.append(tp_state["marks"][n - 8])
            for i, s4 in enumerate(srcs):
                op("y", lambda e, i_=s4, o=dst_ap[:, i * 128:(i + 1) * 128]:
                   e.dma_start_transpose(o, i_),
                   sem=sem, inc=16, wait=tuple(twait) if i == 0 else ())
            tp_state["marks"].append(wv(sem))

        def last_xpose_marks(k=8):
            return tuple(tp_state["marks"][-k:]) if tp_state["marks"] else ()

        for dst, src in ((consts_sb, d_consts), (idf, d_idf),
                         (ones128, d_ones), (onesT, d_onesT), (bias_sb, d_bias)):
            op("g", lambda e, o=dst, i=src: e.dma_start(out=o[:], in_=i), sem=G, inc=16)
        for nm, src in (("g", d_Wg), ("v", d_Wv), ("d", d_Wd)):
            for k in range(8):
                b = ATTN_BASE[nm] + k * D
                op("g", lambda e, o=wqg[:, b:b + D], i=src[k * 128:(k + 1) * 128, :]:
                   e.dma_start(out=o, in_=i), sem=G, inc=16)
        g_after_consts = G.n

        op("v", lambda e: e.memset(carry[:], 0.0))
        op("v", lambda e: e.memset(nhalf[:], -0.5))
        op("v", lambda e: e.memset(phalf[:], 0.5))

        for i in range(NT):
            buf, sq = wchunk[0], wchunk[1]
            op("y", lambda e, o=buf, i_=d_x[i * 128:(i + 1) * 128, :]:
               e.dma_start(out=o[:], in_=i_), sem=LA0, inc=16, wait=(wv(V), wv(S)))
            ldw = wv(LA0)
            op("v", lambda e, i_=buf, o=qt[:, i:i + 1]:
               e.tensor_reduce(o, i_[:], AX.X, OP.add), wait=(ldw,))
            op("s", lambda e, i_=buf, o=sq: e.activation(o[:], i_[:], AF.Square, bias=0.0, scale=1.0),
               wait=(ldw, wv(V)))
            op("v", lambda e, i_=sq, o=qcm[:, i:i + 1]:
               e.tensor_reduce(o, i_[:], AX.X, OP.add), wait=(wv(S),))
        op("v", lambda e: e.tensor_scalar(meanall[:], qt[:, 0:NT], 1.0 / D, None, OP.mult))
        op("v", lambda e: e.tensor_scalar(qnd[:, 0:NT], qcm[:, 0:NT], 1.0 / D, None, OP.mult))
        op("v", lambda e: e.tensor_tensor(qg1[:, 0:NT], meanall[:], meanall[:], OP.mult))
        op("v", lambda e: e.tensor_tensor(varall[:], qnd[:, 0:NT], qg1[:, 0:NT], OP.subtract))
        op("s", lambda e: e.activation(qt[:, 32:32 + NT], varall[:], AF.Sqrt, bias=C_EPS, scale=1.0),
           wait=(wv(V),))
        op("v", lambda e: e.reciprocal(qcm[:, 32:32 + NT], qt[:, 32:32 + NT]), wait=(wv(S),))
        op("v", lambda e: e.tensor_tensor(qnd[:, 32:32 + NT], qcm[:, 32:32 + NT],
                                          qcm[:, 32:32 + NT], OP.mult))
        op("v", lambda e: e.tensor_scalar(qg1[:, 32:32 + NT], varall[:], 1e-5, None, OP.add))
        op("v", lambda e: e.tensor_tensor(qt[:, 64:64 + NT], qnd[:, 32:32 + NT],
                                          qg1[:, 32:32 + NT], OP.mult))
        op("v", lambda e: e.tensor_scalar(qcm[:, 64:64 + NT], qt[:, 64:64 + NT],
                                          -0.5, 1.5, OP.mult, OP.add))
        op("v", lambda e: e.tensor_tensor(rstdall[:], qcm[:, 32:32 + NT],
                                          qcm[:, 64:64 + NT], OP.mult))

        def qtmp4(cc, c0, w):
            return pqstage[:, cc * D + c0:cc * D + c0 + w]

        Pst = {"slotmark": {}, "slot_xp": {}, "xbig_st": None, "pend": [],
               "ld": 0, "st": {}, "chunk_slot": {}, "chunk_ld": {}}

        def p_flush():
            pend, Pst["pend"] = Pst["pend"], []
            for f in pend:
                f()

        def p_load(src_ap):
            slot = Pst["ld"] % 4
            Pst["ld"] += 1
            g = Pst["slotmark"].get(slot)
            if g is None:
                g = (wv(V), wv(S))
            else:
                g = (g,)
            op("y", lambda e, o=wchunk[slot], i_=src_ap: e.dma_start(out=o, in_=i_),
               sem=LP[slot], inc=16, wait=g)
            return slot, wv(LP[slot])

        runpair = {"d": (scal[:, 4:5], scal[:, 5:6]),
                   "g": (scal[:, 6:7], scal[:, 7:8]),
                   "u": (dvecs[:, 0:1], dvecs[:, 1:2])}
        rtmp = dvecs[:, 2:3]
        swcol = {"d": 4, "g": 0, "u": 2}

        def mk_mean(mat, src_ap, mi):
            def emit(profile):
                p_flush()
                slot, ldw = p_load(src_ap)
                ra, rb = runpair[mat]
                src_run, dst_run = (ra, rb) if mi % 2 == 0 else (rb, ra)
                op("v", lambda e, i_=wchunk[slot], o=rtmp:
                   e.tensor_reduce(o, i_[:], AX.X, OP.add,
                                   apply_absolute_value=True), wait=(ldw,))
                if mi == 0:
                    op("v", lambda e, o=dst_run: e.tensor_copy(o, rtmp))
                else:
                    op("v", lambda e, a=src_run, o=dst_run:
                       e.tensor_tensor(o, a, rtmp, OP.add))
                Pst["slotmark"][slot] = wv(V)
            return emit

        def mk_scal(mat, sw_col, dw_col, inv_n):
            incol = runpair[mat][0]
            def emit(profile):
                p_flush()
                op("t", lambda e, i_=incol: e.matmul(pb[6][0:1, 0:1], ones128[:], i_,
                   start=True, stop=True), sem=T, inc=1, wait=(wv(V),))
                op("v", lambda e: e.tensor_copy(tin1[0:1, 0:1], pb[6][0:1, 0:1]),
                   wait=(wv(T),))
                op("v", lambda e: e.tensor_scalar(tin1[0:1, 1:2], tin1[0:1, 0:1],
                                                  inv_n, None, OP.mult))
                op("v", lambda e: e.tensor_scalar(tin1[0:1, 2:3], tin1[0:1, 1:2],
                                                  1e-5, None, OP.max))
                op("v", lambda e: e.reciprocal(tin1[0:1, 3:4], tin1[0:1, 2:3]))
                op("v", lambda e: e.reciprocal(tin1[0:1, 4:5], tin1[0:1, 3:4]))
                for srcc, col in ((3, sw_col), (4, dw_col)):
                    op("t", lambda e, i_=tin1[0:1, srcc:srcc + 1]:
                       e.matmul(pb[6][:, 1:2], onesT[:], i_, start=True, stop=True),
                       sem=T, inc=1, wait=(wv(V),))
                    op("v", lambda e, o=wsc[:, col:col + 1]:
                       e.tensor_copy(o, pb[6][:, 1:2]), wait=(wv(T),))
            return emit

        pd_state = {"n": 0, "marks": []}

        def p_xpose(li, vmark, gdst, mat):
            xw = [vmark]
            if li % 8 == 0 and Pst["xbig_st"] is not None:
                xw.extend(Pst["xbig_st"])
            n = pd_state["n"]
            pd_state["n"] += 1
            sem = PD[n % 4]
            if n - 4 >= 0:
                xw.append(pd_state["marks"][n - 4])
            cb = (li % 8) * 128
            for j in range(8):
                op("y", lambda e, i_=qtmp4(2 + li % 2, j * 128, 128),
                   o=xbig[:, j * 1024 + cb:j * 1024 + cb + 128]:
                   e.dma_start_transpose(o, i_), sem=sem, inc=16,
                   wait=tuple(xw) if j == 0 else ())
            pd_state["marks"].append(wv(sem))
            Pst["slot_xp"][li % 2] = pd_state["marks"][-1]
            if li % 8 == 7:
                gm = tuple(pd_state["marks"][-8:])
                Pst["pend"].append(
                    lambda gmm=gm, l2=li: p_store_cb(l2, gmm, gdst, mat))

        def p_store_cb(li, gmarks, gdst, mat):
            g8 = li // 8
            for j in range(8):
                op("y", lambda e, i_=xbig[:, j * 1024:(j + 1) * 1024],
                   o=gdst(g8, j):
                   e.dma_start(out=o, in_=i_), sem=SWs[j % 4], inc=16,
                   wait=gmarks if j == 0 else ())
            Pst["xbig_st"] = tuple(wv(s4) for s4 in SWs)
            if g8 == 3:
                Pst["st"][mat] = Pst["xbig_st"]

        def mk_quant(mat, src_ap, li, pc, gdst):
            def emit(profile):
                p_flush()
                if pc == 0:
                    slot, ldw = p_load(src_ap)
                    Pst["chunk_slot"][(mat, li)] = slot
                    Pst["chunk_ld"][(mat, li)] = ldw
                slot = Pst["chunk_slot"][(mat, li)]
                tq, b1b, b2b = ((qcm, qg1, sg_slab) if profile == "early"
                                else (qt, qnd, sg_slab))
                w0 = [(SA, SA.n)]
                if pc == 0:
                    w0.append(Pst["chunk_ld"][(mat, li)])
                    xg = Pst["slot_xp"].get(li % 2)
                    if xg is not None:
                        w0.append(xg)
                op("v", lambda e, a=wchunk[slot][:, pc * 512:(pc + 1) * 512],
                   sv=wsc[:, swcol[mat]:swcol[mat] + 1], o=tq[:, 0:512]:
                   e.tensor_scalar(o, a, sv, None, OP.mult), wait=tuple(w0))
                op("v", lambda e, a=nhalf[:, 0:512], b=tq[:, 0:512], o=b1b[:, 0:512]:
                   e.tensor_tensor(o, a, b, OP.is_gt))
                op("v", lambda e, a=tq[:, 0:512], b=phalf[:, 0:512], o=b2b[:, 0:512]:
                   e.tensor_tensor(o, a, b, OP.is_gt))
                op("v", lambda e, a=b1b[:, 0:512], b=b2b[:, 0:512],
                   o=qtmp4(2 + li % 2, pc * 512, 512):
                   e.tensor_tensor(o, a, b, OP.subtract))
                if pc == 1:
                    vd = wv(V)
                    Pst["slotmark"][slot] = vd
                    Pst["pend"].append(
                        lambda l2=li, v2=vd: p_xpose(l2, v2, gdst, mat))
            return emit

        gd_d = lambda g8, j: scr_wdT[g8 * D + j * 128:g8 * D + (j + 1) * 128, :]
        gd_g = lambda g8, j: scr_wgT[j * 128:(j + 1) * 128, g8 * 1024:(g8 + 1) * 1024]
        gd_u = lambda g8, j: scr_wuT[j * 128:(j + 1) * 128, g8 * 1024:(g8 + 1) * 1024]
        down_m = [d_wdown[c * 128:(c + 1) * 128, q * D:(q + 1) * D]
                  for c in range(8) for q in range(4)]
        down_q = [d_wdown[c * 128:(c + 1) * 128, q * D:(q + 1) * D]
                  for q in range(4) for c in range(8)]
        gate_c = [d_wgate[c * 128:(c + 1) * 128, :] for c in range(32)]
        up_c = [d_wup[c * 128:(c + 1) * 128, :] for c in range(32)]
        prep_items = []
        for mat, mean_l, quant_l, gd, swc, dwc, inv in (
                ("d", down_m, down_q, gd_d, 4, 5, 1.0 / (D * INNER)),
                ("g", gate_c, gate_c, gd_g, 0, 1, 1.0 / (INNER * D)),
                ("u", up_c, up_c, gd_u, 2, 3, 1.0 / (INNER * D))):
            prep_items += [mk_mean(mat, s, mi) for mi, s in enumerate(mean_l)]
            prep_items += [mk_scal(mat, swc, dwc, inv)]
            prep_items += [mk_quant(mat, s, li, pc, gd)
                           for li, s in enumerate(quant_l) for pc in range(2)]
        prep_cur = {"i": 0}

        def prep_pump(nmax, profile):
            k = 0
            while k < nmax and prep_cur["i"] < len(prep_items):
                prep_items[prep_cur["i"]](profile)
                prep_cur["i"] += 1
                k += 1

        sa_group = {}
        xqT_store = {}
        xq_xp_marks = {}
        hbank_mark = [None, None]
        act_marks = [None, None]
        sn_group = {}
        mm_done_mark = None
        xg_read_done = {}
        for g in range(NG):
            xg = x_grp[g % 2]
            lx = LX[g % 2]
            xl_wait = (xg_read_done[g - 2],) if g >= 2 else (wv(V),)
            for tt in range(2):
                op("y", lambda e, o=xg[:, tt * D:(tt + 1) * D],
                   i_=d_x[g * GRP + tt * 128:g * GRP + (tt + 1) * 128, :]:
                   e.dma_start(out=o, in_=i_),
                   sem=lx, inc=16, wait=xl_wait if tt == 0 else ())
            ld_x = wv(lx)
            if g > 0:
                op("v", lambda e: e.nop(), wait=((SA, sa_group[g - 1]),))
                op("s", lambda e: e.nop(), wait=((SA, sa_group[g - 1]),))

            for tt in range(2):
                idx = 2 * g + tt
                xtile = xg[:, tt * D:(tt + 1) * D]
                xnb = xnbf(tt * D, D)
                ln_wait = [ld_x]
                if g > 0 and tt == 0:
                    ln_wait.append((SN, sn_group[g - 1]))
                op("v", lambda e, i_=xtile, o=xnb, m=meanall[:, idx:idx + 1],
                   r=rstdall[:, idx:idx + 1]:
                   e.tensor_scalar(o, i_, m, r, OP.subtract, OP.mult),
                   wait=tuple(ln_wait))
                mx, mc, rec, sv = scal[:, 0:1], scal[:, 1:2], scal[:, 2:3], scal[:, 3:4]
                op("v", lambda e, i_=xnb, o=mx:
                   e.tensor_reduce(o, i_, AX.X, OP.max, apply_absolute_value=True))
                op("v", lambda e, i_=mx, o=mc: e.tensor_scalar(o, i_, 1e-5, None, OP.max))
                op("v", lambda e, i_=mc, o=rec: e.reciprocal(o, i_))
                op("v", lambda e, i_=rec, o=sv: e.tensor_scalar(o, i_, 127.0, None, OP.mult))
                op("v", lambda e, i_=mc, o=dxall[:, idx:idx + 1]:
                   e.tensor_scalar(o, i_, 1.0 / 127.0, None, OP.mult))
                for pc in range(2):
                    qw = tuple(xq_xp_marks[2 * (g - 1) + tt]) if (g > 0 and pc == 0) else ()
                    emit_round_neg(xnb[:, pc * 512:(pc + 1) * 512], sv,
                                   pqstage[:, tt * D + pc * 512:tt * D + (pc + 1) * 512],
                                   pre_wait=qw)
            vq_done = wv(V)
            for tt in range(2):
                rows = g * GRP + tt * 128
                op("y", lambda e, i_=xnbf(tt * D, D), o=scr_xn[rows:rows + 128, :]:
                   e.dma_start(out=o, in_=i_), sem=SN, inc=16, wait=(vq_done,))
            sn_done = wv(SN)
            for tt in range(2):
                ti2 = 2 * g + tt
                xstg = qtmp if ti2 % 2 == 0 else pqroll
                xw = [vq_done]
                if ti2 - 2 in xqT_store:
                    xw.append(xqT_store[ti2 - 2])
                for half in range(2):
                    emit_xpose4(
                        [pqstage[:, tt * D + half * 512 + b4 * 128:
                                 tt * D + half * 512 + (b4 + 1) * 128]
                         for b4 in range(4)],
                        xstg[:, half * 512:(half + 1) * 512],
                        src_wait=tuple(xw))
                xq_xp_marks[ti2] = tuple(last_xpose_marks(2))
                op("y", lambda e, i_=xstg, o=scr_xqT[ti2]:
                   e.dma_start(out=o, in_=i_[:]), sem=SXT[ti2 % 2], inc=16,
                   wait=xq_xp_marks[ti2])
                xqT_store[ti2] = wv(SXT[ti2 % 2])
            xn_rd_wait = [sn_done]
            if g > 0:
                xn_rd_wait.append(mm_done_mark)
            for k in range(8):
                op("y", lambda e, o=xnF(k * GRP, GRP),
                   i_=scr_xn[g * GRP:(g + 1) * GRP, k * 128:(k + 1) * 128]:
                   e.dma_start_transpose(o, i_), sem=TD[k], inc=16,
                   wait=tuple(xn_rd_wait) if k == 0 else ())
            xpose_marks_g = last_xpose_marks(8)

            xn_evac_marks = tuple((TD[k], TD[k].n) for k in range(8))
            sd_mark = None
            eps_mark = None
            scan_marks = {}

            def emit_htrans_stage(mh):
                for tt in range(2):
                    pdst2 = pb[4 + tt][:, 0:128]
                    tw = [scan_marks[mh]]
                    if hbank_mark[tt] is not None:
                        tw.append(hbank_mark[tt])
                    op("t", lambda e, i_=epi[:, 5 * GRP + tt * 128:5 * GRP + (tt + 1) * 128], o=pdst2:
                       e.transpose(o, i_, idf[:]), sem=T, inc=1, wait=tuple(tw))
                    hmark = wv(T)
                    stage = (qt, qcm, qnd, qg1)[tt * 2 + (mh // 4)]
                    scol = (mh % 4) * 128
                    op("v", lambda e, i_=pdst2, o=stage[:, scol:scol + 128],
                       x_=xg[:, tt * D + mh * 128: tt * D + mh * 128 + 128]:
                       e.tensor_tensor(o, i_, x_, OP.add), wait=(hmark,))
                    hbank_mark[tt] = wv(V)

            def emit_acc_dma(h2):
                vd = wv(V)
                for tt in range(2):
                    stage = (qt, qcm, qnd, qg1)[tt * 2 + h2]
                    rows = g * GRP + tt * 128
                    op("y", lambda e, i_=stage,
                       o=scr_acc[rows:rows + 128, h2 * 512:(h2 + 1) * 512]:
                       e.dma_start(out=o, in_=i_[:]), sem=SA, inc=16, wait=(vd,))

            for m in range(8):
                par = m % 2
                pg = pb[2 * par][:, 0:GRP]
                pv = pb[2 * par][:, GRP:2 * GRP]
                pd = pb[2 * par + 1][:, 0:GRP]
                mm_wait = list(xn_evac_marks) if m < 2 else []
                if act_marks[par] is not None:
                    mm_wait.append(act_marks[par])
                mm_wait = tuple(mm_wait)
                first = True
                for mat, pdst in (("g", pg), ("v", pv), ("d", pd)):
                    for k in range(8):
                        op("t", lambda e, l=attn_lhs(mat, k, m),
                           r=xnF(k * GRP, GRP), o=pdst, st=(k == 0), sp=(k == 7):
                           e.matmul(o, l, r, start=st, stop=sp),
                           sem=T, inc=(1 if k == 7 else 0),
                           wait=mm_wait if first else ())
                        first = False
                t_gvd = wv(T)
                if m >= 1:
                    emit_htrans_stage(m - 1)
                    if m == 4:
                        emit_acc_dma(0)
                op("s", lambda e, i_=pg, o=sgb, b=bias_sb[:, m:m + 1]:
                   e.activation(o, i_, AF.Sigmoid, bias=b, scale=1.0),
                   wait=(t_gvd,) if eps_mark is None else (t_gvd, eps_mark))
                op("s", lambda e, i_=pv, o=thb, b=bias_sb[:, 8 + m:9 + m]:
                   e.activation(o, i_, AF.Tanh, bias=b, scale=1.0))
                op("s", lambda e, i_=pd, o=sdb, b=bias_sb[:, 16 + m:17 + m]:
                   e.activation(o, i_, AF.Sigmoid, bias=b, scale=1.0))
                sd_mark = wv(S)
                act_marks[par] = sd_mark
                op("v", lambda e: e.tensor_tensor(xsb, sgb,
                                                  thb, OP.mult), wait=(sd_mark,))
                op("v", lambda e: e.tensor_scalar(ab, sdb,
                                                  0.998, 0.001, OP.mult, OP.add))
                eps_mark = wv(V)
                op("v", lambda e, c=carry[:, m:m + 1]: e.tensor_tensor_scan(
                    hb, ab, xsb, c, OP.mult, OP.add))
                op("v", lambda e, c=carry[:, m:m + 1]: e.tensor_copy(c, epi[:, 6 * GRP - 1:6 * GRP]))
                scan_marks[m] = wv(V)
                if m >= 1:
                    prep_pump(2 if m <= 4 else 3, "early" if m <= 4 else "late")
            emit_htrans_stage(7)
            xg_read_done[g] = wv(V)
            emit_acc_dma(1)

            sa_group[g] = SA.n
            sn_group[g] = sn_done[1]
            mm_done_mark = wv(T)

        op("v", lambda e: e.nop(), wait=((SA, sa_group[NG - 1]),))
        op("s", lambda e: e.nop(), wait=((SA, sa_group[NG - 1]),))
        while prep_cur["i"] < len(prep_items):
            prep_items[prep_cur["i"]]("late")
            prep_cur["i"] += 1
        p_flush()
        p_flush()
        p_flush()
        st_wd = Pst["st"]["d"]
        st_wg = Pst["st"]["g"]
        st_wu = Pst["st"]["u"]

        last_attn_T = wv(T)
        for k in range(8):
            op("y", lambda e, o=wqg[:, k * INNER:(k + 1) * INNER],
               i_=scr_wgT[k * 128:(k + 1) * 128, :]:
               e.dma_start(out=o, in_=i_), sem=WT, inc=16,
               wait=(st_wg + (last_attn_T,)) if k == 0 else ())
        for k in range(8):
            op("y", lambda e, o=wqu[:, k * INNER:(k + 1) * INNER],
               i_=scr_wuT[k * 128:(k + 1) * 128, :]:
               e.dma_start(out=o, in_=i_), sem=WT, inc=16,
               wait=(st_wu + (last_attn_T, (SN, sn_group[NG - 1]))) if k == 0 else ())
        wt_done = wv(WT)

        slab_ctr = {"n": 0}
        st_pq = {}
        t_after_tile = {}
        bank_marks = [None] * 4
        sg2_marks = [None, None]
        prep_w_marks = (wt_done,)
        roll = [pqroll[:, 0:512], pqroll[:, 512:1024], qtmp[:, 0:512], qtmp[:, 512:1024]]
        b1_piece_marks = {}
        roll_reader = {}
        gqs = {}
        gfin = {}

        def b1_xpose_piece(ti, pc, extra=()):
            slot = roll[pc % 4]
            emit_xpose4([slot[:, b4 * 128:(b4 + 1) * 128] for b4 in range(4)],
                        pqstage[:, pc * 512:(pc + 1) * 512],
                        src_wait=(gfin[(ti, pc)],), extra_wait=extra)
            b1_piece_marks[(ti, pc)] = tp_state["marks"][-1]
            roll_reader[(ti, pc + 4)] = tp_state["marks"][-1]
            if pc + 4 >= 8:
                roll_reader[(ti + 1, pc + 4 - 8)] = tp_state["marks"][-1]

        def b1_front(ti, pc):
            pk = (ti, pc - 2) if (ti, pc - 2) in gqs else (ti - 1, pc + 6)
            qg = (gqs[pk][2],) if pk in gqs else ()
            gqs[(ti, pc)] = gq_front(pbuf[:, pc * 512:(pc + 1) * 512],
                                     scal[:, 1:2], pc % 2, qt_guard=qg)

        def b1_tail(ti, pc):
            bg = []
            if (ti, pc) in roll_reader:
                bg.append(roll_reader[(ti, pc)])
            if ti == 0 and pc < 4:
                bg.append(xqT_store[NT - 1] if pc < 2 else xqT_store[NT - 2])
            gfin[(ti, pc)] = gq_tail(gqs[(ti, pc)], roll[pc % 4],
                                     b_guard=tuple(bg))

        st_pq_lo = {}
        st_pq_hi = {}

        def b1_store_lo(ti):
            marks = tuple(b1_piece_marks[(ti, pc)] for pc in range(4))
            op("g", lambda e, i_=pqstage[:, 0:2048], o=scr_pqF[ti][:, 0:2048]:
               e.dma_start(out=o, in_=i_), sem=SP, inc=16, wait=marks)
            st_pq_lo[ti] = SP.n

        def b1_store_hi(ti):
            marks = tuple(b1_piece_marks[(ti, pc)] for pc in range(4, 8))
            op("g", lambda e, i_=pqstage[:, 2048:4096], o=scr_pqF[ti][:, 2048:4096]:
               e.dma_start(out=o, in_=i_), sem=SP2, inc=16, wait=marks)
            st_pq_hi[ti] = SP2.n

        t_k_bykg = {}
        t_sup_mark = {}
        evac_v_marks = {}
        wd_ld = {}
        pq_ld = {}
        b2_ld_ptr = {}
        acc_stage = [qcm[:, 0:512], sg_slab[:, 0:512]]
        out_stage = [epi[:, 1024:1536], qg1[:, 0:512]]
        acc_rd_mark = [None, None]
        out_rd_mark = [None, None]
        so_done = {0: 0, 1: 0}

        def b2_load_kstep(u, k):
            kg = u * 32 + k
            lw = LW[kg % 8]
            ww = []
            if kg < 6:
                ww.extend(st_wd)
            if kg - 3 >= 0:
                ww.append((T, t_k_bykg[kg - 3]))
            op("y", lambda e, o=wdr[:, (kg % 3) * 1024:(kg % 3 + 1) * 1024],
               i_=scr_wdT[k * 128:(k + 1) * 128, :]:
               e.dma_start(out=o, in_=i_), sem=lw, inc=16, wait=tuple(ww))
            wd_ld[kg] = wv(lw)
            lp = LP[k % 4]
            pw = []
            if k == 0:
                pw.append((SP, st_pq_lo[2 * u + 1]))
            if k == 16:
                pw.append((SP2, st_pq_hi[2 * u + 1]))
            if kg - 4 >= 0:
                pw.append((T, t_k_bykg[kg - 4]))
            for j in range(2):
                op("y", lambda e,
                   o=pqs[:, (k % 4) * 256 + j * 128:(k % 4) * 256 + (j + 1) * 128],
                   i_=scr_pqF[2 * u + j][:, k * 128:(k + 1) * 128]:
                   e.dma_start(out=o, in_=i_), sem=lp, inc=16,
                   wait=tuple(pw) if j == 0 else ())
            pq_ld[kg] = wv(lp)

        def b2_mm_kstep(u, k):
            kg = u * 32 + k
            first = [wd_ld[kg], pq_ld[kg]]
            if k == 0 and (u - 1) in evac_v_marks:
                first.append((V, evac_v_marks[u - 1]))
            for j in range(2):
                for h in range(2):
                    op("t", lambda e, l=pqs[:, (k % 4) * 256 + j * 128:
                                            (k % 4) * 256 + (j + 1) * 128],
                       r=wdr[:, (kg % 3) * 1024 + h * 512:
                             (kg % 3) * 1024 + (h + 1) * 512],
                       o=pb[4 + 2 * j + h][:], st=(k == 0), sp=(k == 31):
                       e.matmul(o, l, r, start=st, stop=sp),
                       sem=T, inc=(1 if (j == 1 and h == 1) else 0),
                       wait=tuple(first) if (j == 0 and h == 0) else ())
            t_k_bykg[kg] = T.n
            if k == 31:
                t_sup_mark[u] = T.n

        def b2_step(u, sig):
            if u not in b2_ld_ptr:
                b2_ld_ptr[u] = 0
            tgt = min(2 * sig + 2, 31)
            while b2_ld_ptr[u] <= tgt:
                b2_load_kstep(u, b2_ld_ptr[u])
                b2_ld_ptr[u] += 1
            b2_mm_kstep(u, 2 * sig)
            b2_mm_kstep(u, 2 * sig + 1)

        def b2_evac(u):
            tmark = (T, t_sup_mark[u])
            for j in range(2):
                for h in range(2):
                    par = (2 * j + h) % 2
                    aw = []
                    if j == 0 and h == 0:
                        aw.append((SA, sa_group[NG - 1] if u == 0 else sa_group[u]))
                    if acc_rd_mark[par] is not None:
                        aw.append(acc_rd_mark[par])
                    op("g", lambda e, o=acc_stage[par],
                       i_=scr_acc[u * SUP + j * 128:u * SUP + (j + 1) * 128,
                                  h * 512:(h + 1) * 512]:
                       e.dma_start(out=o, in_=i_), sem=LB[par], inc=16,
                       wait=tuple(aw))
                    am = wv(LB[par])
                    sw = [tmark, am]
                    if u == 0 and par == 1 and out_rd_mark[1] is None:
                        sw.append((SA, sa_group[NG - 1]))
                    if out_rd_mark[par] is not None:
                        sw.append(out_rd_mark[par])
                    op("v", lambda e, o=out_stage[par], p_=pb[4 + 2 * j + h][:],
                       sc=dpwall[:, 2 * u + j:2 * u + j + 1], a_=acc_stage[par]:
                       e.scalar_tensor_tensor(o, p_, sc, a_, OP.mult, OP.add),
                       wait=tuple(sw))
                    acc_rd_mark[par] = wv(V)
                    rows = u * SUP + j * 128
                    op("g", lambda e, i_=out_stage[par],
                       o=d_out[rows:rows + 128, h * 512:(h + 1) * 512]:
                       e.dma_start(out=o, in_=i_[:]), sem=SO[par], inc=16,
                       wait=(acc_rd_mark[par],))
                    out_rd_mark[par] = wv(SO[par])
            evac_v_marks[u] = V.n

        ld_xq_mark = {}

        def emit_xq_load(ti):
            xq_wait = [xqT_store[ti]]
            if ti >= 2:
                xq_wait.append((T, t_after_tile[ti - 2]))
            op("y", lambda e, o=xqF_t[ti % 2][:], i_=scr_xqT[ti]:
               e.dma_start(out=o, in_=i_), sem=LQ[ti % 2], inc=16,
               wait=tuple(xq_wait))
            ld_xq_mark[ti] = wv(LQ[ti % 2])

        for i in range(NT):
            xqt = xqF_t[i % 2]
            if i == 0:
                emit_xq_load(0)
            if i + 1 < NT:
                emit_xq_load(i + 1)
            ld_xq = (ld_xq_mark[i],)
            op("v", lambda e, o=dvecs[:, 0:1], a=dxall[:, i:i + 1], b=wsc[:, 1:2]:
               e.tensor_tensor(o, a, b, OP.mult), wait=(wv(S),))
            op("v", lambda e, o=dvecs[:, 1:2], a=dxall[:, i:i + 1], b=wsc[:, 3:4]:
               e.tensor_tensor(o, a, b, OP.mult))
            dv_mark = wv(V)

            for s_ in range(8):
                bkg = (2 * slab_ctr["n"]) % 4
                bku = (2 * slab_ctr["n"] + 1) % 4
                slab_ctr["n"] += 1
                pg_, pu_ = pb[bkg][:], pb[bku][:]
                mmw = list(ld_xq) if s_ == 0 else []
                if i == 0 and s_ == 0:
                    mmw.extend(prep_w_marks)
                for bk in (bkg, bku):
                    if bank_marks[bk] is not None:
                        mmw.append(bank_marks[bk])
                for k in range(8):
                    bg0 = k * INNER + s_ * 512
                    op("t", lambda e, l=xqt[:, k * 128:(k + 1) * 128],
                       r=wqg[:, bg0:bg0 + 512], o=pg_, st=(k == 0), sp=(k == 7):
                       e.matmul(o, l, r, start=st, stop=sp),
                       sem=T, inc=(1 if k == 7 else 0), wait=tuple(mmw) if k == 0 else ())
                    op("t", lambda e, l=xqt[:, k * 128:(k + 1) * 128],
                       r=wqu[:, bg0:bg0 + 512], o=pu_, st=(k == 0), sp=(k == 7):
                       e.matmul(o, l, r, start=st, stop=sp),
                       sem=T, inc=(1 if k == 7 else 0))
                tmm = wv(T)
                spar = slab_ctr["n"] % 2
                sw = [tmm]
                if sg2_marks[spar] is not None:
                    sw.append(sg2_marks[spar])
                if s_ <= 1:
                    sw.append(dv_mark)
                op("s", lambda e, i_=pg_, o=sg2[spar], sc=dvecs[:, 0:1]:
                   e.activation(o, i_, AF.Silu, bias=0.0, scale=sc), wait=tuple(sw))
                silu_mark = wv(S)
                bank_marks[bkg] = silu_mark
                if i > 0:
                    if s_ == 0:
                        b1_front(i - 1, 0)
                    if s_ <= 6:
                        b1_front(i - 1, s_ + 1)
                op("v", lambda e, i_=pu_, o=pbuf[:, s_ * 512:(s_ + 1) * 512], sc=dvecs[:, 1:2],
                   sg=sg2[spar]:
                   e.scalar_tensor_tensor(o, i_, sc, sg, OP.mult, OP.mult),
                   wait=(silu_mark,))
                sg2_marks[spar] = wv(V)
                bank_marks[bku] = wv(V)
                if i > 0:
                    b1_tail(i - 1, s_)
                    if s_ >= 1:
                        if s_ == 1 and i >= 2:
                            ex = ((SP, st_pq_lo[i - 2]),)
                        elif s_ == 1 and i == 1:
                            ex = st_wu
                        elif s_ == 5 and i >= 2:
                            ex = ((SP2, st_pq_hi[i - 2]),)
                        else:
                            ex = ()
                        b1_xpose_piece(i - 1, s_ - 1, extra=ex)
                    if s_ == 6:
                        b1_store_lo(i - 1)
                if i >= 3:
                    b2_step((i - 3) // 2, ((i - 3) % 2) * 8 + s_)
            t_after_tile[i] = T.n
            if i > 0:
                b1_xpose_piece(i - 1, 7)
                b1_store_hi(i - 1)
            mxp, mcp, recp, svp = scal[:, 6:7], scal[:, 7:8], scal[:, 0:1], scal[:, 1:2]
            op("v", lambda e, o=mxp: e.tensor_reduce(o, pbuf[:], AX.X, OP.max,
                                                     apply_absolute_value=True))
            op("v", lambda e, i_=mxp, o=mcp: e.tensor_scalar(o, i_, 1e-5, None, OP.max))
            op("v", lambda e, i_=mcp, o=recp: e.reciprocal(o, i_))
            op("v", lambda e, i_=recp, o=svp: e.tensor_scalar(o, i_, 127.0, None, OP.mult))
            op("v", lambda e, i_=mcp, o=scal[:, 2:3]:
               e.tensor_scalar(o, i_, 1.0 / 127.0, None, OP.mult))
            op("v", lambda e, o=dpwall[:, i:i + 1], a=scal[:, 2:3], b=wsc[:, 5:6]:
               e.tensor_tensor(o, a, b, OP.mult))
            if i >= 3 and (i - 3) % 2 == 1:
                b2_evac((i - 3) // 2)
        i = NT - 1
        for pc in range(8):
            b1_front(i, pc)
            b1_tail(i, pc)
            if pc >= 1:
                if pc == 1:
                    ex = ((SP, st_pq_lo[NT - 2]),)
                elif pc == 5:
                    ex = ((SP2, st_pq_hi[NT - 2]),)
                else:
                    ex = ()
                b1_xpose_piece(i, pc - 1, extra=ex)
            if pc == 4:
                b1_store_lo(i)
        b1_xpose_piece(i, 7)
        b1_store_hi(i)

        for sig in range(8, 16):
            b2_step(14, sig)
        b2_evac(14)
        for sig in range(16):
            b2_step(15, sig)
        b2_evac(15)

        fin0, fin1 = SO[0].n, SO[1].n
        op("y", lambda e, a=fin0, b=fin1: e.wait_ge(SO[0].h, a))
        op("y", lambda e, a=fin0, b=fin1: e.wait_ge(SO[1].h, b))

        @block.sync
        def _(e):
            for f in plan["y"]:
                f(e)

        @block.gpsimd
        def _(e):
            for f in plan["g"]:
                f(e)

        @block.vector
        def _(e):
            e.wait_ge(G.h, g_after_consts)
            for f in plan["v"]:
                f(e)

        @block.scalar
        def _(e):
            e.wait_ge(G.h, g_after_consts)
            for f in plan["s"]:
                f(e)

        @block.tensor
        def _(e):
            e.wait_ge(G.h, g_after_consts)
            for f in plan["t"]:
                f(e)

    return nc


_NC_CACHE = {}


def kernel(L0_attn_x, L0_attn_Wg, L0_attn_bg, L0_attn_Wv, L0_attn_bv,
           L0_attn_Wd, L0_attn_bd, L0_ffn_w_gate, L0_ffn_w_up, L0_ffn_w_down):
    x = np.ascontiguousarray(np.asarray(L0_attn_x, dtype=np.float32))
    B = x.shape[0]
    Wg = np.ascontiguousarray(np.asarray(L0_attn_Wg, np.float32))
    Wv = np.ascontiguousarray(np.asarray(L0_attn_Wv, np.float32))
    Wd = np.ascontiguousarray(np.asarray(L0_attn_Wd, np.float32))
    bg = np.asarray(L0_attn_bg, np.float32).reshape(8, 128).T
    bv = np.asarray(L0_attn_bv, np.float32).reshape(8, 128).T
    bd = np.asarray(L0_attn_bd, np.float32).reshape(8, 128).T
    biasT = np.ascontiguousarray(np.concatenate([bg, bv, bd], axis=1))
    wgate = np.ascontiguousarray(np.asarray(L0_ffn_w_gate, np.float32))
    wup = np.ascontiguousarray(np.asarray(L0_ffn_w_up, np.float32))
    wdown = np.ascontiguousarray(np.asarray(L0_ffn_w_down, np.float32))

    consts = np.zeros((128, 16), np.float32)
    consts[:, 0] = -0.5
    consts[:, 1] = 0.5
    consts[:, 2] = MAGIC
    consts[:, 3] = -MAGIC
    consts[:, 4] = 1e-5
    identf = np.eye(128, dtype=np.float32)
    identb = np.asarray(np.eye(128, dtype=np.float32).astype(ml_dtypes.bfloat16))
    ones128 = np.ones((128, 1), np.float32)
    onesT = np.ones((1, 128), np.float32)

    if "nc" not in _NC_CACHE:
        _NC_CACHE["nc"] = build_program()
    nc = _NC_CACHE["nc"]

    in_maps = []
    for b in range(B):
        in_maps.append(dict(
            x=np.ascontiguousarray(x[b]), Wg=Wg, Wv=Wv, Wd=Wd, biasT=biasT,
            wgate=wgate, wup=wup, wdown=wdown, consts=consts,
            identf=identf, identb=identb, ones128=ones128, onesT=onesT,
        ))
    res = run_bass_kernel_spmd(nc, in_maps, core_ids=list(range(B)))
    out = np.stack([np.asarray(r["out"], np.float32) for r in res.results], axis=0)
    return out


# version 18
# speedup vs baseline: 1.2248x; 1.1341x over previous
import numpy as np
import ml_dtypes
import concourse.bass as bass
import concourse.mybir as mybir
from concourse.bass_utils import run_bass_kernel_spmd
from contextlib import ExitStack

F32 = mybir.dt.float32
BF16 = mybir.dt.bfloat16
I32 = mybir.dt.int32
AF = mybir.ActivationFunctionType
OP = mybir.AluOpType
AX = mybir.AxisListType

MAGIC = 12582912.0
T_TOK = 4096
D = 1024
INNER = 4096
NT = T_TOK // 128
GRP = 256
NG = T_TOK // GRP
SUP = 256
NSUP = T_TOK // SUP


class SemC:
    def __init__(self, name):
        self.name = name
        self.h = None
        self.n = 0


def build_program():
    nc = bass.Bass("TRN2", target_bir_lowering=False)

    d_x = nc.dram_tensor("x", [T_TOK, D], F32, kind="ExternalInput").ap()
    d_Wg = nc.dram_tensor("Wg", [D, D], F32, kind="ExternalInput").ap()
    d_Wv = nc.dram_tensor("Wv", [D, D], F32, kind="ExternalInput").ap()
    d_Wd = nc.dram_tensor("Wd", [D, D], F32, kind="ExternalInput").ap()
    d_bias = nc.dram_tensor("biasT", [128, 24], F32, kind="ExternalInput").ap()
    d_wgate = nc.dram_tensor("wgate", [INNER, D], F32, kind="ExternalInput").ap()
    d_wup = nc.dram_tensor("wup", [INNER, D], F32, kind="ExternalInput").ap()
    d_wdown = nc.dram_tensor("wdown", [D, INNER], F32, kind="ExternalInput").ap()
    d_consts = nc.dram_tensor("consts", [128, 16], F32, kind="ExternalInput").ap()
    d_idf = nc.dram_tensor("identf", [128, 128], F32, kind="ExternalInput").ap()
    d_idb = nc.dram_tensor("identb", [128, 128], BF16, kind="ExternalInput").ap()
    d_ones = nc.dram_tensor("ones128", [128, 1], F32, kind="ExternalInput").ap()
    d_onesT = nc.dram_tensor("onesT", [1, 128], F32, kind="ExternalInput").ap()
    d_out = nc.dram_tensor("out", [T_TOK, D], F32, kind="ExternalOutput").ap()

    scr_acc = nc.dram_tensor("scr_acc", [T_TOK, D], F32, kind="Internal").ap()
    scr_xqT = nc.dram_tensor("scr_xqT", [NT, 128, D], BF16, kind="Internal").ap()
    scr_xn = nc.dram_tensor("scr_xn", [T_TOK, D], BF16, kind="Internal").ap()
    scr_pqF = nc.dram_tensor("scr_pqF", [NT, 128, INNER], BF16, kind="Internal").ap()
    scr_wgT = nc.dram_tensor("scr_wgT", [D, INNER], BF16, kind="Internal").ap()
    scr_wuT = nc.dram_tensor("scr_wuT", [D, INNER], BF16, kind="Internal").ap()
    scr_wdT = nc.dram_tensor("scr_wdT", [INNER, D], BF16, kind="Internal").ap()

    es = ExitStack()

    def sb(name, shape, dt):
        return es.enter_context(nc.sbuf_tensor(name, shape, dt))

    def ps(name, shape, dt):
        return es.enter_context(nc.psum_tensor(name, shape, dt))

    with es:
        wqg = sb("wqg", [128, 8 * INNER], BF16)
        wqu = sb("wqu", [128, 8 * INNER], BF16)
        x_grp = [sb("x_grp0", [128, 2 * D], F32), sb("x_grp1", [128, 2 * D], F32)]
        epi = sb("epi", [128, 6 * GRP], F32)
        sgb = epi[:, 0 * GRP:1 * GRP]
        thb = epi[:, 1 * GRP:2 * GRP]
        sdb = epi[:, 2 * GRP:3 * GRP]
        xsb = epi[:, 3 * GRP:4 * GRP]
        ab = epi[:, 4 * GRP:5 * GRP]
        hb = epi[:, 5 * GRP:6 * GRP]
        carry = sb("carry", [128, 8], F32)
        qt = sb("qt", [128, 512], F32)
        qcm = sb("qcm", [128, 512], F32)
        qnd = sb("qnd", [128, 512], F32)
        qg1 = sb("qg1", [128, 512], F32)
        nhalf = sb("nhalf", [128, 512], F32)
        phalf = sb("phalf", [128, 512], F32)
        qci = sb("qci", [128, 512], I32)
        qtmp = sb("qtmp", [128, D], BF16)
        meanall = sb("meanall", [128, NT], F32)
        rstdall = sb("rstdall", [128, NT], F32)
        dxall = sb("dxall", [128, NT], F32)
        varall = dxall
        dpwall = sb("dpwall", [128, NT], F32)
        wsc = sb("wsc", [128, 8], F32)
        consts_sb = sb("consts_sb", [128, 16], F32)
        bias_sb = sb("bias_sb", [128, 24], F32)
        idf = sb("idf_sb", [128, 128], F32)
        ones128 = sb("ones128_sb", [128, 1], F32)
        onesT = sb("onesT_sb", [1, 128], F32)
        scal = sb("scal", [128, 8], F32)
        tin1 = sb("tin1", [1, 8], F32)
        pbuf = sb("pbuf", [128, INNER], F32)
        sg_slab = sb("sg_slab", [128, 512], F32)
        sg2 = [epi[:, 512:1024], epi[:, 0:512]]
        wdr = sb("wdr", [128, 3 * 1024], BF16)
        pqs = sb("pqs", [128, 4 * 256], BF16)
        pqroll = sb("pqroll", [128, 1024], BF16)
        pqstage = sb("pqstage", [128, INNER], BF16)
        xqF_t = [sb("xqF_t0", [128, 8 * 128], BF16), sb("xqF_t1", [128, 8 * 128], BF16)]
        xbig = wqu[:, 8192:16384]
        dvecs = sb("dvecs", [128, 4], F32)
        wchunk = [pbuf[:, j * D:(j + 1) * D] for j in range(4)]

        def accsup(c0, w):
            return pbuf[:, c0:c0 + w]

        gnd2 = x_grp[0][:, 0:512]
        gb1 = x_grp[0][:, 512:1024]
        gb2 = x_grp[0][:, 1024:1536]
        gc1 = x_grp[0][:, 1536:2048]
        gcm = [x_grp[1][:, 0:512], x_grp[1][:, 512:1024]]

        ATTN_BASE = {"g": 0, "v": 8192, "d": 16384}

        def attn_lhs(mat, k, m):
            b = ATTN_BASE[mat] + k * D + m * 128
            return wqg[:, b:b + 128]

        def xnF(c0, w):
            return wqu[:, c0:c0 + w]

        def xqF(c0, w):
            return wqu[:, 2048 + c0:2048 + c0 + w]

        def xnbf(c0, w):
            return wqu[:, 4096 + c0:4096 + c0 + w]

        def wdT_build(c0, w):
            return wqu[:, c0:c0 + w]

        def pqF_sup(c0, w):
            return wqg[:, c0:c0 + w]

        def wdT_str(j):
            return wqg[:, 8192 + j * D:8192 + (j + 1) * D]

        pb = [ps(f"pb{i}", [128, 512], F32) for i in range(8)]

        names = ["G", "T", "V", "S", "LA0", "LX0", "LX1", "LP0", "LP1",
                 "LQ0", "LQ1", "LB0", "LB1", "LW0", "LW1", "LW2", "LW3",
                 "LP2", "LP3", "LW4", "LW5", "LW6", "LW7",
                 "TD0", "TD1", "TD2", "TD3", "TD4", "TD5", "TD6", "TD7", "SN", "WT",
                 "SW1", "SW2", "SW3", "GC",
                 "SA", "SX", "SW", "SP", "SP2", "SO0", "SO1", "SXT0", "SXT1",
                 "PD0", "PD1", "PD2", "PD3", "PL0", "PL1", "PL2", "PL3"]
        sems = {n: SemC(n) for n in names}
        for smc in sems.values():
            smc.h = es.enter_context(nc.semaphore(smc.name))
        G, T, V, S = sems["G"], sems["T"], sems["V"], sems["S"]
        LA0 = sems["LA0"]
        LX = [sems["LX0"], sems["LX1"]]
        LP = [sems["LP0"], sems["LP1"], sems["LP2"], sems["LP3"]]
        LQ = [sems["LQ0"], sems["LQ1"]]
        LB = [sems["LB0"], sems["LB1"]]
        LW = [sems[f"LW{j}"] for j in range(8)]
        SA, SX, SW, SP = sems["SA"], sems["SX"], sems["SW"], sems["SP"]
        SP2 = sems["SP2"]
        SXT = [sems["SXT0"], sems["SXT1"]]
        PD = [sems[f"PD{j}"] for j in range(4)]
        PL = [sems[f"PL{j}"] for j in range(4)]
        SO = [sems["SO0"], sems["SO1"]]
        TD = [sems[f"TD{j}"] for j in range(8)]
        SN, WT = sems["SN"], sems["WT"]
        SWs = [sems["SW"], sems["SW1"], sems["SW2"], sems["SW3"]]
        GC = sems["GC"]

        block = es.enter_context(nc.Block())
        plan = {"t": [], "v": [], "s": [], "g": [], "y": []}
        ENG_SEM = {"v": V, "s": S, "gc": GC}

        def op(eng, build, sem=None, inc=0, wait=()):
            waits = list(wait)
            if eng == "gc":
                eng = "g"
                esem = ENG_SEM["gc"]
                if esem.n > 0:
                    waits.append((esem, esem.n))
                sem, inc = esem, 1
            elif eng in ("v", "s"):
                esem = ENG_SEM[eng]
                if esem.n > 0:
                    waits.append((esem, esem.n))
                sem, inc = esem, 1
            for (smc, val) in waits:
                if val > 0:
                    plan[eng].append(lambda e, _s=smc, _v=val: e.wait_ge(_s.h, _v))
            if sem is not None and inc:
                sem.n += inc

            def f(e, _b=build, _s=sem, _i=inc):
                ins = _b(e)
                if _s is not None and _i:
                    ins.then_inc(_s.h, _i)
            plan[eng].append(f)

        def wv(smc):
            return (smc, smc.n)

        C_NH = consts_sb[:, 0:1]
        C_PH = consts_sb[:, 1:2]
        C_MG = consts_sb[:, 2:3]
        C_NMG = consts_sb[:, 3:4]
        C_EPS = consts_sb[:, 4:5]

        def emit_round_neg(src_ap, s_vec_ap, dst_ap, clip1=False, pre_wait=()):
            fd = src_ap.shape[-1]
            tq, ci, cm, nd, b1 = (qt[:, 0:fd], qci[:, 0:fd], qcm[:, 0:fd],
                                  qnd[:, 0:fd], qg1[:, 0:fd])
            if s_vec_ap is None:
                op("v", lambda e, a=src_ap, o=tq: e.tensor_copy(o, a), wait=pre_wait)
            else:
                op("v", lambda e, a=src_ap, sv=s_vec_ap, o=tq:
                   e.tensor_scalar(o, a, sv, None, OP.mult), wait=pre_wait)
            if clip1:
                op("v", lambda e, a=nhalf[:, 0:fd], b=tq, o=b1:
                   e.tensor_tensor(o, a, b, OP.is_gt))
                op("v", lambda e, a=tq, b=phalf[:, 0:fd], o=cm:
                   e.tensor_tensor(o, a, b, OP.is_gt))
                op("v", lambda e, a=b1, b=cm, o=dst_ap: e.tensor_tensor(o, a, b, OP.subtract))
                return
            op("v", lambda e, a=tq, o=ci: e.tensor_scalar(o, a, MAGIC, None, OP.add))
            op("v", lambda e, a=ci, o=cm: e.tensor_scalar(o, a, -MAGIC, None, OP.add))
            op("v", lambda e, a=cm, b=tq, o=nd: e.tensor_tensor(o, a, b, OP.subtract))
            op("v", lambda e, a=nhalf[:, 0:fd], b=nd, o=b1:
               e.tensor_tensor(o, a, b, OP.is_gt))
            op("v", lambda e, a=nd, b=phalf[:, 0:fd], o=tq:
               e.tensor_tensor(o, a, b, OP.is_gt))
            op("v", lambda e, a=tq, b=b1, o=nd: e.tensor_tensor(o, a, b, OP.subtract))
            op("v", lambda e, a=nd, b=cm, o=dst_ap: e.tensor_tensor(o, a, b, OP.subtract))

        qtP = [qt[:, 0:512], x_grp[1][:, 1024:1536]]

        def gq_front(src_ap, s_vec_ap, par, qt_guard=(), pre_wait=()):
            cmP = gcm[par]
            tP = qtP[par]
            ndX = qnd[:, 0:512] if par == 0 else gnd2
            op("v", lambda e, a=src_ap, sv=s_vec_ap, o=tP:
               e.tensor_scalar(o, a, sv, None, OP.mult),
               wait=tuple(pre_wait) + tuple(qt_guard))
            op("v", lambda e, a=tP, o=qci[:, 0:512]:
               e.tensor_scalar(o, a, MAGIC, None, OP.add))
            op("v", lambda e, a=qci[:, 0:512], o=cmP:
               e.tensor_scalar(o, a, -MAGIC, None, OP.add))
            vf = wv(V)
            op("gc", lambda e, a=cmP, b=tP, o=ndX:
               e.tensor_tensor(o, a, b, OP.subtract), wait=(vf,))
            return (ndX, cmP, wv(GC))

        def gq_tail(state, dst_ap, b_guard=()):
            ndX, cmP, gmark = state
            op("v", lambda e, a=nhalf[:, 0:512], b=ndX, o=gb1:
               e.tensor_tensor(o, a, b, OP.is_gt), wait=(gmark,) + tuple(b_guard))
            op("v", lambda e, a=ndX, b=phalf[:, 0:512], o=gb2:
               e.tensor_tensor(o, a, b, OP.is_gt))
            op("v", lambda e, a=gb2, b=gb1, o=gc1:
               e.tensor_tensor(o, a, b, OP.subtract))
            op("v", lambda e, a=gc1, b=cmP, o=dst_ap:
               e.tensor_tensor(o, a, b, OP.subtract))
            return wv(V)

        def emit_round_neg_gw(src_ap, s_vec_ap, dst_ap, pre_wait=(), b_guard=()):
            op("v", lambda e, a=src_ap, sv=s_vec_ap, o=qt[:, 0:512]:
               e.tensor_scalar(o, a, sv, None, OP.mult), wait=pre_wait)
            op("v", lambda e, a=nhalf[:, 0:512], b=qt[:, 0:512], o=gb1:
               e.tensor_tensor(o, a, b, OP.is_gt), wait=tuple(b_guard))
            op("v", lambda e, a=qt[:, 0:512], b=phalf[:, 0:512], o=gb2:
               e.tensor_tensor(o, a, b, OP.is_gt))
            vb = wv(V)
            op("gc", lambda e, a=gb1, b=gb2, o=dst_ap:
               e.tensor_tensor(o, a, b, OP.subtract), wait=(vb,))
            return wv(GC)

        tp_state = {"n": 0, "marks": []}

        def emit_xpose4(srcs, dst_ap, src_wait=(), extra_wait=()):
            n = tp_state["n"]
            tp_state["n"] += 1
            sem = TD[n % 8]
            twait = list(src_wait) + list(extra_wait)
            if n - 8 >= 0:
                twait.append(tp_state["marks"][n - 8])
            for i, s4 in enumerate(srcs):
                op("y", lambda e, i_=s4, o=dst_ap[:, i * 128:(i + 1) * 128]:
                   e.dma_start_transpose(o, i_),
                   sem=sem, inc=16, wait=tuple(twait) if i == 0 else ())
            tp_state["marks"].append(wv(sem))

        def last_xpose_marks(k=8):
            return tuple(tp_state["marks"][-k:]) if tp_state["marks"] else ()

        for dst, src in ((consts_sb, d_consts), (idf, d_idf),
                         (ones128, d_ones), (onesT, d_onesT), (bias_sb, d_bias)):
            op("g", lambda e, o=dst, i=src: e.dma_start(out=o[:], in_=i), sem=G, inc=16)
        for nm, src in (("g", d_Wg), ("v", d_Wv), ("d", d_Wd)):
            for k in range(8):
                b = ATTN_BASE[nm] + k * D
                op("g", lambda e, o=wqg[:, b:b + D], i=src[k * 128:(k + 1) * 128, :]:
                   e.dma_start(out=o, in_=i), sem=G, inc=16)
        g_after_consts = G.n

        op("v", lambda e: e.memset(carry[:], 0.0))
        op("v", lambda e: e.memset(nhalf[:], -0.5))
        op("v", lambda e: e.memset(phalf[:], 0.5))

        for i in range(NT):
            buf, sq = wchunk[0], wchunk[1]
            op("y", lambda e, o=buf, i_=d_x[i * 128:(i + 1) * 128, :]:
               e.dma_start(out=o[:], in_=i_), sem=LA0, inc=16, wait=(wv(V), wv(S)))
            ldw = wv(LA0)
            op("v", lambda e, i_=buf, o=qt[:, i:i + 1]:
               e.tensor_reduce(o, i_[:], AX.X, OP.add), wait=(ldw,))
            op("s", lambda e, i_=buf, o=sq: e.activation(o[:], i_[:], AF.Square, bias=0.0, scale=1.0),
               wait=(ldw, wv(V)))
            op("v", lambda e, i_=sq, o=qcm[:, i:i + 1]:
               e.tensor_reduce(o, i_[:], AX.X, OP.add), wait=(wv(S),))
        op("v", lambda e: e.tensor_scalar(meanall[:], qt[:, 0:NT], 1.0 / D, None, OP.mult))
        op("v", lambda e: e.tensor_scalar(qnd[:, 0:NT], qcm[:, 0:NT], 1.0 / D, None, OP.mult))
        op("v", lambda e: e.tensor_tensor(qg1[:, 0:NT], meanall[:], meanall[:], OP.mult))
        op("v", lambda e: e.tensor_tensor(varall[:], qnd[:, 0:NT], qg1[:, 0:NT], OP.subtract))
        op("s", lambda e: e.activation(qt[:, 32:32 + NT], varall[:], AF.Sqrt, bias=C_EPS, scale=1.0),
           wait=(wv(V),))
        op("v", lambda e: e.reciprocal(qcm[:, 32:32 + NT], qt[:, 32:32 + NT]), wait=(wv(S),))
        op("v", lambda e: e.tensor_tensor(qnd[:, 32:32 + NT], qcm[:, 32:32 + NT],
                                          qcm[:, 32:32 + NT], OP.mult))
        op("v", lambda e: e.tensor_scalar(qg1[:, 32:32 + NT], varall[:], 1e-5, None, OP.add))
        op("v", lambda e: e.tensor_tensor(qt[:, 64:64 + NT], qnd[:, 32:32 + NT],
                                          qg1[:, 32:32 + NT], OP.mult))
        op("v", lambda e: e.tensor_scalar(qcm[:, 64:64 + NT], qt[:, 64:64 + NT],
                                          -0.5, 1.5, OP.mult, OP.add))
        op("v", lambda e: e.tensor_tensor(rstdall[:], qcm[:, 32:32 + NT],
                                          qcm[:, 64:64 + NT], OP.mult))

        def qtmp4(cc, c0, w):
            return pqstage[:, cc * D + c0:cc * D + c0 + w]

        Pst = {"slotmark": {}, "slot_xp": {}, "xbig_st": None, "pend": [],
               "ld": 0, "st": {}, "chunk_slot": {}, "chunk_ld": {}}

        def p_flush():
            pend, Pst["pend"] = Pst["pend"], []
            for f in pend:
                f()

        def p_load(src_ap):
            slot = Pst["ld"] % 4
            Pst["ld"] += 1
            g = Pst["slotmark"].get(slot)
            if g is None:
                g = (wv(V), wv(S))
            else:
                g = (g,)
            op("g", lambda e, o=wchunk[slot], i_=src_ap: e.dma_start(out=o, in_=i_),
               sem=PL[slot], inc=16, wait=g)
            return slot, wv(PL[slot])

        runpair = {"d": (scal[:, 4:5], scal[:, 5:6]),
                   "g": (scal[:, 6:7], scal[:, 7:8]),
                   "u": (dvecs[:, 0:1], dvecs[:, 1:2])}
        rtmp = dvecs[:, 2:3]
        swcol = {"d": 4, "g": 0, "u": 2}

        def mk_mean(mat, src_ap, mi):
            def emit(profile):
                p_flush()
                slot, ldw = p_load(src_ap)
                ra, rb = runpair[mat]
                src_run, dst_run = (ra, rb) if mi % 2 == 0 else (rb, ra)
                op("v", lambda e, i_=wchunk[slot], o=rtmp:
                   e.tensor_reduce(o, i_[:], AX.X, OP.add,
                                   apply_absolute_value=True), wait=(ldw,))
                if mi == 0:
                    op("v", lambda e, o=dst_run: e.tensor_copy(o, rtmp))
                else:
                    op("v", lambda e, a=src_run, o=dst_run:
                       e.tensor_tensor(o, a, rtmp, OP.add))
                Pst["slotmark"][slot] = wv(V)
            return emit

        def mk_scal(mat, sw_col, dw_col, inv_n):
            incol = runpair[mat][0]
            def emit(profile):
                p_flush()
                op("t", lambda e, i_=incol: e.matmul(pb[6][0:1, 0:1], ones128[:], i_,
                   start=True, stop=True), sem=T, inc=1, wait=(wv(V),))
                op("v", lambda e: e.tensor_copy(tin1[0:1, 0:1], pb[6][0:1, 0:1]),
                   wait=(wv(T),))
                op("v", lambda e: e.tensor_scalar(tin1[0:1, 1:2], tin1[0:1, 0:1],
                                                  inv_n, None, OP.mult))
                op("v", lambda e: e.tensor_scalar(tin1[0:1, 2:3], tin1[0:1, 1:2],
                                                  1e-5, None, OP.max))
                op("v", lambda e: e.reciprocal(tin1[0:1, 3:4], tin1[0:1, 2:3]))
                op("v", lambda e: e.reciprocal(tin1[0:1, 4:5], tin1[0:1, 3:4]))
                for srcc, col in ((3, sw_col), (4, dw_col)):
                    op("t", lambda e, i_=tin1[0:1, srcc:srcc + 1]:
                       e.matmul(pb[6][:, 1:2], onesT[:], i_, start=True, stop=True),
                       sem=T, inc=1, wait=(wv(V),))
                    op("v", lambda e, o=wsc[:, col:col + 1]:
                       e.tensor_copy(o, pb[6][:, 1:2]), wait=(wv(T),))
            return emit

        pd_state = {"n": 0, "marks": []}

        def p_xpose(li, vmark, gdst, mat):
            xw = [vmark]
            if li % 8 == 0 and Pst["xbig_st"] is not None:
                xw.extend(Pst["xbig_st"])
            n = pd_state["n"]
            pd_state["n"] += 1
            sem = PD[n % 4]
            if n - 4 >= 0:
                xw.append(pd_state["marks"][n - 4])
            cb = (li % 8) * 128
            for j in range(8):
                op("y", lambda e, i_=qtmp4(2 + li % 2, j * 128, 128),
                   o=xbig[:, j * 1024 + cb:j * 1024 + cb + 128]:
                   e.dma_start_transpose(o, i_), sem=sem, inc=16,
                   wait=tuple(xw) if j == 0 else ())
            pd_state["marks"].append(wv(sem))
            Pst["slot_xp"][li % 2] = pd_state["marks"][-1]
            if li % 8 == 7:
                gm = tuple(pd_state["marks"][-8:])
                Pst["pend"].append(
                    lambda gmm=gm, l2=li: p_store_cb(l2, gmm, gdst, mat))

        def p_store_cb(li, gmarks, gdst, mat):
            g8 = li // 8
            for j in range(8):
                op("g", lambda e, i_=xbig[:, j * 1024:(j + 1) * 1024],
                   o=gdst(g8, j):
                   e.dma_start(out=o, in_=i_), sem=SWs[j % 4], inc=16,
                   wait=gmarks if j == 0 else ())
            Pst["xbig_st"] = tuple(wv(s4) for s4 in SWs)
            if g8 == 3:
                Pst["st"][mat] = Pst["xbig_st"]

        def mk_quant(mat, src_ap, li, pc, gdst):
            def emit(profile):
                p_flush()
                if pc == 0:
                    slot, ldw = p_load(src_ap)
                    Pst["chunk_slot"][(mat, li)] = slot
                    Pst["chunk_ld"][(mat, li)] = ldw
                slot = Pst["chunk_slot"][(mat, li)]
                tq, b1b, b2b = ((qcm, qg1, sg_slab) if profile == "early"
                                else (qt, qnd, sg_slab))
                w0 = [(SA, SA.n)]
                if pc == 0:
                    w0.append(Pst["chunk_ld"][(mat, li)])
                    xg = Pst["slot_xp"].get(li % 2)
                    if xg is not None:
                        w0.append(xg)
                op("v", lambda e, a=wchunk[slot][:, pc * 512:(pc + 1) * 512],
                   sv=wsc[:, swcol[mat]:swcol[mat] + 1], o=tq[:, 0:512]:
                   e.tensor_scalar(o, a, sv, None, OP.mult), wait=tuple(w0))
                op("v", lambda e, a=nhalf[:, 0:512], b=tq[:, 0:512], o=b1b[:, 0:512]:
                   e.tensor_tensor(o, a, b, OP.is_gt))
                op("v", lambda e, a=tq[:, 0:512], b=phalf[:, 0:512], o=b2b[:, 0:512]:
                   e.tensor_tensor(o, a, b, OP.is_gt))
                op("v", lambda e, a=b1b[:, 0:512], b=b2b[:, 0:512],
                   o=qtmp4(2 + li % 2, pc * 512, 512):
                   e.tensor_tensor(o, a, b, OP.subtract))
                if pc == 1:
                    vd = wv(V)
                    Pst["slotmark"][slot] = vd
                    Pst["pend"].append(
                        lambda l2=li, v2=vd: p_xpose(l2, v2, gdst, mat))
            return emit

        gd_d = lambda g8, j: scr_wdT[g8 * D + j * 128:g8 * D + (j + 1) * 128, :]
        gd_g = lambda g8, j: scr_wgT[j * 128:(j + 1) * 128, g8 * 1024:(g8 + 1) * 1024]
        gd_u = lambda g8, j: scr_wuT[j * 128:(j + 1) * 128, g8 * 1024:(g8 + 1) * 1024]
        down_m = [d_wdown[c * 128:(c + 1) * 128, q * D:(q + 1) * D]
                  for c in range(8) for q in range(4)]
        down_q = [d_wdown[c * 128:(c + 1) * 128, q * D:(q + 1) * D]
                  for q in range(4) for c in range(8)]
        gate_c = [d_wgate[c * 128:(c + 1) * 128, :] for c in range(32)]
        up_c = [d_wup[c * 128:(c + 1) * 128, :] for c in range(32)]
        prep_items = []
        for mat, mean_l, quant_l, gd, swc, dwc, inv in (
                ("d", down_m, down_q, gd_d, 4, 5, 1.0 / (D * INNER)),
                ("g", gate_c, gate_c, gd_g, 0, 1, 1.0 / (INNER * D)),
                ("u", up_c, up_c, gd_u, 2, 3, 1.0 / (INNER * D))):
            prep_items += [mk_mean(mat, s, mi) for mi, s in enumerate(mean_l)]
            prep_items += [mk_scal(mat, swc, dwc, inv)]
            prep_items += [mk_quant(mat, s, li, pc, gd)
                           for li, s in enumerate(quant_l) for pc in range(2)]
        prep_cur = {"i": 0}

        def prep_pump(nmax, profile):
            k = 0
            while k < nmax and prep_cur["i"] < len(prep_items):
                prep_items[prep_cur["i"]](profile)
                prep_cur["i"] += 1
                k += 1

        sa_group = {}
        xqT_store = {}
        xq_xp_marks = {}
        hbank_mark = [None, None]
        act_marks = [None, None]
        sn_group = {}
        mm_done_mark = None
        xg_read_done = {}
        for g in range(NG):
            xg = x_grp[g % 2]
            lx = LX[g % 2]
            xl_wait = (xg_read_done[g - 2],) if g >= 2 else (wv(V),)
            for tt in range(2):
                op("y", lambda e, o=xg[:, tt * D:(tt + 1) * D],
                   i_=d_x[g * GRP + tt * 128:g * GRP + (tt + 1) * 128, :]:
                   e.dma_start(out=o, in_=i_),
                   sem=lx, inc=16, wait=xl_wait if tt == 0 else ())
            ld_x = wv(lx)
            if g > 0:
                op("v", lambda e: e.nop(), wait=((SA, sa_group[g - 1]),))
                op("s", lambda e: e.nop(), wait=((SA, sa_group[g - 1]),))

            for tt in range(2):
                idx = 2 * g + tt
                xtile = xg[:, tt * D:(tt + 1) * D]
                xnb = xnbf(tt * D, D)
                ln_wait = [ld_x]
                if g > 0 and tt == 0:
                    ln_wait.append((SN, sn_group[g - 1]))
                op("v", lambda e, i_=xtile, o=xnb, m=meanall[:, idx:idx + 1],
                   r=rstdall[:, idx:idx + 1]:
                   e.tensor_scalar(o, i_, m, r, OP.subtract, OP.mult),
                   wait=tuple(ln_wait))
                mx, mc, rec, sv = scal[:, 0:1], scal[:, 1:2], scal[:, 2:3], scal[:, 3:4]
                op("v", lambda e, i_=xnb, o=mx:
                   e.tensor_reduce(o, i_, AX.X, OP.max, apply_absolute_value=True))
                op("v", lambda e, i_=mx, o=mc: e.tensor_scalar(o, i_, 1e-5, None, OP.max))
                op("v", lambda e, i_=mc, o=rec: e.reciprocal(o, i_))
                op("v", lambda e, i_=rec, o=sv: e.tensor_scalar(o, i_, 127.0, None, OP.mult))
                op("v", lambda e, i_=mc, o=dxall[:, idx:idx + 1]:
                   e.tensor_scalar(o, i_, 1.0 / 127.0, None, OP.mult))
                for pc in range(2):
                    qw = tuple(xq_xp_marks[2 * (g - 1) + tt]) if (g > 0 and pc == 0) else ()
                    emit_round_neg(xnb[:, pc * 512:(pc + 1) * 512], sv,
                                   pqstage[:, tt * D + pc * 512:tt * D + (pc + 1) * 512],
                                   pre_wait=qw)
            vq_done = wv(V)
            for tt in range(2):
                rows = g * GRP + tt * 128
                op("y", lambda e, i_=xnbf(tt * D, D), o=scr_xn[rows:rows + 128, :]:
                   e.dma_start(out=o, in_=i_), sem=SN, inc=16, wait=(vq_done,))
            sn_done = wv(SN)
            for tt in range(2):
                ti2 = 2 * g + tt
                xstg = qtmp if ti2 % 2 == 0 else pqroll
                xw = [vq_done]
                if ti2 - 2 in xqT_store:
                    xw.append(xqT_store[ti2 - 2])
                for half in range(2):
                    emit_xpose4(
                        [pqstage[:, tt * D + half * 512 + b4 * 128:
                                 tt * D + half * 512 + (b4 + 1) * 128]
                         for b4 in range(4)],
                        xstg[:, half * 512:(half + 1) * 512],
                        src_wait=tuple(xw))
                xq_xp_marks[ti2] = tuple(last_xpose_marks(2))
                op("y", lambda e, i_=xstg, o=scr_xqT[ti2]:
                   e.dma_start(out=o, in_=i_[:]), sem=SXT[ti2 % 2], inc=16,
                   wait=xq_xp_marks[ti2])
                xqT_store[ti2] = wv(SXT[ti2 % 2])
            xn_rd_wait = [sn_done]
            if g > 0:
                xn_rd_wait.append(mm_done_mark)
            for k in range(8):
                op("y", lambda e, o=xnF(k * GRP, GRP),
                   i_=scr_xn[g * GRP:(g + 1) * GRP, k * 128:(k + 1) * 128]:
                   e.dma_start_transpose(o, i_), sem=TD[k], inc=16,
                   wait=tuple(xn_rd_wait) if k == 0 else ())
            xpose_marks_g = last_xpose_marks(8)

            xn_evac_marks = tuple((TD[k], TD[k].n) for k in range(8))
            sd_mark = None
            eps_mark = None
            scan_marks = {}

            def emit_htrans_stage(mh):
                for tt in range(2):
                    pdst2 = pb[4 + tt][:, 0:128]
                    tw = [scan_marks[mh]]
                    if hbank_mark[tt] is not None:
                        tw.append(hbank_mark[tt])
                    op("t", lambda e, i_=epi[:, 5 * GRP + tt * 128:5 * GRP + (tt + 1) * 128], o=pdst2:
                       e.transpose(o, i_, idf[:]), sem=T, inc=1, wait=tuple(tw))
                    hmark = wv(T)
                    stage = (qt, qcm, qnd, qg1)[tt * 2 + (mh // 4)]
                    scol = (mh % 4) * 128
                    op("v", lambda e, i_=pdst2, o=stage[:, scol:scol + 128],
                       x_=xg[:, tt * D + mh * 128: tt * D + mh * 128 + 128]:
                       e.tensor_tensor(o, i_, x_, OP.add), wait=(hmark,))
                    hbank_mark[tt] = wv(V)

            def emit_acc_dma(h2):
                vd = wv(V)
                for tt in range(2):
                    stage = (qt, qcm, qnd, qg1)[tt * 2 + h2]
                    rows = g * GRP + tt * 128
                    op("y", lambda e, i_=stage,
                       o=scr_acc[rows:rows + 128, h2 * 512:(h2 + 1) * 512]:
                       e.dma_start(out=o, in_=i_[:]), sem=SA, inc=16, wait=(vd,))

            for m in range(8):
                par = m % 2
                pg = pb[2 * par][:, 0:GRP]
                pv = pb[2 * par][:, GRP:2 * GRP]
                pd = pb[2 * par + 1][:, 0:GRP]
                mm_wait = list(xn_evac_marks) if m < 2 else []
                if act_marks[par] is not None:
                    mm_wait.append(act_marks[par])
                mm_wait = tuple(mm_wait)
                first = True
                for mat, pdst in (("g", pg), ("v", pv), ("d", pd)):
                    for k in range(8):
                        op("t", lambda e, l=attn_lhs(mat, k, m),
                           r=xnF(k * GRP, GRP), o=pdst, st=(k == 0), sp=(k == 7):
                           e.matmul(o, l, r, start=st, stop=sp),
                           sem=T, inc=(1 if k == 7 else 0),
                           wait=mm_wait if first else ())
                        first = False
                t_gvd = wv(T)
                if m >= 1:
                    emit_htrans_stage(m - 1)
                    if m == 4:
                        emit_acc_dma(0)
                op("s", lambda e, i_=pg, o=sgb, b=bias_sb[:, m:m + 1]:
                   e.activation(o, i_, AF.Sigmoid, bias=b, scale=1.0),
                   wait=(t_gvd,) if eps_mark is None else (t_gvd, eps_mark))
                op("s", lambda e, i_=pv, o=thb, b=bias_sb[:, 8 + m:9 + m]:
                   e.activation(o, i_, AF.Tanh, bias=b, scale=1.0))
                op("s", lambda e, i_=pd, o=sdb, b=bias_sb[:, 16 + m:17 + m]:
                   e.activation(o, i_, AF.Sigmoid, bias=b, scale=1.0))
                sd_mark = wv(S)
                act_marks[par] = sd_mark
                op("v", lambda e: e.tensor_tensor(xsb, sgb,
                                                  thb, OP.mult), wait=(sd_mark,))
                op("v", lambda e: e.tensor_scalar(ab, sdb,
                                                  0.998, 0.001, OP.mult, OP.add))
                eps_mark = wv(V)
                op("v", lambda e, c=carry[:, m:m + 1]: e.tensor_tensor_scan(
                    hb, ab, xsb, c, OP.mult, OP.add))
                op("v", lambda e, c=carry[:, m:m + 1]: e.tensor_copy(c, epi[:, 6 * GRP - 1:6 * GRP]))
                scan_marks[m] = wv(V)
                if m >= 1:
                    prep_pump(2 if m <= 4 else 3, "early" if m <= 4 else "late")
            emit_htrans_stage(7)
            xg_read_done[g] = wv(V)
            emit_acc_dma(1)

            sa_group[g] = SA.n
            sn_group[g] = sn_done[1]
            mm_done_mark = wv(T)

        op("v", lambda e: e.nop(), wait=((SA, sa_group[NG - 1]),))
        op("s", lambda e: e.nop(), wait=((SA, sa_group[NG - 1]),))
        while prep_cur["i"] < len(prep_items):
            prep_items[prep_cur["i"]]("late")
            prep_cur["i"] += 1
        p_flush()
        p_flush()
        p_flush()
        st_wd = Pst["st"]["d"]
        st_wg = Pst["st"]["g"]
        st_wu = Pst["st"]["u"]

        last_attn_T = wv(T)
        for k in range(8):
            op("y", lambda e, o=wqg[:, k * INNER:(k + 1) * INNER],
               i_=scr_wgT[k * 128:(k + 1) * 128, :]:
               e.dma_start(out=o, in_=i_), sem=WT, inc=16,
               wait=(st_wg + (last_attn_T,)) if k == 0 else ())
        for k in range(8):
            op("y", lambda e, o=wqu[:, k * INNER:(k + 1) * INNER],
               i_=scr_wuT[k * 128:(k + 1) * 128, :]:
               e.dma_start(out=o, in_=i_), sem=WT, inc=16,
               wait=(st_wu + (last_attn_T, (SN, sn_group[NG - 1]))) if k == 0 else ())
        wt_done = wv(WT)

        slab_ctr = {"n": 0}
        st_pq = {}
        t_after_tile = {}
        bank_marks = [None] * 4
        sg2_marks = [None, None]
        prep_w_marks = (wt_done,)
        roll = [pqroll[:, 0:512], pqroll[:, 512:1024], qtmp[:, 0:512], qtmp[:, 512:1024]]
        b1_piece_marks = {}
        roll_reader = {}
        gqs = {}
        gfin = {}

        def b1_xpose_piece(ti, pc, extra=()):
            slot = roll[pc % 4]
            emit_xpose4([slot[:, b4 * 128:(b4 + 1) * 128] for b4 in range(4)],
                        pqstage[:, pc * 512:(pc + 1) * 512],
                        src_wait=(gfin[(ti, pc)],), extra_wait=extra)
            b1_piece_marks[(ti, pc)] = tp_state["marks"][-1]
            roll_reader[(ti, pc + 4)] = tp_state["marks"][-1]
            if pc + 4 >= 8:
                roll_reader[(ti + 1, pc + 4 - 8)] = tp_state["marks"][-1]

        def b1_front(ti, pc):
            pk = (ti, pc - 2) if (ti, pc - 2) in gqs else (ti - 1, pc + 6)
            qg = (gqs[pk][2],) if pk in gqs else ()
            gqs[(ti, pc)] = gq_front(pbuf[:, pc * 512:(pc + 1) * 512],
                                     scal[:, 1:2], pc % 2, qt_guard=qg)

        def b1_tail(ti, pc):
            bg = []
            if (ti, pc) in roll_reader:
                bg.append(roll_reader[(ti, pc)])
            if ti == 0 and pc < 4:
                bg.append(xqT_store[NT - 1] if pc < 2 else xqT_store[NT - 2])
            gfin[(ti, pc)] = gq_tail(gqs[(ti, pc)], roll[pc % 4],
                                     b_guard=tuple(bg))

        st_pq_lo = {}
        st_pq_hi = {}

        def b1_store_lo(ti):
            marks = tuple(b1_piece_marks[(ti, pc)] for pc in range(4))
            op("g", lambda e, i_=pqstage[:, 0:2048], o=scr_pqF[ti][:, 0:2048]:
               e.dma_start(out=o, in_=i_), sem=SP, inc=16, wait=marks)
            st_pq_lo[ti] = SP.n

        def b1_store_hi(ti):
            marks = tuple(b1_piece_marks[(ti, pc)] for pc in range(4, 8))
            op("g", lambda e, i_=pqstage[:, 2048:4096], o=scr_pqF[ti][:, 2048:4096]:
               e.dma_start(out=o, in_=i_), sem=SP2, inc=16, wait=marks)
            st_pq_hi[ti] = SP2.n

        t_k_bykg = {}
        t_sup_mark = {}
        evac_v_marks = {}
        wd_ld = {}
        pq_ld = {}
        b2_ld_ptr = {}
        acc_stage = [qcm[:, 0:512], sg_slab[:, 0:512]]
        out_stage = [epi[:, 1024:1536], qg1[:, 0:512]]
        acc_rd_mark = [None, None]
        out_rd_mark = [None, None]
        so_done = {0: 0, 1: 0}

        def b2_load_kstep(u, k):
            kg = u * 32 + k
            lw = LW[kg % 8]
            ww = []
            if kg < 6:
                ww.extend(st_wd)
            if kg - 3 >= 0:
                ww.append((T, t_k_bykg[kg - 3]))
            op("y", lambda e, o=wdr[:, (kg % 3) * 1024:(kg % 3 + 1) * 1024],
               i_=scr_wdT[k * 128:(k + 1) * 128, :]:
               e.dma_start(out=o, in_=i_), sem=lw, inc=16, wait=tuple(ww))
            wd_ld[kg] = wv(lw)
            lp = LP[k % 4]
            pw = []
            if k == 0:
                pw.append((SP, st_pq_lo[2 * u + 1]))
            if k == 16:
                pw.append((SP2, st_pq_hi[2 * u + 1]))
            if kg - 4 >= 0:
                pw.append((T, t_k_bykg[kg - 4]))
            for j in range(2):
                op("y", lambda e,
                   o=pqs[:, (k % 4) * 256 + j * 128:(k % 4) * 256 + (j + 1) * 128],
                   i_=scr_pqF[2 * u + j][:, k * 128:(k + 1) * 128]:
                   e.dma_start(out=o, in_=i_), sem=lp, inc=16,
                   wait=tuple(pw) if j == 0 else ())
            pq_ld[kg] = wv(lp)

        def b2_mm_kstep(u, k):
            kg = u * 32 + k
            first = [wd_ld[kg], pq_ld[kg]]
            if k == 0 and (u - 1) in evac_v_marks:
                first.append((V, evac_v_marks[u - 1]))
            for j in range(2):
                for h in range(2):
                    op("t", lambda e, l=pqs[:, (k % 4) * 256 + j * 128:
                                            (k % 4) * 256 + (j + 1) * 128],
                       r=wdr[:, (kg % 3) * 1024 + h * 512:
                             (kg % 3) * 1024 + (h + 1) * 512],
                       o=pb[4 + 2 * j + h][:], st=(k == 0), sp=(k == 31):
                       e.matmul(o, l, r, start=st, stop=sp),
                       sem=T, inc=(1 if (j == 1 and h == 1) else 0),
                       wait=tuple(first) if (j == 0 and h == 0) else ())
            t_k_bykg[kg] = T.n
            if k == 31:
                t_sup_mark[u] = T.n

        def b2_step(u, sig):
            if u not in b2_ld_ptr:
                b2_ld_ptr[u] = 0
            tgt = min(2 * sig + 2, 31)
            while b2_ld_ptr[u] <= tgt:
                b2_load_kstep(u, b2_ld_ptr[u])
                b2_ld_ptr[u] += 1
            b2_mm_kstep(u, 2 * sig)
            b2_mm_kstep(u, 2 * sig + 1)

        def b2_evac(u):
            tmark = (T, t_sup_mark[u])
            for j in range(2):
                for h in range(2):
                    par = (2 * j + h) % 2
                    aw = []
                    if j == 0 and h == 0:
                        aw.append((SA, sa_group[NG - 1] if u == 0 else sa_group[u]))
                    if acc_rd_mark[par] is not None:
                        aw.append(acc_rd_mark[par])
                    op("g", lambda e, o=acc_stage[par],
                       i_=scr_acc[u * SUP + j * 128:u * SUP + (j + 1) * 128,
                                  h * 512:(h + 1) * 512]:
                       e.dma_start(out=o, in_=i_), sem=LB[par], inc=16,
                       wait=tuple(aw))
                    am = wv(LB[par])
                    sw = [tmark, am]
                    if u == 0 and par == 1 and out_rd_mark[1] is None:
                        sw.append((SA, sa_group[NG - 1]))
                    if out_rd_mark[par] is not None:
                        sw.append(out_rd_mark[par])
                    op("v", lambda e, o=out_stage[par], p_=pb[4 + 2 * j + h][:],
                       sc=dpwall[:, 2 * u + j:2 * u + j + 1], a_=acc_stage[par]:
                       e.scalar_tensor_tensor(o, p_, sc, a_, OP.mult, OP.add),
                       wait=tuple(sw))
                    acc_rd_mark[par] = wv(V)
                    rows = u * SUP + j * 128
                    op("g", lambda e, i_=out_stage[par],
                       o=d_out[rows:rows + 128, h * 512:(h + 1) * 512]:
                       e.dma_start(out=o, in_=i_[:]), sem=SO[par], inc=16,
                       wait=(acc_rd_mark[par],))
                    out_rd_mark[par] = wv(SO[par])
            evac_v_marks[u] = V.n

        ld_xq_mark = {}

        def emit_xq_load(ti):
            xq_wait = [xqT_store[ti]]
            if ti >= 2:
                xq_wait.append((T, t_after_tile[ti - 2]))
            op("y", lambda e, o=xqF_t[ti % 2][:], i_=scr_xqT[ti]:
               e.dma_start(out=o, in_=i_), sem=LQ[ti % 2], inc=16,
               wait=tuple(xq_wait))
            ld_xq_mark[ti] = wv(LQ[ti % 2])

        for i in range(NT):
            xqt = xqF_t[i % 2]
            if i == 0:
                emit_xq_load(0)
            if i + 1 < NT:
                emit_xq_load(i + 1)
            ld_xq = (ld_xq_mark[i],)
            op("v", lambda e, o=dvecs[:, 0:1], a=dxall[:, i:i + 1], b=wsc[:, 1:2]:
               e.tensor_tensor(o, a, b, OP.mult), wait=(wv(S),))
            op("v", lambda e, o=dvecs[:, 1:2], a=dxall[:, i:i + 1], b=wsc[:, 3:4]:
               e.tensor_tensor(o, a, b, OP.mult))
            dv_mark = wv(V)

            for s_ in range(8):
                bkg = (2 * slab_ctr["n"]) % 4
                bku = (2 * slab_ctr["n"] + 1) % 4
                slab_ctr["n"] += 1
                pg_, pu_ = pb[bkg][:], pb[bku][:]
                mmw = list(ld_xq) if s_ == 0 else []
                if i == 0 and s_ == 0:
                    mmw.extend(prep_w_marks)
                for bk in (bkg, bku):
                    if bank_marks[bk] is not None:
                        mmw.append(bank_marks[bk])
                for k in range(8):
                    bg0 = k * INNER + s_ * 512
                    op("t", lambda e, l=xqt[:, k * 128:(k + 1) * 128],
                       r=wqg[:, bg0:bg0 + 512], o=pg_, st=(k == 0), sp=(k == 7):
                       e.matmul(o, l, r, start=st, stop=sp),
                       sem=T, inc=(1 if k == 7 else 0), wait=tuple(mmw) if k == 0 else ())
                    op("t", lambda e, l=xqt[:, k * 128:(k + 1) * 128],
                       r=wqu[:, bg0:bg0 + 512], o=pu_, st=(k == 0), sp=(k == 7):
                       e.matmul(o, l, r, start=st, stop=sp),
                       sem=T, inc=(1 if k == 7 else 0))
                tmm = wv(T)
                spar = slab_ctr["n"] % 2
                sw = [tmm]
                if sg2_marks[spar] is not None:
                    sw.append(sg2_marks[spar])
                if s_ <= 1:
                    sw.append(dv_mark)
                op("s", lambda e, i_=pg_, o=sg2[spar], sc=dvecs[:, 0:1]:
                   e.activation(o, i_, AF.Silu, bias=0.0, scale=sc), wait=tuple(sw))
                silu_mark = wv(S)
                bank_marks[bkg] = silu_mark
                if i > 0:
                    if s_ == 0:
                        b1_front(i - 1, 0)
                    if s_ <= 6:
                        b1_front(i - 1, s_ + 1)
                op("v", lambda e, i_=pu_, o=pbuf[:, s_ * 512:(s_ + 1) * 512], sc=dvecs[:, 1:2],
                   sg=sg2[spar]:
                   e.scalar_tensor_tensor(o, i_, sc, sg, OP.mult, OP.mult),
                   wait=(silu_mark,))
                sg2_marks[spar] = wv(V)
                bank_marks[bku] = wv(V)
                if i > 0:
                    b1_tail(i - 1, s_)
                    if s_ >= 1:
                        if s_ == 1 and i >= 2:
                            ex = ((SP, st_pq_lo[i - 2]),)
                        elif s_ == 1 and i == 1:
                            ex = st_wu
                        elif s_ == 5 and i >= 2:
                            ex = ((SP2, st_pq_hi[i - 2]),)
                        else:
                            ex = ()
                        b1_xpose_piece(i - 1, s_ - 1, extra=ex)
                    if s_ == 6:
                        b1_store_lo(i - 1)
                if i >= 3:
                    b2_step((i - 3) // 2, ((i - 3) % 2) * 8 + s_)
            t_after_tile[i] = T.n
            if i > 0:
                b1_xpose_piece(i - 1, 7)
                b1_store_hi(i - 1)
            mxp, mcp, recp, svp = scal[:, 6:7], scal[:, 7:8], scal[:, 0:1], scal[:, 1:2]
            op("v", lambda e, o=mxp: e.tensor_reduce(o, pbuf[:], AX.X, OP.max,
                                                     apply_absolute_value=True))
            op("v", lambda e, i_=mxp, o=mcp: e.tensor_scalar(o, i_, 1e-5, None, OP.max))
            op("v", lambda e, i_=mcp, o=recp: e.reciprocal(o, i_))
            op("v", lambda e, i_=recp, o=svp: e.tensor_scalar(o, i_, 127.0, None, OP.mult))
            op("v", lambda e, i_=mcp, o=scal[:, 2:3]:
               e.tensor_scalar(o, i_, 1.0 / 127.0, None, OP.mult))
            op("v", lambda e, o=dpwall[:, i:i + 1], a=scal[:, 2:3], b=wsc[:, 5:6]:
               e.tensor_tensor(o, a, b, OP.mult))
            if i >= 3 and (i - 3) % 2 == 1:
                b2_evac((i - 3) // 2)
        i = NT - 1
        for pc in range(8):
            b1_front(i, pc)
            b1_tail(i, pc)
            if pc >= 1:
                if pc == 1:
                    ex = ((SP, st_pq_lo[NT - 2]),)
                elif pc == 5:
                    ex = ((SP2, st_pq_hi[NT - 2]),)
                else:
                    ex = ()
                b1_xpose_piece(i, pc - 1, extra=ex)
            if pc == 4:
                b1_store_lo(i)
        b1_xpose_piece(i, 7)
        b1_store_hi(i)

        for sig in range(8, 16):
            b2_step(14, sig)
        b2_evac(14)
        for sig in range(16):
            b2_step(15, sig)
        b2_evac(15)

        fin0, fin1 = SO[0].n, SO[1].n
        op("y", lambda e, a=fin0, b=fin1: e.wait_ge(SO[0].h, a))
        op("y", lambda e, a=fin0, b=fin1: e.wait_ge(SO[1].h, b))

        @block.sync
        def _(e):
            for f in plan["y"]:
                f(e)

        @block.gpsimd
        def _(e):
            for f in plan["g"]:
                f(e)

        @block.vector
        def _(e):
            e.wait_ge(G.h, g_after_consts)
            for f in plan["v"]:
                f(e)

        @block.scalar
        def _(e):
            e.wait_ge(G.h, g_after_consts)
            for f in plan["s"]:
                f(e)

        @block.tensor
        def _(e):
            e.wait_ge(G.h, g_after_consts)
            for f in plan["t"]:
                f(e)

    return nc


_NC_CACHE = {}


def kernel(L0_attn_x, L0_attn_Wg, L0_attn_bg, L0_attn_Wv, L0_attn_bv,
           L0_attn_Wd, L0_attn_bd, L0_ffn_w_gate, L0_ffn_w_up, L0_ffn_w_down):
    x = np.ascontiguousarray(np.asarray(L0_attn_x, dtype=np.float32))
    B = x.shape[0]
    Wg = np.ascontiguousarray(np.asarray(L0_attn_Wg, np.float32))
    Wv = np.ascontiguousarray(np.asarray(L0_attn_Wv, np.float32))
    Wd = np.ascontiguousarray(np.asarray(L0_attn_Wd, np.float32))
    bg = np.asarray(L0_attn_bg, np.float32).reshape(8, 128).T
    bv = np.asarray(L0_attn_bv, np.float32).reshape(8, 128).T
    bd = np.asarray(L0_attn_bd, np.float32).reshape(8, 128).T
    biasT = np.ascontiguousarray(np.concatenate([bg, bv, bd], axis=1))
    wgate = np.ascontiguousarray(np.asarray(L0_ffn_w_gate, np.float32))
    wup = np.ascontiguousarray(np.asarray(L0_ffn_w_up, np.float32))
    wdown = np.ascontiguousarray(np.asarray(L0_ffn_w_down, np.float32))

    consts = np.zeros((128, 16), np.float32)
    consts[:, 0] = -0.5
    consts[:, 1] = 0.5
    consts[:, 2] = MAGIC
    consts[:, 3] = -MAGIC
    consts[:, 4] = 1e-5
    identf = np.eye(128, dtype=np.float32)
    identb = np.asarray(np.eye(128, dtype=np.float32).astype(ml_dtypes.bfloat16))
    ones128 = np.ones((128, 1), np.float32)
    onesT = np.ones((1, 128), np.float32)

    if "nc" not in _NC_CACHE:
        _NC_CACHE["nc"] = build_program()
    nc = _NC_CACHE["nc"]

    in_maps = []
    for b in range(B):
        in_maps.append(dict(
            x=np.ascontiguousarray(x[b]), Wg=Wg, Wv=Wv, Wd=Wd, biasT=biasT,
            wgate=wgate, wup=wup, wdown=wdown, consts=consts,
            identf=identf, identb=identb, ones128=ones128, onesT=onesT,
        ))
    res = run_bass_kernel_spmd(nc, in_maps, core_ids=list(range(B)))
    out = np.stack([np.asarray(r["out"], np.float32) for r in res.results], axis=0)
    return out
